# version 05
# baseline (speedup 1.0000x reference)
import numpy as np
from contextlib import ExitStack

import concourse.bass as bass
import concourse.mybir as mybir
from concourse.bass_utils import run_bass_kernel_spmd

F32 = mybir.dt.float32
BF16 = mybir.dt.bfloat16
AF = mybir.ActivationFunctionType
OP = mybir.AluOpType

N_CORES = 8
B, T, D = 4, 4096, 1024
HALF = T // 2
NT = D // 128

ENGINES = ("tensor", "vector", "scalar", "gpsimd", "sync")
DMA_QUEUES = ("sync", "gpsimd", "scalar")


class Ev:
    __slots__ = ("kind", "name", "count")

    def __init__(self, kind, name, count):
        self.kind, self.name, self.count = kind, name, count


class Emitter:

    NSEM = 8

    def __init__(self, nc):
        self.nc = nc
        self.ops = []
        self.serialize = False
        self.ecount = {e: 0 for e in ENGINES}
        self.dcount = {(q, i): 0 for q in DMA_QUEUES for i in range(self.NSEM)}
        self.didx = {q: 0 for q in DMA_QUEUES}

    def op(self, engine, fn, deps=()):
        if self.serialize:
            deps = list(self._flat(deps)) + self.barrier()
        deps = list(self._flat(deps))
        if self.ecount[engine] > 0:
            deps.append(Ev("e", engine, self.ecount[engine]))
        self.ecount[engine] += 1
        ev = Ev("e", engine, self.ecount[engine])
        self.ops.append((engine, False, fn, tuple(deps), ev))
        return ev

    def dma(self, queue, out, in_, deps=()):
        if self.serialize:
            deps = list(self._flat(deps)) + self.barrier()
        slot = self.didx[queue] % self.NSEM
        self.didx[queue] += 1
        prev = self.dcount[(queue, slot)]
        self.dcount[(queue, slot)] = prev + 16
        ev = Ev("d", (queue, slot), prev + 16)
        pre = Ev("d", (queue, slot), prev) if prev else None

        def fn(eng, out=out, in_=in_):
            return eng.dma_start(out=out, in_=in_)

        deps = list(self._flat(deps))
        if pre is not None:
            deps.append(pre)
        self.ops.append((queue, True, fn, tuple(deps), ev))
        return ev

    @staticmethod
    def _flat(deps):
        out = []
        for d in deps:
            if d is None:
                continue
            if isinstance(d, Ev):
                out.append(d)
            else:
                out.extend(x for x in d if x is not None)
        return tuple(out)

    def barrier(self):
        evs = [Ev("e", e, c) for e, c in self.ecount.items() if c > 0]
        evs += [Ev("d", qs, c) for qs, c in self.dcount.items() if c > 0]
        return evs

    def finalize(self):
        nc = self.nc
        with ExitStack() as ctx:
            esem = {e: ctx.enter_context(nc.semaphore(f"es_{e}")) for e in ENGINES}
            dsem = {}
            for q in DMA_QUEUES:
                for i in range(self.NSEM):
                    dsem[(q, i)] = ctx.enter_context(nc.semaphore(f"ds_{q}{i}"))
            block = ctx.enter_context(nc.Block())
            runs = []
            for item in self.ops:
                if runs and runs[-1][0] == item[0]:
                    runs[-1][1].append(item)
                else:
                    runs.append((item[0], [item]))
            for engine, items in runs:
                section = getattr(block, engine)

                def body(eng, items=items):
                    for _, is_dma, fn, deps, ev in items:
                        best = {}
                        for d in deps:
                            key = (d.kind, d.name if isinstance(d.name, str)
                                   else d.name)
                            if best.get(key, -1) < d.count:
                                best[key] = d.count
                        for (kind, name), count in best.items():
                            sem = esem[name] if kind == "e" else dsem[name]
                            eng.wait_ge(sem, count)
                        ins = fn(eng)
                        if ev is not None:
                            if ev.kind == "e":
                                ins.then_inc(esem[ev.name], 1)
                            else:
                                ins.then_inc(dsem[ev.name], 16)

                section(body)

    def wait_tail(self):
        evs = self.barrier()

        def fn(eng):
            return eng.nop()

        self.ops.append(("sync", False, fn, tuple(evs), None))


def cdiv(a, b):
    return -(-a // b)



class Ctx:
    def __init__(self, name):
        self.nc = bass.Bass("TRN2", target_bir_lowering=False, debug=False)
        self.e = Emitter(self.nc)
        self.stack = ExitStack()
        self.name = name
        self._uid = 0

    def uid(self):
        self._uid += 1
        return self._uid

    def scope(self):
        from contextlib import contextmanager

        @contextmanager
        def _s(self=self):
            st = ExitStack()
            old = self.stack
            self.stack = st
            try:
                yield
            finally:
                self.stack = old
                st.close()
        return _s()

    def sb(self, shape, dtype, name=None):
        name = name or f"sb{self.uid()}"
        return self.stack.enter_context(self.nc.sbuf_tensor(name, shape, dtype))

    def ps(self, shape, dtype=F32, name=None):
        name = name or f"ps{self.uid()}"
        return self.stack.enter_context(self.nc.psum_tensor(name, shape, dtype))

    def inp(self, name, shape, dtype=F32):
        return self.nc.dram_tensor(name, shape, dtype, kind="ExternalInput").ap()

    def out(self, name, shape, dtype=F32):
        return self.nc.dram_tensor(name, shape, dtype, kind="ExternalOutput").ap()

    def finish(self):
        self.e.wait_tail()
        self.e.finalize()
        self.stack.close()
        return self.nc



def load_const_tiles(c, hbm_ap, rows, cols, dtype=BF16, queue="gpsimd", name=None):
    tiles, evs = [], []
    for i in range(cdiv(rows, 128)):
        r = min(128, rows - i * 128)
        t = c.sb([r, cols], dtype, name and f"{name}{i}")
        ev = c.e.dma(queue, out=t[:, :], in_=hbm_ap[i * 128:i * 128 + r, :])
        tiles.append(t)
        evs.append(ev)
    return tiles, evs


def emit_layernorm_fm(c, x_tiles, cols, out_bf, deps, out_f32=None,
                      gamma=None, eps=1e-5, dmodel=1024, rms=False, saveAB=None,
                      col_off=0, shared_f32=None, consume_fn=None,
                      rms1_gamma=None, rms1_eps=1e-6, ln_gamma=None,
                      ln_beta=None):
    e, nc = c.e, c.nc
    n_tiles = len(x_tiles)
    if not hasattr(c, "_ln_scratch"):
        ones_col = c.sb([128, 1], F32, "ln_ones_col")
        ones_row = c.sb([1, 128], F32, "ln_ones_row")
        ev_oc = e.op("vector", lambda v: v.memset(ones_col[:, :], 1.0))
        ev_or = e.op("vector", lambda v: v.memset(ones_row[:, :], 1.0))
        c._ln_scratch = dict(
            ones_col=ones_col, ones_row=ones_row, ev_oc=ev_oc, ev_or=ev_or,
            sq=[c.sb([128, 512], F32, f"ln_sq{i}") for i in range(2)],
            r_sq=c.sb([1, 512], F32, "ln_rsq"),
            r_mean=c.sb([1, 512], F32, "ln_rmean"),
            r_tmp=c.sb([1, 512], F32, "ln_rtmp"),
            r_A=c.sb([1, 512], F32, "ln_rA"),
            r_B=c.sb([1, 512], F32, "ln_rB"))
    L = c._ln_scratch
    ones_col, ones_row = L["ones_col"], L["ones_row"]
    ev_oc, ev_or = L["ev_oc"], L["ev_or"]
    sq = L["sq"]
    r_sq, r_mean, r_tmp = L["r_sq"], L["r_mean"], L["r_tmp"]
    r_A, r_B = L["r_A"], L["r_B"]
    psstack = ExitStack()
    _oldstack, c.stack = c.stack, psstack
    ps_sum = c.ps([1, 512])
    ps_sq = c.ps([1, 512])
    ps_A = c.ps([128, 512])
    ps_B = c.ps([128, 512])
    c.stack = _oldstack
    done = []
    prev_chunk = []
    for ci, c0 in enumerate(list(range(0, cols, 512))):
        w = min(512, cols - c0)
        if not rms:
            ev_s = None
            for i, xt in enumerate(x_tiles):
                ev_s = e.op("tensor", lambda t, xt=xt, i=i, c0=c0, w=w: t.matmul(
                    ps_sum[0:1, :w], ones_col[:, :], xt[:, c0:c0 + w],
                    start=(i == 0), stop=(i == n_tiles - 1)),
                    deps=list(deps) + [ev_oc] + prev_chunk)
        ev_q = None
        sq_evs = []
        for i, xt in enumerate(x_tiles):
            buf = sq[i % 2]
            ev_w = e.op("scalar", lambda s, xt=xt, buf=buf, c0=c0, w=w: s.activation(
                buf[:, :w], xt[:, c0:c0 + w], AF.Square),
                deps=list(deps) + prev_chunk + sq_evs[-2:-1])
            ev_q = e.op("tensor", lambda t, buf=buf, i=i, w=w: t.matmul(
                ps_sq[0:1, :w], ones_col[:, :], buf[:, :w],
                start=(i == 0), stop=(i == n_tiles - 1)),
                deps=[ev_w, ev_oc] + prev_chunk)
            sq_evs.append(ev_q)
        if rms:
            def rowf(v, w=w):
                return v.tensor_scalar(r_tmp[0:1, :w], ps_sq[0:1, :w], 1.0 / dmodel,
                                       eps, OP.mult, OP.add)
            ev_r1 = e.op("vector", rowf, deps=[ev_q])
        else:
            ev_a1 = e.op("vector", lambda v, w=w: v.tensor_scalar_mul(
                r_mean[0:1, :w], ps_sum[0:1, :w], 1.0 / dmodel), deps=[ev_s, ev_q])
            ev_a2 = e.op("vector", lambda v, w=w: v.tensor_tensor(
                r_tmp[0:1, :w], r_mean[0:1, :w], r_mean[0:1, :w], OP.mult),
                deps=[ev_a1])
            ev_a3 = e.op("vector", lambda v, w=w: v.tensor_scalar(
                r_sq[0:1, :w], ps_sq[0:1, :w], 1.0 / dmodel, eps, OP.mult, OP.add),
                deps=[ev_a2])
            ev_r1 = e.op("vector", lambda v, w=w: v.tensor_tensor(
                r_tmp[0:1, :w], r_sq[0:1, :w], r_tmp[0:1, :w], OP.subtract),
                deps=[ev_a3])
        ev_r2 = e.op("scalar", lambda s, w=w: s.activation(
            r_tmp[0:1, :w], r_tmp[0:1, :w], AF.Sqrt), deps=[ev_r1])

        ev_r3 = e.op("vector", lambda v, w=w: v.reciprocal(
            r_A[0:1, :w], r_tmp[0:1, :w]), deps=[ev_r2])
        if not rms:
            ev_r3 = e.op("vector", lambda v, w=w: v.scalar_tensor_tensor(
                r_B[0:1, :w], r_mean[0:1, :w], -1.0, r_A[0:1, :w],
                OP.mult, OP.mult), deps=[ev_r3])
        if rms1_gamma is not None:
            ev_h0 = e.op("vector", lambda v, w=w: v.tensor_tensor(
                r_tmp[0:1, :w], r_A[0:1, :w], r_A[0:1, :w], OP.mult), deps=[ev_r3])
            ev_h1 = e.op("vector", lambda v, w=w: v.tensor_scalar(
                r_tmp[0:1, :w], r_tmp[0:1, :w], -eps, 1.0 + rms1_eps,
                OP.mult, OP.add), deps=[ev_h0])
            ev_h2 = e.op("scalar", lambda s, w=w: s.activation(
                r_tmp[0:1, :w], r_tmp[0:1, :w], AF.Sqrt), deps=[ev_h1])

            ev_i0 = e.op("vector", lambda v, w=w: v.reciprocal(
                r_sq[0:1, :w], r_tmp[0:1, :w]), deps=[ev_h2])
            ev_i1 = e.op("vector", lambda v, w=w: v.tensor_tensor(
                r_A[0:1, :w], r_A[0:1, :w], r_sq[0:1, :w], OP.mult), deps=[ev_i0])
            ev_r3 = e.op("vector", lambda v, w=w: v.tensor_tensor(
                r_B[0:1, :w], r_B[0:1, :w], r_sq[0:1, :w], OP.mult), deps=[ev_i1])
        ev_bA = e.op("tensor", lambda t, w=w: t.matmul(
            ps_A[:, :w], ones_row[0:1, :], r_A[0:1, :w], start=True, stop=True),
            deps=[ev_r3, ev_or])
        if not rms:
            ev_bB = e.op("tensor", lambda t, w=w: t.matmul(
                ps_B[:, :w], ones_row[0:1, :], r_B[0:1, :w], start=True, stop=True),
                deps=[ev_r3, ev_or])
        if saveAB is not None:
            rowA_s, rowB_s = saveAB
            ev_sv = e.op("vector", lambda v, w=w, o=col_off + c0: v.tensor_copy(
                rowA_s[0:1, o:o + w], r_A[0:1, :w]), deps=[ev_r3])
            if not rms:
                ev_sv2 = e.op("vector", lambda v, w=w, o=col_off + c0: v.tensor_copy(
                    rowB_s[0:1, o:o + w], r_B[0:1, :w]), deps=[ev_r3])
        chunk_evs = []
        for i, xt in enumerate(x_tiles):
            buf = sq[i % 2]
            if rms:
                if gamma is not None:
                    ev1 = e.op("vector", lambda v, xt=xt, buf=buf, c0=c0, w=w: v.tensor_tensor(
                        buf[:, :w], xt[:, c0:c0 + w], ps_A[:, :w], OP.mult),
                        deps=[ev_bA] + sq_evs)
                    ev2 = e.op("vector", lambda v, i=i, buf=buf, o=col_off + c0, w=w: v.tensor_scalar_mul(
                        out_bf[i][:, o:o + w], buf[:, :w], gamma[i][:, 0:1]),
                        deps=[ev1])
                else:
                    ev2 = e.op("vector", lambda v, xt=xt, i=i, c0=c0, o=col_off + c0, w=w: v.tensor_tensor(
                        out_bf[i][:, o:o + w], xt[:, c0:c0 + w], ps_A[:, :w], OP.mult),
                        deps=[ev_bA] + sq_evs)
                chunk_evs.append(ev2)
            else:
                ev1 = e.op("vector", lambda v, xt=xt, buf=buf, c0=c0, w=w: v.tensor_tensor(
                    buf[:, :w], xt[:, c0:c0 + w], ps_A[:, :w], OP.mult),
                    deps=[ev_bA] + sq_evs)
                if shared_f32 is not None:
                    evf = e.op("vector", lambda v, buf=buf, w=w: v.tensor_tensor(
                        shared_f32[:, :w], buf[:, :w], ps_B[:, :w], OP.add),
                        deps=[ev_bB, ev1] + getattr(c, "_lnsh_war", []))
                    c._lnsh_war = list(consume_fn(i, evf))
                    ev2 = evf
                elif out_f32 is not None:
                    evf = e.op("vector", lambda v, i=i, buf=buf, o=col_off + c0, w=w: v.tensor_tensor(
                        out_f32[i][:, o:o + w], buf[:, :w], ps_B[:, :w], OP.add),
                        deps=[ev_bB, ev1])
                    if out_bf is None:
                        ev2 = evf
                    else:
                        ev2 = e.op("vector", lambda v, i=i, o=col_off + c0, w=w: v.tensor_copy(
                            out_bf[i][:, o:o + w], out_f32[i][:, o:o + w]),
                            deps=[evf])
                elif rms1_gamma is not None:
                    ev15 = e.op("vector", lambda v, buf=buf, w=w: v.tensor_tensor(
                        buf[:, :w], buf[:, :w], ps_B[:, :w], OP.add),
                        deps=[ev_bB, ev1])
                    ev2 = e.op("vector", lambda v, i=i, buf=buf, o=col_off + c0, w=w: v.tensor_scalar_mul(
                        out_bf[i][:, o:o + w], buf[:, :w], rms1_gamma[i][:, 0:1]),
                        deps=[ev15])
                elif ln_gamma is not None:
                    ev15 = e.op("vector", lambda v, buf=buf, w=w: v.tensor_tensor(
                        buf[:, :w], buf[:, :w], ps_B[:, :w], OP.add),
                        deps=[ev_bB, ev1])
                    ev2 = e.op("vector", lambda v, i=i, buf=buf, o=col_off + c0, w=w: v.scalar_tensor_tensor(
                        out_bf[i][:, o:o + w], buf[:, :w], ln_gamma[i][:, 0:1],
                        ln_beta[i][:, 0:1].broadcast_to([128, w]),
                        OP.mult, OP.add), deps=[ev15])
                else:
                    ev2 = e.op("vector", lambda v, i=i, buf=buf, o=col_off + c0, w=w: v.tensor_tensor(
                        out_bf[i][:, o:o + w], buf[:, :w], ps_B[:, :w], OP.add),
                        deps=[ev_bB, ev1])
                chunk_evs.append(ev2)
        prev_chunk = chunk_evs
        done.extend(chunk_evs)
    psstack.close()
    return done


def emit_proj_fm(c, w_tiles, w_evs, xin_bf, cols, out_psum_fn, m_tiles, deps,
                 n_k=8, nw=512):
    e = c.e
    out = {}
    for m in m_tiles:
        for ci, c0 in enumerate(range(0, cols, nw)):
            w_ = min(nw, cols - c0)
            pb = out_psum_fn(m, ci)
            ev = None
            for k in range(n_k):
                ev = e.op("tensor", lambda t, k=k, m=m, c0=c0, w_=w_, pb=pb: t.matmul(
                    pb[:, :w_], w_tiles(k, m), xin_bf[k][:, c0:c0 + w_],
                    start=(k == 0), stop=(k == n_k - 1)),
                    deps=list(deps) + list(w_evs))
            out[(m, ci)] = ev
    return out



def emit_stage2_body(c, x, xT_hbm, yT_hbm, wr_sb, wk_sb, wv_sb, wev,
                     mur_sb, muk_sb, x_deps, cols=HALF):
    e = c.e
    EC = 1 + cols
    xln = [c.sb([128, EC], BF16, f"s2xln{i}") for i in range(NT)]
    ln_done = emit_layernorm_fm(c, x, EC, xln, deps=x_deps)

    nchunk = cols // 512
    xr = [c.sb([128, 512], BF16) for _ in range(NT)]
    xk = [c.sb([128, 512], BF16) for _ in range(NT)]
    rb = [c.sb([128, 512], BF16) for _ in range(NT)]
    ka = [c.sb([128, 512], BF16) for _ in range(NT)]
    dsc = [c.sb([128, 512], BF16) for _ in range(2)]
    ksc = [c.sb([128, 512], F32) for _ in range(2)]
    tmp = [c.sb([128, 512], F32) for _ in range(2)]
    st_all = []
    with c.scope():
        pb = [c.ps([128, 512]) for _ in range(4)]
        pb_war = {i: [] for i in range(4)}
        buf_war = {0: []}
        for ci in range(nchunk):
            c0 = ci * 512
            p = 0
            xrp, xkp, rbp, kap = xr, xk, rb, ka
            mix_done = []
            for i in range(NT):
                cur = xln[i][:, 1 + c0:1 + c0 + 512]
                prv = xln[i][:, c0:c0 + 512]
                dv = dsc[p]
                ev_d = e.op("vector", lambda v, cur=cur, prv=prv, dv=dv: v.tensor_tensor(
                    dv[:, :], cur, prv, OP.subtract),
                    deps=ln_done + buf_war[p])
                ev_r = e.op("vector", lambda v, i=i, prv=prv, dv=dv: v.scalar_tensor_tensor(
                    xrp[i][:, :], dv[:, :], mur_sb[i][:, 0:1], prv, OP.mult, OP.add),
                    deps=[ev_d])
                ev_k = e.op("vector", lambda v, i=i, prv=prv, dv=dv: v.scalar_tensor_tensor(
                    xkp[i][:, :], dv[:, :], muk_sb[i][:, 0:1], prv, OP.mult, OP.add),
                    deps=[ev_d])
                mix_done += [ev_r, ev_k]
            rk_done = []
            for m in range(NT):
                bank = m % 2
                pbank = pb[bank]
                ev = None
                for k in range(NT):
                    ev = e.op("tensor", lambda t, k=k, m=m, pbank=pbank: t.matmul(
                        pbank[:, :], wr_sb[k][:, m * 128:(m + 1) * 128], xrp[k][:, :],
                        start=(k == 0), stop=(k == NT - 1)),
                        deps=mix_done + wev + pb_war[bank])
                ev_e = e.op("scalar", lambda s, m=m, pbank=pbank: s.activation(
                    rbp[m][:, :], pbank[:, :], AF.Sigmoid), deps=[ev])
                pb_war[bank] = [ev_e]
                bank2 = 2 + m % 2
                pbank2 = pb[bank2]
                ev = None
                for k in range(NT):
                    ev = e.op("tensor", lambda t, k=k, m=m, pbank2=pbank2: t.matmul(
                        pbank2[:, :], wk_sb[k][:, m * 128:(m + 1) * 128], xkp[k][:, :],
                        start=(k == 0), stop=(k == NT - 1)),
                        deps=mix_done + wev + pb_war[bank2])
                kscb = ksc[m % 2]
                ev_r1 = e.op("vector", lambda v, pbank2=pbank2, kscb=kscb: v.tensor_scalar_max(
                    kscb[:, :], pbank2[:, :], 0.0),
                    deps=[ev] + ([rk_done[-2]] if len(rk_done) >= 2 else []))
                pb_war[bank2] = [ev_r1]
                ev_sq = e.op("scalar", lambda s, m=m, kscb=kscb: s.activation(
                    kap[m][:, :], kscb[:, :], AF.Square), deps=[ev_r1])
                rk_done.append(ev_sq)
            ch_done = []
            for m in range(NT):
                bank = m % 2
                pbank = pb[bank]
                ev = None
                for k in range(NT):
                    ev = e.op("tensor", lambda t, k=k, m=m, pbank=pbank: t.matmul(
                        pbank[:, :], wv_sb[k][:, m * 128:(m + 1) * 128], kap[k][:, :],
                        start=(k == 0), stop=(k == NT - 1)),
                        deps=rk_done + pb_war[bank])
                tb = tmp[m % 2]
                ev_m = e.op("vector", lambda v, m=m, pbank=pbank, tb=tb: v.tensor_tensor(
                    tb[:, :], pbank[:, :], rbp[m][:, :], OP.mult),
                    deps=[ev] + ([ch_done[-2]] if len(ch_done) >= 2 else []))
                pb_war[bank] = [ev_m]
                ev_a = e.op("vector", lambda v, m=m, c0=c0, tb=tb: v.tensor_tensor(
                    tb[:, :], tb[:, :], x[m][:, 1 + c0:1 + c0 + 512], OP.add),
                    deps=[ev_m])
                ev_st = e.dma("sync", out=yT_hbm[m * 128:(m + 1) * 128, c0:c0 + 512],
                              in_=tb[:, :], deps=[ev_a])
                ch_done.append(ev_st)
            buf_war[p] = ch_done
            st_all += ch_done
    return st_all


def build_stage2():
    c = Ctx("stage2")
    e = c.e
    xT = c.inp("xT", [D, HALF])
    prevT = c.inp("prevT", [D, 1])
    wrT = c.inp("wrT", [D, D])
    wkT = c.inp("wkT", [D, D])
    wvT = c.inp("wvT", [D, D])
    mur = c.inp("mur", [D, 1])
    muk = c.inp("muk", [D, 1])
    yT = c.out("yT", [D, HALF])

    x = [c.sb([128, 1 + HALF], F32, f"x{i}") for i in range(NT)]
    mur_sb = [c.sb([128, 1], F32) for _ in range(NT)]
    muk_sb = [c.sb([128, 1], F32) for _ in range(NT)]
    wr_sb = [c.sb([128, D], BF16, f"wr{i}") for i in range(NT)]
    wk_sb = [c.sb([128, D], BF16, f"wk{i}") for i in range(NT)]
    wv_sb = [c.sb([128, D], BF16, f"wv{i}") for i in range(NT)]

    ld = []
    for i in range(NT):
        ld.append(e.dma("sync", out=x[i][:, 1:1 + HALF], in_=xT[i * 128:(i + 1) * 128, :]))
        ld.append(e.dma("sync", out=x[i][:, 0:1], in_=prevT[i * 128:(i + 1) * 128, :]))
        ld.append(e.dma("sync", out=mur_sb[i][:, :], in_=mur[i * 128:(i + 1) * 128, :]))
        ld.append(e.dma("sync", out=muk_sb[i][:, :], in_=muk[i * 128:(i + 1) * 128, :]))
    wev = []
    for i in range(NT):
        wev.append(e.dma("gpsimd", out=wr_sb[i][:, :], in_=wrT[i * 128:(i + 1) * 128, :]))
        wev.append(e.dma("gpsimd", out=wk_sb[i][:, :], in_=wkT[i * 128:(i + 1) * 128, :]))
        wev.append(e.dma("gpsimd", out=wv_sb[i][:, :], in_=wvT[i * 128:(i + 1) * 128, :]))

    emit_stage2_body(c, x, xT, yT, wr_sb, wk_sb, wv_sb, wev, mur_sb, muk_sb, ld)
    return c.finish()



def emit_full_attention(c, xln, NQ, NK, wq_hbm, wk_hbm, wv_hbm, a2_bf,
                        xln_deps, bound, apply_gelu, qk_scale=0.125,
                        wq_col=None, wk_col=None, wv_col=None):
    e = c.e
    wq_col = wq_col or (lambda hp: hp * 128)
    wk_col = wk_col or (lambda hp: hp * 128)
    wv_col = wv_col or (lambda hp: hp * 128)
    k2 = c.sb([128, NK], BF16, "fa_k2")
    q2 = c.sb([128, NQ], BF16, "fa_q2")
    v2e = c.sb([128, (NK // 128) * 130], BF16, "fa_v2e")
    wqp = [[c.sb([128, 128], BF16) for _ in range(NT)] for _ in range(2)]
    wkp = [[c.sb([128, 128], BF16) for _ in range(NT)] for _ in range(2)]
    wvp = [[c.sb([128, 128], BF16) for _ in range(NT)] for _ in range(2)]
    pstr = [c.sb([128, 512], BF16) for _ in range(2)]
    ysc = c.sb([64, 512], F32, "fa_ysc")
    ysc2 = c.sb([64, 512], F32, "fa_ysc2")
    rrow = c.sb([1, 512], F32, "fa_rrow")
    ones64 = c.sb([1, 64], F32, "fa_ones64")
    ev_o = e.op("vector", lambda v: v.memset(ones64[:, :], 1.0))
    nbias = c.sb([128, 1], F32, "fa_nbias")
    ev_nb = e.op("vector", lambda v: v.memset(nbias[:, :], -float(bound)))
    nkt = NK // 128
    v2e_3d = v2e[:, :].rearrange("p (t c) -> p t c", c=130)
    ones_evs = [
        e.op("vector", lambda v: v.memset(v2e_3d[:, :, 64:65], 1.0)),
        e.op("vector", lambda v: v.memset(v2e_3d[:, :, 129:130], 1.0)),
    ]
    out_evs = []
    with c.scope():
        ps_s = [c.ps([128, 512]) for _ in range(2)]
        ps_y = c.ps([128, 512], name="fa_psy")
        ps_b = c.ps([128, 512], name="fa_psb")
        war = {"k2": [], "q2": [], "v2e": ones_evs, "ps0": [], "ps1": [],
               "psy": [], "psb": [], "p0": [], "p1": [], "ysc": [], "ysc2": [],
               "rrow": [], "w0": [], "w1": []}
        for hp in range(8):
            wp = hp % 2
            wev = []
            for k in range(NT):
                wev.append(e.dma("gpsimd", out=wqp[wp][k][:, :],
                                 in_=wq_hbm[k * 128:(k + 1) * 128,
                                            wq_col(hp):wq_col(hp) + 128],
                                 deps=war[f"w{wp}"]))
                wev.append(e.dma("gpsimd", out=wkp[wp][k][:, :],
                                 in_=wk_hbm[k * 128:(k + 1) * 128,
                                            wk_col(hp):wk_col(hp) + 128],
                                 deps=war[f"w{wp}"]))
                wev.append(e.dma("gpsimd", out=wvp[wp][k][:, :],
                                 in_=wv_hbm[k * 128:(k + 1) * 128,
                                            wv_col(hp):wv_col(hp) + 128],
                                 deps=war[f"w{wp}"]))
            kq_evs = []
            kqv_mm = []
            for ci in range(NK // 512):
                c0 = ci * 512
                bank = ci % 2
                ev = None
                for k in range(NT):
                    ev = e.op("tensor", lambda t, k=k, c0=c0, wp=wp, bank=bank: t.matmul(
                        ps_s[bank][:, :], wkp[wp][k][:, :], xln[k][:, c0:c0 + 512],
                        start=(k == 0), stop=(k == NT - 1)),
                        deps=list(xln_deps) + wev + war[f"ps{bank}"])
                kqv_mm.append(ev)
                ev_e = e.op("vector", lambda v, c0=c0, bank=bank: v.tensor_copy(
                    k2[:, c0:c0 + 512], ps_s[bank][:, :]),
                    deps=[ev] + war["k2"])
                war[f"ps{bank}"] = [ev_e]
                kq_evs.append(ev_e)
            for ci in range(NQ // 512):
                c0 = ci * 512
                bank = ci % 2
                ev = None
                for k in range(NT):
                    ev = e.op("tensor", lambda t, k=k, c0=c0, wp=wp, bank=bank: t.matmul(
                        ps_s[bank][:, :], wqp[wp][k][:, :], xln[k][:, c0:c0 + 512],
                        start=(k == 0), stop=(k == NT - 1)),
                        deps=list(xln_deps) + wev + war[f"ps{bank}"])
                kqv_mm.append(ev)
                ev_e = e.op("vector", lambda v, c0=c0, bank=bank: v.tensor_copy(
                    q2[:, c0:c0 + 512], ps_s[bank][:, :]),
                    deps=[ev] + war["q2"])
                war[f"ps{bank}"] = [ev_e]
                kq_evs.append(ev_e)
            v_evs = []
            for t0 in range(nkt):
                bank = t0 % 2
                ev = None
                for k in range(NT):
                    ev = e.op("tensor", lambda t, k=k, t0=t0, wp=wp, bank=bank: t.matmul(
                        ps_s[bank][:, 0:128], xln[k][:, t0 * 128:(t0 + 1) * 128],
                        wvp[wp][k][:, :],
                        start=(k == 0), stop=(k == NT - 1)),
                        deps=list(xln_deps) + wev + war[f"ps{bank}"])
                kqv_mm.append(ev)

                b0 = t0 * 130
                ev_e1 = e.op("vector", lambda v, b0=b0, bank=bank: v.tensor_copy(
                    v2e[:, b0:b0 + 64], ps_s[bank][:, 0:64]),
                    deps=[ev] + war["v2e"])
                ev_e = e.op("vector", lambda v, b0=b0, bank=bank: v.tensor_copy(
                    v2e[:, b0 + 65:b0 + 129], ps_s[bank][:, 64:128]),
                    deps=[ev_e1])
                war[f"ps{bank}"] = [ev_e]
                v_evs.append(ev_e)
            war[f"w{wp}"] = kqv_mm
            head_evs = []
            for hh in range(2):
                hb = hh * 64
                for qc in range(NQ // 512):
                    q0 = qc * 512
                    ev_pv = None
                    for ks in range(nkt):
                        bank = ks % 2
                        ev_s = e.op("tensor", lambda t, ks=ks, q0=q0, hb=hb, bank=bank: t.matmul(
                            ps_s[bank][:, :], k2[hb:hb + 64, ks * 128:(ks + 1) * 128],
                            q2[hb:hb + 64, q0:q0 + 512],
                            start=True, stop=True),
                            deps=kq_evs + war[f"ps{bank}"] + war[f"p{bank}"])
                        ev_p = e.op("scalar", lambda s, bank=bank: s.activation(
                            pstr[bank][:, :], ps_s[bank][:, :], AF.Exp,
                            bias=nbias[:, 0:1], scale=qk_scale),
                            deps=[ev_s, ev_nb])
                        war[f"ps{bank}"] = [ev_p]
                        ev_pv = e.op("tensor", lambda t, ks=ks, hh=hh, bank=bank: t.matmul(
                            ps_y[0:65, :], v2e[:, ks * 130 + hh * 65:ks * 130 + hh * 65 + 65],
                            pstr[bank][:, :],
                            start=(ks == 0), stop=(ks == nkt - 1)),
                            deps=[ev_p] + v_evs + war["psy"])
                        war[f"p{bank}"] = [ev_pv]
                    ev_r = e.op("vector", lambda v: v.reciprocal(
                        rrow[0:1, :], ps_y[64:65, :]), deps=[ev_pv] + war["rrow"])
                    ev_b = e.op("tensor", lambda t: t.matmul(
                        ps_b[0:64, :], ones64[0:1, :], rrow[0:1, :],
                        start=True, stop=True), deps=[ev_r, ev_o] + war["psb"])
                    ev_ye = e.op("scalar", lambda s: s.copy(
                        ysc[:, :], ps_y[0:64, :]), deps=[ev_pv] + war["ysc"])
                    war["psy"] = [ev_ye]
                    ev_n = e.op("vector", lambda v: v.tensor_tensor(
                        ysc2[:, :], ysc[:, :], ps_b[0:64, :], OP.mult),
                        deps=[ev_ye, ev_b] + war["ysc2"])
                    war["rrow"] = [ev_b]
                    war["psb"] = [ev_n]
                    war["ysc"] = [ev_n]
                    tile, rb = hp, hb
                    if apply_gelu:
                        ev_g = e.op("scalar", lambda s, tile=tile, rb=rb, q0=q0: s.activation(
                            a2_bf[tile][rb:rb + 64, q0:q0 + 512], ysc2[:, :], AF.Gelu),
                            deps=[ev_n])
                    else:
                        ev_g = e.op("vector", lambda v, tile=tile, rb=rb, q0=q0: v.tensor_copy(
                            a2_bf[tile][rb:rb + 64, q0:q0 + 512], ysc2[:, :]),
                            deps=[ev_n])
                    war["ysc2"] = [ev_g]
                    head_evs.append(ev_g)
            war["k2"] = head_evs
            war["q2"] = head_evs
            war["v2e"] = head_evs
            out_evs.extend(head_evs)
    return out_evs


def build_stage1():
    c = Ctx("stage1")
    e = c.e
    xT = c.inp("xT", [D, T])
    wq = c.inp("wq", [D, D])
    wk = c.inp("wk", [D, D])
    wv = c.inp("wv", [D, D])
    wpost = c.inp("wpost", [D, D])
    yT = c.out("yT", [D, HALF])

    xln = [c.sb([128, T], BF16, f"xln{i}") for i in range(NT)]
    a2 = [c.sb([128, HALF], BF16, f"a2_{i}") for i in range(NT)]
    xc = [c.sb([128, 512], F32) for _ in range(NT)]

    ln_evs = []
    prev = []
    for ci in range(T // 512):
        c0 = ci * 512
        ld = [e.dma("sync", out=xc[i][:, :], in_=xT[i * 128:(i + 1) * 128, c0:c0 + 512],
                    deps=prev) for i in range(NT)]
        evs = emit_layernorm_fm(c, xc, 512, [t for t in xln], deps=ld, col_off=c0)
        prev = evs
        ln_evs.extend(evs)

    at_evs = emit_full_attention(c, xln, HALF, T, wq, wk, wv,
                                 a2, ln_evs, bound=16.0, apply_gelu=True)

    wp_sb = [c.sb([128, D], BF16, f"wp{i}") for i in range(NT)]
    wpev = [e.dma("gpsimd", out=wp_sb[i][:, :], in_=wpost[i * 128:(i + 1) * 128, :])
            for i in range(NT)]
    lnf = c.sb([128, 512], F32, "lnf")
    prev3 = []
    with c.scope():
        pb = [c.ps([128, 512]) for _ in range(2)]
        pb_war = {0: at_evs, 1: at_evs}
        tmp = [c.sb([128, 512], F32) for _ in range(2)]
        tmp_war = {0: [], 1: []}
        st3 = {"prev": []}

        for ci in range(HALF // 512):
            c0 = ci * 512
            ld3 = [e.dma("sync", out=xc[i][:, :],
                         in_=xT[i * 128:(i + 1) * 128, c0:c0 + 512],
                         deps=ln_evs + prev3) for i in range(NT)]
            ch3 = []

            def consume(m, evf, c0=c0, ch3=ch3, ld3=ld3):
                bank = m % 2
                ev = None
                for k in range(NT):
                    ev = e.op("tensor", lambda t, k=k, m=m, c0=c0, bank=bank: t.matmul(
                        pb[bank][:, :], wp_sb[k][:, m * 128:(m + 1) * 128],
                        a2[k][:, c0:c0 + 512],
                        start=(k == 0), stop=(k == NT - 1)),
                        deps=at_evs + wpev + pb_war[bank])
                tb = tmp[bank]
                ev_1 = e.op("vector", lambda v, m=m, tb=tb: v.tensor_tensor(
                    tb[:, :], xc[m][:, :], lnf[:, :], OP.add),
                    deps=[evf] + tmp_war[bank])
                ev_2 = e.op("vector", lambda v, tb=tb, bank=bank: v.tensor_tensor(
                    tb[:, :], tb[:, :], pb[bank][:, :], OP.add), deps=[ev_1, ev])
                ev_st = e.dma("sync", out=yT[m * 128:(m + 1) * 128, c0:c0 + 512],
                              in_=tb[:, :], deps=[ev_2])
                pb_war[bank] = [ev_2]
                tmp_war[bank] = [ev_st]
                ch3.append(ev_2)
                ch3.append(ev_st)
                return [ev_1]

            emit_layernorm_fm(c, xc, 512, None, deps=ld3 + at_evs,
                              shared_f32=lnf, consume_fn=consume)
            prev3 = ch3
    return c.finish()


def build_stage5():
    c = Ctx("stage5")
    e = c.e
    xT = c.inp("xT", [D, T])
    wqs = c.inp("wqs", [D, D])
    wks = c.inp("wks", [D, D])
    wvs = c.inp("wvs", [D, D])
    yT = c.out("yT", [D, HALF])

    xln = [c.sb([128, T], BF16, f"xln{i}") for i in range(NT)]
    a2 = [c.sb([128, HALF], F32, f"a2_{i}") for i in range(NT)]
    xc = [c.sb([128, 512], F32) for _ in range(NT)]

    ln_evs = []
    prev = []
    for ci in range(T // 512):
        c0 = ci * 512
        ld = [e.dma("sync", out=xc[i][:, :], in_=xT[i * 128:(i + 1) * 128, c0:c0 + 512],
                    deps=prev) for i in range(NT)]
        evs = emit_layernorm_fm(c, xc, 512, xln, deps=ld, col_off=c0)
        prev = evs
        ln_evs.extend(evs)

    at_evs = emit_full_attention(c, xln, HALF, T, wqs, wks, wvs,
                                 a2, ln_evs, bound=24.0, apply_gelu=False)

    prev3 = []
    tmp = [c.sb([128, 512], F32) for _ in range(2)]
    tmp_war = {0: [], 1: []}
    for ci in range(HALF // 512):
        c0 = ci * 512
        ch3 = []
        for m in range(NT):
            bank = m % 2
            xb = xc[m]
            ev_x = e.dma("sync", out=xb[:, :],
                         in_=xT[m * 128:(m + 1) * 128, c0:c0 + 512],
                         deps=ln_evs + prev3)
            tb = tmp[bank]
            ev_1 = e.op("vector", lambda v, m=m, c0=c0, xb=xb, tb=tb: v.tensor_tensor(
                tb[:, :], xb[:, :], a2[m][:, c0:c0 + 512], OP.add),
                deps=[ev_x] + at_evs + tmp_war[bank])
            ev_st = e.dma("sync", out=yT[m * 128:(m + 1) * 128, c0:c0 + 512],
                          in_=tb[:, :], deps=[ev_1])
            tmp_war[bank] = [ev_st]
            ch3 += [ev_1, ev_st]
        prev3 = ch3
    return c.finish()


def run_stage5(x_full, W):
    nc = _get_prog("stage5", build_stage5)
    Wr = np.ascontiguousarray(W.reshape(D, 16, 192))
    wqs = np.ascontiguousarray(Wr[:, :, 0:64].reshape(D, 1024))
    wks = np.ascontiguousarray(Wr[:, :, 64:128].reshape(D, 1024))
    wvs = np.ascontiguousarray(Wr[:, :, 128:192].reshape(D, 1024))
    in_maps = []
    for core in range(N_CORES):
        b, h = core // 2, core % 2
        own = x_full[b, h * HALF:(h + 1) * HALF, :]
        other = x_full[b, (1 - h) * HALF:(2 - h) * HALF, :]
        xT = np.ascontiguousarray(np.concatenate([own, other], axis=0).T)
        in_maps.append(dict(xT=xT, wqs=wqs, wks=wks, wvs=wvs))
    res = run_bass_kernel_spmd(nc, in_maps, core_ids=list(range(N_CORES)))
    out = np.empty_like(x_full)
    for core in range(N_CORES):
        b, h = core // 2, core % 2
        out[b, h * HALF:(h + 1) * HALF, :] = res.results[core]["yT"].T
    return out


def build_stage7():
    c = Ctx("stage7")
    e = c.e
    Q0 = HALF
    xT = c.inp("xT", [D, T])
    wq = c.inp("wq", [D, 256])
    wk = c.inp("wk", [D, 256])
    wv = c.inp("wv", [D, D])
    wo = c.inp("wo", [D, D])
    pmcol = c.inp("pmcol", [128, 1])
    dmask = c.inp("dmask", [512, 512])
    hmask = c.inp("hmask", [512, 512])
    yT = c.out("yT", [D, HALF])

    SQI = 1.0 / np.sqrt(32.0)
    SQB = float(np.sqrt(2.0) / 2.0)

    xln = [c.sb([128, T], BF16, f"xln{i}") for i in range(NT)]
    a2 = [c.sb([128, HALF], BF16, f"a2_{i}") for i in range(NT)]
    xc = [c.sb([128, 512], F32) for _ in range(NT)]
    wq_sb = [c.sb([128, 256], BF16) for _ in range(NT)]
    wk_sb = [c.sb([128, 256], BF16) for _ in range(NT)]
    wv_sb = [c.sb([128, D], BF16, f"wv{i}") for i in range(NT)]
    qa = [c.sb([16, HALF], BF16) for _ in range(2)]
    ka = [c.sb([16, T], BF16) for _ in range(2)]
    v2e = c.sb([128, (T // 128) * 130], BF16, "t7_v2e")
    pstr = [c.sb([128, 512], BF16) for _ in range(2)]
    half_c = c.sb([128, 512], BF16, "half_c")
    dm = [c.sb([128, 512], BF16) for _ in range(4)]
    hm = [c.sb([128, 512], BF16) for _ in range(4)]
    pm_sb = c.sb([128, 1], F32)
    ysc = c.sb([64, 512], F32)
    ysc2 = c.sb([64, 512], F32)
    rrow = c.sb([1, 512], F32)
    zrow = c.sb([1, 512], F32)
    ones64 = c.sb([1, 64], F32)
    sqb_col = c.sb([128, 1], F32)
    ev_con = [
        e.op("vector", lambda v: v.memset(half_c[:, :], 0.5)),
        e.op("vector", lambda v: v.memset(ones64[:, :], 1.0)),
        e.op("vector", lambda v: v.memset(sqb_col[:, :], SQB)),
    ]
    ld_c = [e.dma("sync", out=pm_sb[:, :], in_=pmcol[:, :])]
    for j in range(4):
        ld_c.append(e.dma("gpsimd", out=dm[j][:, :], in_=dmask[j * 128:(j + 1) * 128, :]))
        ld_c.append(e.dma("gpsimd", out=hm[j][:, :], in_=hmask[j * 128:(j + 1) * 128, :]))
    wev = []
    for i in range(NT):
        wev.append(e.dma("gpsimd", out=wq_sb[i][:, :], in_=wq[i * 128:(i + 1) * 128, :]))
        wev.append(e.dma("gpsimd", out=wk_sb[i][:, :], in_=wk[i * 128:(i + 1) * 128, :]))
        wev.append(e.dma("gpsimd", out=wv_sb[i][:, :], in_=wv[i * 128:(i + 1) * 128, :]))

    ln_evs = []
    prev = []
    for ci in range(T // 512):
        c0 = ci * 512
        ld = [e.dma("sync", out=xc[i][:, :], in_=xT[i * 128:(i + 1) * 128, c0:c0 + 512],
                    deps=prev) for i in range(NT)]
        evs = emit_layernorm_fm(c, xc, 512, xln, deps=ld, col_off=c0)
        prev = evs
        ln_evs.extend(evs)

    NKT = T // 128
    with c.scope():
        ps_s = [c.ps([128, 512]) for _ in range(2)]
        ps_y = c.ps([128, 512], name="t7_psy")
        ps_b = c.ps([128, 512], name="t7_psb")
        war = {"ps0": [], "ps1": [], "psy": [], "psb": [], "p0": [], "p1": [],
               "qk": [], "v2e": [], "ysc": [], "ysc2": [], "rrow": []}
        out_evs = []
        for hp in range(8):
            kq_evs = []
            for hh in range(2):
                h = hp * 2 + hh
                for ci in range(HALF // 512):
                    c0 = ci * 512
                    bank = ci % 2
                    ev = None
                    for k in range(NT):
                        ev = e.op("tensor", lambda t, k=k, h=h, c0=c0, bank=bank: t.matmul(
                            ps_s[bank][0:16, :], wq_sb[k][:, h * 16:h * 16 + 16],
                            xln[k][:, Q0 + c0:Q0 + c0 + 512],
                            start=(k == 0), stop=(k == NT - 1)),
                            deps=ln_evs + wev + war[f"ps{bank}"])
                    ev_e = e.op("vector", lambda v, hh=hh, c0=c0, bank=bank: v.tensor_copy(
                        qa[hh][:, c0:c0 + 512], ps_s[bank][0:16, :]),
                        deps=[ev] + war["qk"])
                    war[f"ps{bank}"] = [ev_e]
                    kq_evs.append(ev_e)
                for ci in range(T // 512):
                    c0 = ci * 512
                    bank = ci % 2
                    ev = None
                    for k in range(NT):
                        ev = e.op("tensor", lambda t, k=k, h=h, c0=c0, bank=bank: t.matmul(
                            ps_s[bank][0:16, :], wk_sb[k][:, h * 16:h * 16 + 16],
                            xln[k][:, c0:c0 + 512],
                            start=(k == 0), stop=(k == NT - 1)),
                            deps=ln_evs + wev + war[f"ps{bank}"])
                    ev_e = e.op("vector", lambda v, hh=hh, c0=c0, bank=bank: v.tensor_copy(
                        ka[hh][:, c0:c0 + 512], ps_s[bank][0:16, :]),
                        deps=[ev] + war["qk"])
                    war[f"ps{bank}"] = [ev_e]
                    kq_evs.append(ev_e)
            v_evs = []
            for t0 in range(NKT):
                bank = t0 % 2
                ev = None
                for k in range(NT):
                    ev = e.op("tensor", lambda t, k=k, t0=t0, hp=hp, bank=bank: t.matmul(
                        ps_s[bank][:, 0:128], xln[k][:, t0 * 128:(t0 + 1) * 128],
                        wv_sb[k][:, hp * 128:(hp + 1) * 128],
                        start=(k == 0), stop=(k == NT - 1)),
                        deps=ln_evs + wev + war[f"ps{bank}"])

                b0 = t0 * 130
                ev_e1 = e.op("vector", lambda v, b0=b0, bank=bank: v.tensor_copy(
                    v2e[:, b0:b0 + 64], ps_s[bank][:, 0:64]),
                    deps=[ev] + war["v2e"])
                ev_e = e.op("vector", lambda v, b0=b0, bank=bank: v.tensor_copy(
                    v2e[:, b0 + 65:b0 + 129], ps_s[bank][:, 64:128]),
                    deps=[ev_e1])
                if t0 < NKT // 2:
                    ev_o1 = e.op("vector", lambda v, b0=b0: v.tensor_copy(
                        v2e[:, b0 + 64:b0 + 65], pm_sb[:, 0:1]),
                        deps=[ev_e] + ld_c)
                    ev_o = e.op("vector", lambda v, b0=b0: v.tensor_copy(
                        v2e[:, b0 + 129:b0 + 130], pm_sb[:, 0:1]), deps=[ev_o1])
                else:
                    ev_o1 = e.op("vector", lambda v, b0=b0: v.memset(
                        v2e[:, b0 + 64:b0 + 65], 1.0), deps=[ev_e] + ld_c)
                    ev_o = e.op("vector", lambda v, b0=b0: v.memset(
                        v2e[:, b0 + 129:b0 + 130], 1.0), deps=[ev_o1])
                war[f"ps{bank}"] = [ev_e]
                v_evs.append(ev_o)
                v_evs.append(ev_e)
            head_evs = []
            for hh in range(2):
                qh, kh = qa[hh], ka[hh]
                for qc in range(HALF // 512):
                    q0 = qc * 512
                    nfull = (HALF // 128) + qc * 4
                    strips = list(range(nfull + 4))
                    ev_pv = None
                    for ks in strips:
                        bank = ks % 2
                        diag = ks >= nfull
                        ev_s = e.op("tensor", lambda t, ks=ks, q0=q0, qh=qh, kh=kh, bank=bank: t.matmul(
                            ps_s[bank][:, :], kh[:, ks * 128:(ks + 1) * 128],
                            qh[:, q0:q0 + 512],
                            start=True, stop=True),
                            deps=kq_evs + war[f"ps{bank}"] + war[f"p{bank}"])
                        ev_p = e.op("scalar", lambda s, bank=bank: s.activation(
                            pstr[bank][:, :], ps_s[bank][:, :], AF.Square,
                            bias=sqb_col[:, 0:1], scale=SQI),
                            deps=[ev_s] + ev_con)
                        war[f"ps{bank}"] = [ev_p]
                        if diag:
                            j = ks - nfull
                            ev_p = e.op("vector", lambda v, j=j, bank=bank: v.tensor_tensor(
                                pstr[bank][:, :], pstr[bank][:, :], dm[j][:, :],
                                OP.mult), deps=[ev_p] + ld_c)
                            war[f"ps{bank}"][0] = ev_p
                        vsl = v2e[:, ks * 130 + hh * 65:ks * 130 + hh * 65 + 65]
                        ev_pv = e.op("tensor", lambda t, vsl=vsl, bank=bank, first=(ks == 0): t.matmul(
                            ps_y[0:65, :], vsl, pstr[bank][:, :],
                            start=first, stop=False),
                            deps=[ev_p] + v_evs + (war["psy"] if ks == 0 else []))
                        war[f"p{bank}"] = [ev_pv]
                        rhs_h = hm[ks - nfull] if diag else half_c
                        ev_pv = e.op("tensor", lambda t, vsl=vsl, rhs_h=rhs_h, last=(ks == strips[-1]): t.matmul(
                            ps_y[0:65, :], vsl, rhs_h[:, :],
                            start=False, stop=last),
                            deps=[ev_pv] + ev_con + ld_c)
                    ev_z = e.op("vector", lambda v: v.tensor_scalar_add(
                        zrow[0:1, :], ps_y[64:65, :], 1e-12), deps=[ev_pv] + war["rrow"])
                    ev_r = e.op("vector", lambda v: v.reciprocal(
                        rrow[0:1, :], zrow[0:1, :]), deps=[ev_z])
                    ev_b = e.op("tensor", lambda t: t.matmul(
                        ps_b[0:64, :], ones64[0:1, :], rrow[0:1, :],
                        start=True, stop=True), deps=[ev_r] + ev_con + war["psb"])
                    ev_ye = e.op("scalar", lambda s: s.copy(
                        ysc[:, :], ps_y[0:64, :]), deps=[ev_pv] + war["ysc"])
                    war["psy"] = [ev_ye]
                    ev_n = e.op("vector", lambda v: v.tensor_tensor(
                        ysc2[:, :], ysc[:, :], ps_b[0:64, :], OP.mult),
                        deps=[ev_ye, ev_b] + war["ysc2"])
                    war["rrow"] = [ev_b]
                    war["psb"] = [ev_n]
                    war["ysc"] = [ev_n]
                    hb = hh * 64
                    ev_g = e.op("vector", lambda v, hp=hp, hb=hb, q0=q0: v.tensor_copy(
                        a2[hp][hb:hb + 64, q0:q0 + 512], ysc2[:, :]), deps=[ev_n])
                    war["ysc2"] = [ev_g]
                    head_evs.append(ev_g)
            war["qk"] = head_evs
            war["v2e"] = head_evs
            out_evs.extend(head_evs)
    at_evs = out_evs

    wo_sb = wv_sb
    woev = [e.dma("gpsimd", out=wo_sb[i][:, :], in_=wo[i * 128:(i + 1) * 128, :],
                  deps=at_evs) for i in range(NT)]
    prev3 = []
    with c.scope():
        pb = [c.ps([128, 512]) for _ in range(2)]
        pb_war = {0: at_evs, 1: at_evs}
        tmp = [c.sb([128, 512], F32) for _ in range(2)]
        tmp_war = {0: [], 1: []}
        for ci in range(HALF // 512):
            c0 = ci * 512
            for m in range(NT):
                bank = m % 2
                ev = None
                for k in range(NT):
                    ev = e.op("tensor", lambda t, k=k, m=m, c0=c0, bank=bank: t.matmul(
                        pb[bank][:, :], wo_sb[k][:, m * 128:(m + 1) * 128],
                        a2[k][:, c0:c0 + 512],
                        start=(k == 0), stop=(k == NT - 1)),
                        deps=at_evs + woev + pb_war[bank])
                xb = xc[m]
                ev_x = e.dma("sync", out=xb[:, :],
                             in_=xT[m * 128:(m + 1) * 128, Q0 + c0:Q0 + c0 + 512],
                             deps=ln_evs + prev3)
                tb = tmp[bank]
                ev_1 = e.op("vector", lambda v, xb=xb, tb=tb, bank=bank: v.tensor_tensor(
                    tb[:, :], xb[:, :], pb[bank][:, :], OP.add),
                    deps=[ev, ev_x] + tmp_war[bank])
                ev_st = e.dma("sync", out=yT[m * 128:(m + 1) * 128, c0:c0 + 512],
                              in_=tb[:, :], deps=[ev_1])
                pb_war[bank] = [ev_1]
                tmp_war[bank] = [ev_st]
                prev3 = [ev_1, ev_st]
    return c.finish()


def run_stage7(x_full, Wq, Wk, Wv, Wo):
    nc = _get_prog("stage7", build_stage7)
    r = np.arange(512)
    kk = np.arange(512)
    dmask = (kk[:, None] <= r[None, :]).astype(np.float32)
    hmask = 0.5 * dmask
    in_maps = []
    for core in range(N_CORES):
        b, h = core // 2, core % 2
        own = x_full[b, h * HALF:(h + 1) * HALF, :]
        if h == 0:
            prev = np.zeros_like(own)
            pm = np.zeros((128, 1), np.float32)
        else:
            prev = x_full[b, 0:HALF, :]
            pm = np.ones((128, 1), np.float32)
        xT = np.ascontiguousarray(np.concatenate([prev, own], axis=0).T)
        in_maps.append(dict(xT=xT, wq=Wq, wk=Wk, wv=Wv, wo=Wo, pmcol=pm,
                            dmask=dmask, hmask=hmask))
    res = run_bass_kernel_spmd(nc, in_maps, core_ids=list(range(N_CORES)))
    out = np.empty_like(x_full)
    for core in range(N_CORES):
        b, h = core // 2, core % 2
        out[b, h * HALF:(h + 1) * HALF, :] = res.results[core]["yT"].T
    return out


def build_stage3a():
    c = Ctx("stage3a")
    e = c.e
    Q0 = HALF
    xT = c.inp("xT", [D, T])
    qkvw = c.inp("qkvw", [D, 3072])
    kvb_row = c.inp("kvb_row", [1, 2048])
    qb_col = c.inp("qb_col", [D, 1])
    kb_col = c.inp("kb_col", [D, 1])
    ob_col = c.inp("ob_col", [D, 1])
    n1_col = c.inp("n1_col", [D, 1])
    ow = c.inp("ow", [D, D])
    cosw_row = c.inp("cosw_row", [1, T])
    sinw_row = c.inp("sinw_row", [1, T])
    coswc = c.inp("coswc", [128, T // 128])
    sinwc = c.inp("sinwc", [128, T // 128])
    dmask = c.inp("dmask", [512, 512])
    hT = c.out("hT", [D, HALF])
    NKT = T // 128

    xc = [c.sb([128, 512], F32) for _ in range(NT)]
    a2 = [c.sb([128, HALF], BF16, f"a2_{i}") for i in range(NT)]
    cos_b = c.sb([128, HALF], BF16, "cos_b")
    sin_b = c.sb([128, HALF], BF16, "sin_b")
    cwc = c.sb([128, NKT], F32)
    swc = c.sb([128, NKT], F32)
    kb_r = c.sb([1, 128], BF16)
    vb_r = c.sb([1, 128], BF16)
    qb = [c.sb([128, 1], F32) for _ in range(NT)]
    obc = [c.sb([128, 1], F32) for _ in range(NT)]
    n1c = [c.sb([128, 1], F32) for _ in range(NT)]
    dm = [c.sb([128, 512], BF16) for _ in range(4)]
    ones1 = c.sb([1, 128], BF16)
    onesr = c.sb([1, 128], F32)
    ones64 = c.sb([1, 64], F32)
    crow = c.sb([1, 512], F32)
    ld_c = [e.dma("sync", out=cwc[:, :], in_=coswc[:, :]),
            e.dma("sync", out=swc[:, :], in_=sinwc[:, :])]
    kbc = [c.sb([128, 1], F32) for _ in range(NT)]
    for i in range(NT):
        ld_c.append(e.dma("sync", out=qb[i][:, :], in_=qb_col[i * 128:(i + 1) * 128, :]))
        ld_c.append(e.dma("sync", out=kbc[i][:, :], in_=kb_col[i * 128:(i + 1) * 128, :]))
        ld_c.append(e.dma("sync", out=obc[i][:, :], in_=ob_col[i * 128:(i + 1) * 128, :]))
        ld_c.append(e.dma("sync", out=n1c[i][:, :], in_=n1_col[i * 128:(i + 1) * 128, :]))
    for j in range(4):
        ld_c.append(e.dma("gpsimd", out=dm[j][:, :], in_=dmask[j * 128:(j + 1) * 128, :]))
    ev_con = [e.op("vector", lambda v: v.memset(ones1[:, :], 1.0)),
              e.op("vector", lambda v: v.memset(onesr[:, :], 1.0)),
              e.op("vector", lambda v: v.memset(ones64[:, :], 1.0))]
    with c.scope():
        ps_t = c.ps([128, 512])
        war_t = []
        for src_row, dst in ((cosw_row, cos_b), (sinw_row, sin_b)):
            for ci in range(HALF // 512):
                c0 = ci * 512
                ev_l = e.dma("sync", out=crow[0:1, :], in_=src_row[0:1, Q0 + c0:Q0 + c0 + 512],
                             deps=war_t)
                ev_m = e.op("tensor", lambda t, c0=c0: t.matmul(
                    ps_t[:, :], onesr[0:1, :], crow[0:1, :], start=True, stop=True),
                    deps=[ev_l] + ev_con + war_t)
                ev_e = e.op("vector", lambda v, c0=c0, dst=dst: v.tensor_copy(
                    dst[:, c0:c0 + 512], ps_t[:, :]), deps=[ev_m])
                war_t = [ev_e]
        tab_evs = [ev_e]

    rms1 = [c.sb([128, T], BF16, f"rms1_{i}") for i in range(NT)]
    ln_evs = []
    prev = []
    for ci in range(T // 512):
        c0 = ci * 512
        ld = [e.dma("sync", out=xc[i][:, :], in_=xT[i * 128:(i + 1) * 128, c0:c0 + 512],
                    deps=prev) for i in range(NT)]
        evs = emit_layernorm_fm(c, xc, 512, rms1, deps=ld + ld_c, col_off=c0,
                                rms1_gamma=n1c, rms1_eps=1e-6)
        prev = evs
        ln_evs.extend(evs)

    with c.scope():
        wqp = [c.sb([128, 128], BF16) for _ in range(NT)]
        wkp = [c.sb([128, 128], BF16) for _ in range(NT)]
        wvp = [c.sb([128, 128], BF16) for _ in range(NT)]
        qcp = c.sb([128, HALF], BF16, "qcp")
        qsp = c.sb([128, HALF], BF16, "qsp")
        kfc = c.sb([128, HALF], BF16, "kfc")
        kfs = c.sb([128, HALF], BF16, "kfs")
        ktc = c.sb([128, NKT * 128], BF16, "ktc")
        kts = c.sb([128, NKT * 128], BF16, "kts")
        v2e = c.sb([128, NKT * 130], BF16, "c_v2e")
        qrel = c.sb([128, 512], F32)
        sbf = [c.sb([128, 512], BF16) for _ in range(2)]
        stc_bf = c.sb([128, 65], BF16)
        sts_bf = c.sb([128, 65], BF16)
        stc_lo = c.sb([128, 65], BF16)
        sts_lo = c.sb([128, 65], BF16)
        ysc = c.sb([64, 512], F32)
        ysc2 = c.sb([64, 512], F32)
        rrow = c.sb([1, 512], F32)
        zrow = c.sb([1, 512], F32)
        ps_s = [c.ps([128, 512]) for _ in range(2)]
        ps_y = c.ps([128, 512])
        ps_b = c.ps([128, 512])
        ps_stc = c.ps([64, 65])
        ps_sts = c.ps([64, 65])
        war = {"ps0": [], "ps1": [], "psy": [], "psb": [], "s0": [], "s1": [],
               "qk": [], "v2e": [], "ysc": [], "ysc2": [], "rrow": [],
               "w": [], "qrel": [], "st": [], "stbf": []}
        v2e_3d = v2e[:, :].rearrange("p (t c) -> p t c", c=130)
        ev_vo = [e.op("vector", lambda v: v.memset(v2e_3d[:, :, 64:65], 1.0)),
                 e.op("vector", lambda v: v.memset(v2e_3d[:, :, 129:130], 1.0))]
        out_evs = []
        for hp in range(8):
            wev = [e.dma("gpsimd", out=kb_r[:, :],
                         in_=kvb_row[0:1, hp * 128:hp * 128 + 128], deps=war["w"]),
                   e.dma("gpsimd", out=vb_r[:, :],
                         in_=kvb_row[0:1, 1024 + hp * 128:1024 + hp * 128 + 128],
                         deps=war["w"])]
            for k in range(NT):
                wev.append(e.dma("gpsimd", out=wqp[k][:, :],
                                 in_=qkvw[k * 128:(k + 1) * 128, hp * 128:hp * 128 + 128],
                                 deps=war["w"]))
                wev.append(e.dma("gpsimd", out=wkp[k][:, :],
                                 in_=qkvw[k * 128:(k + 1) * 128, 1024 + hp * 128:1024 + hp * 128 + 128],
                                 deps=war["w"]))
                wev.append(e.dma("gpsimd", out=wvp[k][:, :],
                                 in_=qkvw[k * 128:(k + 1) * 128, 2048 + hp * 128:2048 + hp * 128 + 128],
                                 deps=war["w"]))
            proj_mm = []
            kq_evs = []
            for ci in range(HALF // 512):
                c0 = ci * 512
                bank = ci % 2
                ev = None
                for k in range(NT):
                    ev = e.op("tensor", lambda t, k=k, c0=c0, bank=bank: t.matmul(
                        ps_s[bank][:, :], wqp[k][:, :], rms1[k][:, Q0 + c0:Q0 + c0 + 512],
                        start=(k == 0), stop=(k == NT - 1)),
                        deps=ln_evs + wev + war[f"ps{bank}"])
                proj_mm.append(ev)
                ev_e = e.op("scalar", lambda s, hp=hp, bank=bank: s.activation(
                    qrel[:, :], ps_s[bank][:, :], AF.Relu, bias=qb[hp][:, 0:1]),
                    deps=[ev] + war["qrel"])
                war[f"ps{bank}"] = [ev_e]
                evq = []
                evq.append(e.op("vector", lambda v, c0=c0: v.tensor_tensor(
                    qcp[:, c0:c0 + 512], qrel[:, :],
                    cos_b[:, c0:c0 + 512], OP.mult),
                    deps=[ev_e] + tab_evs + war["qk"]))
                evq.append(e.op("vector", lambda v, c0=c0: v.tensor_tensor(
                    qsp[:, c0:c0 + 512], qrel[:, :],
                    sin_b[:, c0:c0 + 512], OP.mult),
                    deps=[ev_e] + tab_evs + war["qk"]))
                war["qrel"] = evq
                kq_evs += evq
            for ci in range(HALF // 512):
                c0 = ci * 512
                bank = ci % 2
                ev = None
                for k in range(NT):
                    ev = e.op("tensor", lambda t, k=k, c0=c0, bank=bank: t.matmul(
                        ps_s[bank][:, :], wkp[k][:, :], rms1[k][:, Q0 + c0:Q0 + c0 + 512],
                        start=(k == 0), stop=(k == NT - 1)),
                        deps=ln_evs + wev + war[f"ps{bank}"])
                proj_mm.append(ev)
                ev_e = e.op("scalar", lambda s, hp=hp, bank=bank: s.activation(
                    qrel[:, :], ps_s[bank][:, :], AF.Relu, bias=kbc[hp][:, 0:1]),
                    deps=[ev] + war["qrel"] + ld_c)
                war[f"ps{bank}"] = [ev_e]
                ev1 = e.op("vector", lambda v, c0=c0: v.tensor_tensor(
                    kfc[:, c0:c0 + 512], qrel[:, :], cos_b[:, c0:c0 + 512], OP.mult),
                    deps=[ev_e] + tab_evs + war["qk"])
                ev2 = e.op("vector", lambda v, c0=c0: v.tensor_tensor(
                    kfs[:, c0:c0 + 512], qrel[:, :], sin_b[:, c0:c0 + 512], OP.mult),
                    deps=[ev_e] + tab_evs + war["qk"])
                war["qrel"] = [ev1, ev2]
                kq_evs += [ev1, ev2]
            v_evs = []
            for t0 in range(NKT):
                bank = t0 % 2
                ev = None
                for k in range(NT):
                    ev = e.op("tensor", lambda t, k=k, t0=t0, bank=bank: t.matmul(
                        ps_s[bank][:, 0:128], rms1[k][:, t0 * 128:(t0 + 1) * 128],
                        wkp[k][:, :], start=(k == 0), stop=False),
                        deps=ln_evs + wev + war[f"ps{bank}"])
                ev = e.op("tensor", lambda t, bank=bank: t.matmul(
                    ps_s[bank][:, 0:128], ones1[0:1, :], kb_r[0:1, :],
                    start=False, stop=True),
                    deps=[ev] + wev + ev_con)
                proj_mm.append(ev)
                ev_e = e.op("scalar", lambda s, bank=bank: s.activation(
                    qrel[:, 0:128], ps_s[bank][:, 0:128], AF.Relu),
                    deps=[ev] + war["qrel"])
                war[f"ps{bank}"] = [ev_e]
                ev1 = e.op("vector", lambda v, t0=t0: v.tensor_scalar_mul(
                    ktc[:, t0 * 128:(t0 + 1) * 128], qrel[:, 0:128], cwc[:, t0:t0 + 1]),
                    deps=[ev_e] + ld_c + war["qk"])
                ev2 = e.op("vector", lambda v, t0=t0: v.tensor_scalar_mul(
                    kts[:, t0 * 128:(t0 + 1) * 128], qrel[:, 0:128], swc[:, t0:t0 + 1]),
                    deps=[ev_e] + ld_c + war["qk"])
                war["qrel"] = [ev1, ev2]
                kq_evs += [ev1, ev2]
                bank = (t0 + 1) % 2
                ev = None
                for k in range(NT):
                    ev = e.op("tensor", lambda t, k=k, t0=t0, bank=bank: t.matmul(
                        ps_s[bank][:, 0:128], rms1[k][:, t0 * 128:(t0 + 1) * 128],
                        wvp[k][:, :], start=(k == 0), stop=False),
                        deps=ln_evs + wev + war[f"ps{bank}"])
                ev = e.op("tensor", lambda t, bank=bank: t.matmul(
                    ps_s[bank][:, 0:128], ones1[0:1, :], vb_r[0:1, :],
                    start=False, stop=True),
                    deps=[ev] + wev + ev_con)
                proj_mm.append(ev)

                b0 = t0 * 130
                ev_e1 = e.op("vector", lambda v, b0=b0, bank=bank: v.tensor_copy(
                    v2e[:, b0:b0 + 64], ps_s[bank][:, 0:64]),
                    deps=[ev] + war["v2e"] + ev_vo)
                ev_e = e.op("vector", lambda v, b0=b0, bank=bank: v.tensor_copy(
                    v2e[:, b0 + 65:b0 + 129], ps_s[bank][:, 64:128]),
                    deps=[ev_e1])
                war[f"ps{bank}"] = [ev_e]
                v_evs.append(ev_e)
            war["w"] = proj_mm
            head_evs = []
            for hh in range(2):
                hb = hh * 64
                ev_su = None
                NPS = HALF // 128
                for t0 in range(NPS):
                    ev_su = e.op("tensor", lambda t, t0=t0, hb=hb, hh=hh, first=(t0 == 0), lst=(t0 == NPS - 1): t.matmul(
                        ps_stc[:, :], ktc[:, t0 * 128 + hb:t0 * 128 + hb + 64],
                        v2e[:, t0 * 130 + hh * 65:t0 * 130 + hh * 65 + 65],
                        start=first, stop=lst, skip_group_check=True),
                        deps=kq_evs + v_evs + war["st"])
                    ev_su = e.op("tensor", lambda t, t0=t0, hb=hb, hh=hh, first=(t0 == 0), lst=(t0 == NPS - 1): t.matmul(
                        ps_sts[:, :], kts[:, t0 * 128 + hb:t0 * 128 + hb + 64],
                        v2e[:, t0 * 130 + hh * 65:t0 * 130 + hh * 65 + 65],
                        start=first, stop=lst, skip_group_check=True),
                        deps=kq_evs + v_evs + war["st"])
                for qc in range(HALF // 512):
                    q0 = qc * 512
                    ev_sc = e.op("scalar", lambda s, hb=hb: s.copy(
                        stc_bf[hb:hb + 64, :], ps_stc[:, :]),
                        deps=[ev_su] + war["stbf"])
                    ev_ss = e.op("scalar", lambda s, hb=hb: s.copy(
                        sts_bf[hb:hb + 64, :], ps_sts[:, :]),
                        deps=[ev_su] + war["stbf"])
                    ev_sc = e.op("vector", lambda v, hb=hb: v.tensor_tensor(
                        stc_lo[hb:hb + 64, :], ps_stc[:, :], stc_bf[hb:hb + 64, :],
                        OP.subtract), deps=[ev_sc])
                    ev_ss = e.op("vector", lambda v, hb=hb: v.tensor_tensor(
                        sts_lo[hb:hb + 64, :], ps_sts[:, :], sts_bf[hb:hb + 64, :],
                        OP.subtract), deps=[ev_ss])
                    ev_pv = None
                    for j in range(4):
                        ks = (Q0 + q0) // 128 + j
                        bank = j % 2
                        ko = ks * 128 - Q0
                        ev_s1 = e.op("tensor", lambda t, ko=ko, q0=q0, hb=hb, bank=bank: t.matmul(
                            ps_s[bank][:, :], kfc[hb:hb + 64, ko:ko + 128],
                            qcp[hb:hb + 64, q0:q0 + 512], start=True, stop=False),
                            deps=kq_evs + war[f"ps{bank}"] + war[f"s{bank}"])
                        ev_s2 = e.op("tensor", lambda t, ko=ko, q0=q0, hb=hb, bank=bank: t.matmul(
                            ps_s[bank][:, :], kfs[hb:hb + 64, ko:ko + 128],
                            qsp[hb:hb + 64, q0:q0 + 512], start=False, stop=True),
                            deps=[ev_s1])
                        ev_m = e.op("vector", lambda v, j=j, bank=bank: v.tensor_tensor(
                            sbf[bank][:, :], ps_s[bank][:, :], dm[j][:, :], OP.mult),
                            deps=[ev_s2] + ld_c + war[f"s{bank}"])
                        war[f"ps{bank}"] = [ev_m]
                        ev_pv = e.op("tensor", lambda t, ks=ks, hh=hh, bank=bank, first=(j == 0): t.matmul(
                            ps_y[0:65, :], v2e[:, ks * 130 + hh * 65:ks * 130 + hh * 65 + 65],
                            sbf[bank][:, :], start=first, stop=False),
                            deps=[ev_m] + v_evs + (war["psy"] if j == 0 else []))
                        war[f"s{bank}"] = [ev_pv]
                    ev_pv = e.op("tensor", lambda t, q0=q0, hb=hb: t.matmul(
                        ps_y[0:65, :], stc_bf[hb:hb + 64, :], qcp[hb:hb + 64, q0:q0 + 512],
                        start=False, stop=False), deps=[ev_pv, ev_sc])
                    ev_pv = e.op("tensor", lambda t, q0=q0, hb=hb: t.matmul(
                        ps_y[0:65, :], stc_lo[hb:hb + 64, :], qcp[hb:hb + 64, q0:q0 + 512],
                        start=False, stop=False), deps=[ev_pv, ev_sc])
                    ev_pv = e.op("tensor", lambda t, q0=q0, hb=hb: t.matmul(
                        ps_y[0:65, :], sts_bf[hb:hb + 64, :], qsp[hb:hb + 64, q0:q0 + 512],
                        start=False, stop=False), deps=[ev_pv, ev_ss])
                    ev_pv = e.op("tensor", lambda t, q0=q0, hb=hb: t.matmul(
                        ps_y[0:65, :], sts_lo[hb:hb + 64, :], qsp[hb:hb + 64, q0:q0 + 512],
                        start=False, stop=True), deps=[ev_pv, ev_ss])
                    war["stbf"] = [ev_pv]
                    ev_z = e.op("vector", lambda v: v.tensor_scalar_add(
                        zrow[0:1, :], ps_y[64:65, :], 1e-6), deps=[ev_pv] + war["rrow"])
                    ev_r = e.op("vector", lambda v: v.reciprocal(
                        rrow[0:1, :], zrow[0:1, :]), deps=[ev_z])
                    ev_b = e.op("tensor", lambda t: t.matmul(
                        ps_b[0:64, :], ones64[0:1, :], rrow[0:1, :],
                        start=True, stop=True), deps=[ev_r] + ev_con + war["psb"])
                    ev_ye = e.op("scalar", lambda s: s.copy(
                        ysc[:, :], ps_y[0:64, :]), deps=[ev_pv] + war["ysc"])
                    war["psy"] = [ev_ye]
                    ev_n = e.op("vector", lambda v: v.tensor_tensor(
                        ysc2[:, :], ysc[:, :], ps_b[0:64, :], OP.mult),
                        deps=[ev_ye, ev_b] + war["ysc2"])
                    war["rrow"] = [ev_b]
                    war["psb"] = [ev_n]
                    war["ysc"] = [ev_n]
                    ev_g = e.op("vector", lambda v, hp=hp, hb=hb, q0=q0: v.tensor_copy(
                        a2[hp][hb:hb + 64, q0:q0 + 512], ysc2[:, :]), deps=[ev_n])
                    war["ysc2"] = [ev_g]
                    head_evs.append(ev_g)
                    for j in range(4):
                        ks = (Q0 + q0) // 128 + j
                        stp = j == 3
                        ev_su = e.op("tensor", lambda t, ks=ks, hb=hb, hh=hh, stp=stp: t.matmul(
                            ps_stc[:, :], ktc[:, ks * 128 + hb:ks * 128 + hb + 64],
                            v2e[:, ks * 130 + hh * 65:ks * 130 + hh * 65 + 65],
                            start=False, stop=stp, skip_group_check=True),
                            deps=[ev_sc, ev_ss] + kq_evs + v_evs)
                        ev_su = e.op("tensor", lambda t, ks=ks, hb=hb, hh=hh, stp=stp: t.matmul(
                            ps_sts[:, :], kts[:, ks * 128 + hb:ks * 128 + hb + 64],
                            v2e[:, ks * 130 + hh * 65:ks * 130 + hh * 65 + 65],
                            start=False, stop=stp, skip_group_check=True),
                            deps=[ev_sc, ev_ss] + kq_evs + v_evs)
                war["st"] = [ev_sc, ev_ss, ev_su]
            war["qk"] = head_evs
            war["v2e"] = head_evs
            out_evs.extend(head_evs)
    at_evs = out_evs

    ow_sb = [c.sb([128, D], BF16, f"ow{i}") for i in range(NT)]
    owev = [e.dma("gpsimd", out=ow_sb[i][:, :], in_=ow[i * 128:(i + 1) * 128, :],
                  deps=at_evs) for i in range(NT)]
    lnf = c.sb([128, 512], F32, "lnf3")
    prev3 = []
    with c.scope():
        pb = [c.ps([128, 512]) for _ in range(2)]
        pb_war = {0: at_evs, 1: at_evs}
        tmp = [c.sb([128, 512], F32) for _ in range(2)]
        tmp_war = {0: [], 1: []}
        for ci in range(HALF // 512):
            c0 = ci * 512
            ld3 = [e.dma("sync", out=xc[i][:, :],
                         in_=xT[i * 128:(i + 1) * 128, Q0 + c0:Q0 + c0 + 512],
                         deps=ln_evs + prev3) for i in range(NT)]
            ch3 = []

            def consume(m, evf, c0=c0, ch3=ch3):
                bank = m % 2
                ev = None
                for k in range(NT):
                    ev = e.op("tensor", lambda t, k=k, m=m, c0=c0, bank=bank: t.matmul(
                        pb[bank][:, :], ow_sb[k][:, m * 128:(m + 1) * 128],
                        a2[k][:, c0:c0 + 512],
                        start=(k == 0), stop=(k == NT - 1)),
                        deps=at_evs + owev + pb_war[bank])
                tb = tmp[bank]
                ev_1 = e.op("vector", lambda v, m=m, tb=tb, bank=bank: v.scalar_tensor_tensor(
                    tb[:, :], pb[bank][:, :], obc[m][:, 0:1], lnf[:, :],
                    OP.add, OP.add), deps=[evf, ev] + tmp_war[bank] + ld_c)
                ev_st = e.dma("sync", out=hT[m * 128:(m + 1) * 128, c0:c0 + 512],
                              in_=tb[:, :], deps=[ev_1])
                pb_war[bank] = [ev_1]
                tmp_war[bank] = [ev_st]
                ch3 += [ev_1, ev_st]
                return [ev_1]

            emit_layernorm_fm(c, xc, 512, None, deps=ld3 + at_evs,
                              shared_f32=lnf, consume_fn=consume)
            prev3 = ch3
    return c.finish()


def build_stage3b():
    c = Ctx("stage3b")
    e = c.e
    xT = c.inp("xT", [D, HALF])
    hT = c.inp("hT", [D, HALF])
    n2_col = c.inp("n2_col", [D, 1])
    w1 = c.inp("w1", [D, 4096])
    b1_col = c.inp("b1_col", [4096, 1])
    w2 = c.inp("w2", [4096, D])
    b2_col = c.inp("b2_col", [D, 1])
    yT = c.out("yT", [D, HALF])

    hx = [c.sb([128, HALF], F32, f"h{i}") for i in range(NT)]
    mi = [c.sb([128, HALF], BF16, f"mi{i}") for i in range(NT)]
    h1g = [c.sb([128, 512], BF16, f"h1g{i}") for i in range(32)]
    xc2 = [c.sb([128, 512], F32) for _ in range(2)]
    n2c = [c.sb([128, 1], F32) for _ in range(NT)]
    b1c = [c.sb([128, 1], F32) for _ in range(32)]
    b2c = [c.sb([128, 1], F32) for _ in range(NT)]
    w1p = [[c.sb([128, 128], BF16) for _ in range(NT)] for _ in range(2)]
    w2p = [c.sb([128, D], BF16) for _ in range(2)]

    ld = []
    for i in range(NT):
        ld.append(e.dma("sync", out=hx[i][:, :], in_=hT[i * 128:(i + 1) * 128, :]))
        ld.append(e.dma("sync", out=n2c[i][:, :], in_=n2_col[i * 128:(i + 1) * 128, :]))
        ld.append(e.dma("sync", out=b2c[i][:, :], in_=b2_col[i * 128:(i + 1) * 128, :]))
    for j in range(32):
        ld.append(e.dma("sync", out=b1c[j][:, :], in_=b1_col[j * 128:(j + 1) * 128, :]))

    rms_evs = emit_layernorm_fm(c, hx, HALF, mi, deps=ld, gamma=n2c, eps=1e-6,
                                rms=True)

    with c.scope():
        pbank = [c.ps([128, 512]) for _ in range(2)]
        obank = [c.ps([128, 512]) for _ in range(4)]
        pb_war = {0: [], 1: []}
        w1_war = {0: [], 1: []}
        w2_war = {0: [], 1: []}
        ob_war = {i: [] for i in range(4)}
        tmp = [c.sb([128, 512], F32) for _ in range(2)]
        tmp_war = {0: [], 1: []}
        xcw = {0: [], 1: []}
        for ci in range(HALF // 512):
            c0 = ci * 512
            g_evs = []
            for f1 in range(32):
                wp = f1 % 2
                wev = [e.dma("gpsimd", out=w1p[wp][k][:, :],
                             in_=w1[k * 128:(k + 1) * 128, f1 * 128:(f1 + 1) * 128],
                             deps=w1_war[wp]) for k in range(NT)]
                bank = f1 % 2
                ev = None
                for k in range(NT):
                    ev = e.op("tensor", lambda t, k=k, c0=c0, wp=wp, bank=bank: t.matmul(
                        pbank[bank][:, :], w1p[wp][k][:, :], mi[k][:, c0:c0 + 512],
                        start=(k == 0), stop=(k == NT - 1)),
                        deps=rms_evs + wev + pb_war[bank])
                w1_war[wp] = [ev]
                ev_g = e.op("scalar", lambda s, f1=f1, bank=bank: s.activation(
                    h1g[f1][:, :], pbank[bank][:, :], AF.Gelu_apprx_tanh,
                    bias=b1c[f1][:, 0:1]), deps=[ev] + ld + g_evs[-2:-1])
                pb_war[bank] = [ev_g]
                g_evs.append(ev_g)
            for grp in range(2):
                ms = list(range(grp * 4, grp * 4 + 4))
                ev_k = {m: None for m in ms}
                for k in range(32):
                    wp = k % 2
                    wev2 = [e.dma("gpsimd", out=w2p[wp][:, :],
                                  in_=w2[k * 128:(k + 1) * 128, :],
                                  deps=w2_war[wp])]
                    for m in ms:
                        bank = m - grp * 4
                        ev_k[m] = e.op("tensor", lambda t, k=k, m=m, wp=wp, bank=bank: t.matmul(
                            obank[bank][:, :], w2p[wp][:, m * 128:(m + 1) * 128],
                            h1g[k][:, :], start=(k == 0), stop=(k == 31)),
                            deps=g_evs + wev2 + ob_war[bank])
                    w2_war[wp] = [ev_k[m] for m in ms]
                for m in ms:
                    bank = m - grp * 4
                    tb = tmp[m % 2]
                    xb = xc2[m % 2]
                    ev_x = e.dma("sync", out=xb[:, :],
                                 in_=xT[m * 128:(m + 1) * 128, c0:c0 + 512],
                                 deps=xcw[m % 2])
                    ev_1 = e.op("vector", lambda v, m=m, c0=c0, tb=tb, bank=bank: v.scalar_tensor_tensor(
                        tb[:, :], obank[bank][:, :], b2c[m][:, 0:1],
                        hx[m][:, c0:c0 + 512], OP.add, OP.add),
                        deps=[ev_k[m]] + tmp_war[m % 2] + ld)
                    ev_2 = e.op("vector", lambda v, tb=tb, xb=xb: v.tensor_tensor(
                        tb[:, :], tb[:, :], xb[:, :], OP.add), deps=[ev_1, ev_x])
                    ev_st = e.dma("sync", out=yT[m * 128:(m + 1) * 128, c0:c0 + 512],
                                  in_=tb[:, :], deps=[ev_2])
                    ob_war[bank] = [ev_1]
                    tmp_war[m % 2] = [ev_st]
                    xcw[m % 2] = [ev_2]
    return c.finish()


def run_stage3(x_full, inp):
    nc_a = _get_prog("stage3a", build_stage3a)
    nc_b = _get_prog("stage3b", build_stage3b)
    qkvw = np.asarray(inp["L0_attn_b_qkv_w"])
    qkvb = np.asarray(inp["L0_attn_b_qkv_b"])
    ow = np.asarray(inp["L0_attn_b_out_w"])
    ob = np.asarray(inp["L0_attn_b_out_b"])
    n1 = np.asarray(inp["L0_attn_b_norm1_w"])
    n2 = np.asarray(inp["L0_attn_b_norm2_w"])
    w1 = np.asarray(inp["L0_attn_b_mlp_w1"])
    b1 = np.asarray(inp["L0_attn_b_mlp_b1"])
    w2 = np.asarray(inp["L0_attn_b_mlp_w2"])
    b2 = np.asarray(inp["L0_attn_b_mlp_b2"])
    pos = np.arange(T, dtype=np.float64)
    cw_g = np.cos(np.pi / 2 * pos / T).astype(np.float32)
    sw_g = np.sin(np.pi / 2 * pos / T).astype(np.float32)
    r = np.arange(512)
    kk = np.arange(512)
    dmask = (kk[:, None] <= r[None, :]).astype(np.float32)
    in_maps = []
    for core in range(N_CORES):
        b, h = core // 2, core % 2
        own = x_full[b, h * HALF:(h + 1) * HALF, :]
        if h == 0:
            prev = np.zeros_like(own)
            cw = np.concatenate([np.zeros(HALF, np.float32), cw_g[0:HALF]])
            sw = np.concatenate([np.zeros(HALF, np.float32), sw_g[0:HALF]])
        else:
            prev = x_full[b, 0:HALF, :]
            cw, sw = cw_g, sw_g
        xT = np.ascontiguousarray(np.concatenate([prev, own], axis=0).T)
        in_maps.append(dict(
            xT=xT, qkvw=qkvw,
            kvb_row=np.ascontiguousarray(qkvb[1024:3072].reshape(1, 2048)),
            qb_col=np.ascontiguousarray(qkvb[0:1024].reshape(D, 1)),
            kb_col=np.ascontiguousarray(qkvb[1024:2048].reshape(D, 1)),
            ob_col=np.ascontiguousarray(ob.reshape(D, 1)),
            n1_col=np.ascontiguousarray(n1.reshape(D, 1)),
            ow=ow,
            cosw_row=np.ascontiguousarray(cw.reshape(1, T)),
            sinw_row=np.ascontiguousarray(sw.reshape(1, T)),
            coswc=np.ascontiguousarray(cw.reshape(T // 128, 128).T),
            sinwc=np.ascontiguousarray(sw.reshape(T // 128, 128).T),
            dmask=dmask))
    res = run_bass_kernel_spmd(nc_a, in_maps, core_ids=list(range(N_CORES)))
    in_maps_b = []
    for core in range(N_CORES):
        b, h = core // 2, core % 2
        own = x_full[b, h * HALF:(h + 1) * HALF, :]
        in_maps_b.append(dict(
            xT=np.ascontiguousarray(own.T),
            hT=np.ascontiguousarray(res.results[core]["hT"]),
            n2_col=np.ascontiguousarray(n2.reshape(D, 1)),
            w1=w1, b1_col=np.ascontiguousarray(b1.reshape(4096, 1)),
            w2=w2, b2_col=np.ascontiguousarray(b2.reshape(D, 1))))
    res_b = run_bass_kernel_spmd(nc_b, in_maps_b, core_ids=list(range(N_CORES)))
    out = np.empty_like(x_full)
    for core in range(N_CORES):
        b, h = core // 2, core % 2
        out[b, h * HALF:(h + 1) * HALF, :] = res_b.results[core]["yT"].T
    return out


DBG_L1 = False
SER_MLP = False
SER_ATTN = False
TE = 2688
O0 = 640
NTE = TE // 128


def build_l1attn(apply_ln):
    c = Ctx(f"l1attn{int(apply_ln)}")
    e = c.e
    e.serialize = SER_ATTN
    xT = c.inp("xT", [D, TE])
    wqkv = c.inp("wqkv", [D, 1536])
    wo = c.inp("wo", [512, D])
    qs_row = c.inp("qs_row", [1, 64])
    ks_row = c.inp("ks_row", [1, 64])
    tqc_i = c.inp("tqc", [128, NTE * 64])
    tqs_i = c.inp("tqs", [128, NTE * 64])
    tkc_i = c.inp("tkc", [128, NTE * 64])
    tks_i = c.inp("tks", [128, NTE * 64])
    pmask = c.inp("pmask", [512, 512])
    dmask = c.inp("dmask", [512, 512])
    pm_col = c.inp("pm_col", [128, 1])
    id_in = c.inp("ident_in", [128, 128])
    ymT = c.out("ymT", [D, HALF])
    if DBG_L1:
        dbg_qf = c.out("dbg_qf", [128, 512])
        dbg_qn = c.out("dbg_qn", [128, 512])
        dbg_ns = c.out("dbg_ns", [128, 8])
        dbg_qr = c.out("dbg_qr", [128, 512])
        dbg_o2 = c.out("dbg_o2", [512, HALF])
        dbg_qT = c.out("dbg_qT", [128, HALF])
        dbg_kT = c.out("dbg_kT", [128, (NTE - 1) * 128])
        dbg_v = c.out("dbg_v", [128, (NTE - 1) * 520])

    xin = [c.sb([128, TE], BF16, f"xin{i}") for i in range(NT)]
    xc = [c.sb([128, 512], F32) for _ in range(NT)]
    wq_sb = [c.sb([128, 1536], BF16, f"wq{i}") for i in range(NT)]

    tqc = c.sb([128, 16 * 64], BF16, "tqc_s")
    tqs = c.sb([128, 16 * 64], BF16, "tqs_s")
    tkc = c.sb([128, 20 * 64], BF16, "tkc_s")
    tks = c.sb([128, 20 * 64], BF16, "tks_s")
    pms = [c.sb([128, 512], BF16) for _ in range(4)]
    dms = [c.sb([128, 512], BF16) for _ in range(4)]
    pm_sb = c.sb([128, 1], F32)
    qsb = c.sb([128, 64], BF16, "qsb")
    ksb = c.sb([128, 64], BF16, "ksb")
    qT = [c.sb([128, HALF], BF16, f"qT{p}") for p in range(4)]
    kT = [c.sb([128, (NTE - 1) * 128], BF16, f"kT{p}") for p in range(4)]
    v2e = c.sb([128, (NTE - 1) * 520], BF16, "v2e")
    o2 = [c.sb([128, HALF], BF16, f"o2_{p}") for p in range(4)]
    ident = c.sb([128, 128], BF16, "ident")
    qrow = c.sb([1, 64], F32)
    nb48 = c.sb([128, 1], F32)
    ones64 = c.sb([1, 64], F32)
    pstr = [c.sb([128, 512], BF16) for _ in range(2)]
    pstr2 = [c.sb([128, 512], BF16) for _ in range(2)]
    ysc = c.sb([64, 512], BF16)
    ysc2 = c.sb([64, 512], BF16)
    rrow = c.sb([1, 512], F32)
    qf = c.sb([128, 512], F32)
    qsq = c.sb([128, 512], BF16)
    nsum = c.sb([128, 8], F32)
    nsum2 = c.sb([128, 8], F32)
    qn = c.sb([128, 512], F32)
    qrot = [c.sb([128, 512], BF16) for _ in range(2)]
    rhs_scr = c.sb([128, 512], BF16)

    ev_con = [e.op("vector", lambda v: v.memset(nb48[:, :], -48.0)),
              e.op("vector", lambda v: v.memset(ones64[:, :], 1.0))]
    ld_c = [e.dma("gpsimd", out=ident[:, :], in_=id_in[:, :])]
    for t, ti, t0w in ((tqc, tqc_i, (5, 21)), (tqs, tqs_i, (5, 21)),
                       (tkc, tkc_i, (1, 21)), (tks, tks_i, (1, 21))):
        ld_c.append(e.dma("gpsimd", out=t[:, :], in_=ti[:, t0w[0] * 64:t0w[1] * 64]))
    ld_c.append(e.dma("sync", out=pm_sb[:, :], in_=pm_col[:, :]))
    for j in range(4):
        ld_c.append(e.dma("gpsimd", out=pms[j][:, :], in_=pmask[j * 128:(j + 1) * 128, :]))
        ld_c.append(e.dma("gpsimd", out=dms[j][:, :], in_=dmask[j * 128:(j + 1) * 128, :]))
    wev = []
    for i in range(NT):
        wev.append(e.dma("gpsimd", out=wq_sb[i][:, :], in_=wqkv[i * 128:(i + 1) * 128, :]))

    onesr_l = c.sb([1, 128], F32, "onesr_l")
    ev_orl = e.op("vector", lambda v: v.memset(onesr_l[:, :], 1.0))
    with c.scope():
        ps_t = c.ps([128, 64])
        ev_l = e.dma("sync", out=qrow[0:1, :], in_=qs_row[0:1, :])
        ev_m = e.op("tensor", lambda t: t.matmul(
            ps_t[:, :], onesr_l[0:1, :], qrow[0:1, :], start=True, stop=True),
            deps=[ev_l, ev_orl] + ev_con)
        ev_q1 = e.op("vector", lambda v: v.tensor_copy(qsb[:, :], ps_t[:, :]),
                     deps=[ev_m])
        ev_l2 = e.dma("sync", out=qrow[0:1, :], in_=ks_row[0:1, :], deps=[ev_m])
        ev_m2 = e.op("tensor", lambda t: t.matmul(
            ps_t[:, :], onesr_l[0:1, :], qrow[0:1, :], start=True, stop=True),
            deps=[ev_l2, ev_q1, ev_orl])
        ev_q2 = e.op("vector", lambda v: v.tensor_copy(ksb[:, :], ps_t[:, :]),
                     deps=[ev_m2])
        qsk_evs = [ev_q1, ev_q2]

    ln_evs = []
    if apply_ln:
        prev = []
        for ci in range(6):
            c0 = ci * 512
            w = min(512, TE - c0)
            ld = [e.dma("sync", out=xc[i][:, :w], in_=xT[i * 128:(i + 1) * 128, c0:c0 + w],
                        deps=prev) for i in range(NT)]
            evs = emit_layernorm_fm(c, xc, w, xin, deps=ld, col_off=c0)
            prev = evs
            ln_evs.extend(evs)
    else:
        for i in range(NT):
            ln_evs.append(e.dma("gpsimd", out=xin[i][:, :], in_=xT[i * 128:(i + 1) * 128, :]))

    def emit_side(t0, psum, tab_c, tab_s, scale_b, is_q, deps, war):
        out = qrot[0 if is_q else 1]
        t0 = t0 - (5 if is_q else 1)
        prev_scr = list(war["scr"])
        ev0 = e.op("vector", lambda v: v.tensor_copy(qf[:, :], psum[:, :]),
                   deps=deps + prev_scr)
        ev1 = e.op("scalar", lambda s: s.activation(qsq[:, :], qf[:, :], AF.Square),
                   deps=[ev0] + prev_scr)
        red = [ev1]
        for h in range(8):
            red.append(e.op("vector", lambda v, h=h: v.reduce_sum(
                nsum[:, h:h + 1], qsq[:, h * 64:(h + 1) * 64],
                mybir.AxisListType.X), deps=red[-1:]))
        ev3 = e.op("vector", lambda v: v.tensor_scalar_max(
            nsum2[:, :], nsum[:, :], 1e-24), deps=red[-1:] + prev_scr)
        ev2 = e.op("scalar", lambda s: s.activation(nsum2[:, :], nsum2[:, :], AF.Ln),
                   deps=[ev3])
        ev4 = e.op("scalar", lambda s: s.activation(nsum2[:, :], nsum2[:, :], AF.Exp,
                                                    scale=-0.5), deps=[ev2])
        qf3 = qf[:, :].rearrange("p (h d) -> p h d", d=64)
        qn3 = qn[:, :].rearrange("p (h d) -> p h d", d=64)
        scr3 = qsq[:, :].rearrange("p (h d) -> p h d", d=64)
        rstd_b = nsum2[:, 0:8].unsqueeze(2).broadcast_to([128, 8, 64])
        ev5 = e.op("vector", lambda v: v.tensor_tensor(qn3, qf3, rstd_b, OP.mult),
                   deps=[ev4])
        sc_b = scale_b[:, 0:64].unsqueeze(1).broadcast_to([128, 8, 64])
        ev6 = e.op("vector", lambda v: v.tensor_tensor(qn3, qn3, sc_b, OP.mult),
                   deps=[ev5] + qsk_evs)
        tc = tab_c[:, t0 * 64:(t0 + 1) * 64].unsqueeze(1).broadcast_to([128, 8, 64])
        ev7 = e.op("vector", lambda v: v.tensor_tensor(
            rhs_scr[:, :].rearrange("p (h d) -> p h d", d=64), qn3, tc, OP.mult),
            deps=[ev6] + ld_c)
        ts_lo = tab_s[:, t0 * 64:t0 * 64 + 32].unsqueeze(1).broadcast_to([128, 8, 32])
        ts_hi = tab_s[:, t0 * 64 + 32:t0 * 64 + 64].unsqueeze(1).broadcast_to([128, 8, 32])
        ev8 = e.op("vector", lambda v: v.scalar_tensor_tensor(
            scr3[:, :, 0:32], qn3[:, :, 32:64], -1.0, ts_lo, OP.mult, OP.mult),
            deps=[ev7])
        ev9 = e.op("vector", lambda v: v.tensor_tensor(
            scr3[:, :, 32:64], qn3[:, :, 0:32], ts_hi, OP.mult), deps=[ev8])
        ev10 = e.op("vector", lambda v, out=out: v.tensor_tensor(
            out[:, :], rhs_scr[:, :], qsq[:, :], OP.add),
            deps=[ev9] + war["rot" + ("q" if is_q else "k")])
        war["scr"] = [ev1, ev4, ev10]
        return ev10

    qT_evs, kT_evs, v_evs = [], [], []
    with c.scope():
        ps_q = c.ps([128, 512])
        ps_k = c.ps([128, 512])
        ps_v = c.ps([128, 512])
        ps_tr = c.ps([128, 128], BF16)
        war = {"q": [], "k": [], "v": [], "tr": [], "scr": [], "rotq": [],
               "rotk": []}
        v2e_3d = v2e[:, :].rearrange("p (t c) -> p t c", c=65)
        ev_vo = [e.op("vector", lambda v: v.memset(v2e_3d[:, :, 64:65], 1.0))]
        ev_vo.append(e.op("vector", lambda v: v.tensor_copy(
            v2e_3d[:, 0:32, 64:65],
            pm_sb[:, 0:1].unsqueeze(1).broadcast_to([128, 32, 1])),
            deps=ev_vo + ld_c))
        for t0 in range(1, NTE):
            own = t0 >= 5
            sides = [("k", ps_k, 512), ("v", ps_v, 1024)]
            if own:
                sides = [("q", ps_q, 0)] + sides
            mm_evs = {}
            for nm, ps, col0 in sides:
                ev = None
                for k in range(NT):
                    if k < 4:
                        lo = t0 * 128
                    else:
                        lo = t0 * 128 - 1
                    ev = e.op("tensor", lambda t, k=k, ps=ps, col0=col0, lo=lo: t.matmul(
                        ps[:, :], xin[k][:, lo:lo + 128], wq_sb[k][:, col0:col0 + 512],
                        start=(k == 0), stop=(k == NT - 1)),
                        deps=ln_evs + wev + war[nm])
                mm_evs[nm] = ev
            b0v = (t0 - 1) * 520
            ev_v = None
            for h8 in range(8):
                ev_v = e.op("vector", lambda v, b0=b0v, h=h8: v.tensor_copy(
                    v2e[:, b0 + h * 65:b0 + h * 65 + 64],
                    ps_v[:, h * 64:(h + 1) * 64]),
                    deps=([mm_evs["v"]] + ev_vo + v_evs[-1:]) if h8 == 0 else [ev_v])
            war["v"] = [ev_v]
            v_evs.append(ev_v)
            ev_k = emit_side(t0, ps_k, tkc, tks, ksb, False, [mm_evs["k"]], war)
            war["k"] = [ev_k]
            if DBG_L1 and t0 == NTE - 1:
                dd = [e.dma("gpsimd", out=dbg_qf[:, :], in_=qf[:, :], deps=[ev_k]),
                      e.dma("gpsimd", out=dbg_qn[:, :], in_=qn[:, :], deps=[ev_k]),
                      e.dma("gpsimd", out=dbg_ns[:, :], in_=nsum2[:, :], deps=[ev_k])]
                war["scr"] = war["scr"] + dd
            if own:
                ev_q = emit_side(t0, ps_q, tqc, tqs, qsb, True, [mm_evs["q"]], war)
                war["q"] = [ev_q]
            else:
                ev_q = None
            tr_evs = []
            for p in range(4):
                ev_t = e.op("tensor", lambda t, p=p: t.transpose(
                    ps_tr[:, :], qrot[1][:, p * 128:(p + 1) * 128], ident[:, :]),
                    deps=[ev_k, ld_c[0]] + war["tr"])
                ev_te = e.op("scalar", lambda sc, p=p, t0=t0: sc.copy(
                    kT[p][:, (t0 - 1) * 128:t0 * 128], ps_tr[:, :]), deps=[ev_t])
                war["tr"] = [ev_te]
                tr_evs.append(ev_te)
                if own:
                    ev_t2 = e.op("tensor", lambda t, p=p: t.transpose(
                        ps_tr[:, :], qrot[0][:, p * 128:(p + 1) * 128], ident[:, :]),
                        deps=[ev_q, ld_c[0]] + war["tr"])
                    ev_te2 = e.op("scalar", lambda sc, p=p, t0=t0: sc.copy(
                        qT[p][:, (t0 - 5) * 128:(t0 - 4) * 128], ps_tr[:, :]),
                        deps=[ev_t2])
                    war["tr"] = [ev_te2]
                    tr_evs.append(ev_te2)
            war["rotk"] = tr_evs
            war["rotq"] = tr_evs
            kT_evs += tr_evs
        pp_done = kT_evs + v_evs

        ps_y = c.ps([128, 512])
        ps_b = c.ps([128, 512])
        awar = {"ps0": [], "ps1": [], "psy": [], "psb": [], "p0": [], "p1": [],
                "ysc": [], "ysc2": [], "rrow": [], "o2": []}
        at_evs = []
        for p in range(4):
            for hh in range(2):
                hb = hh * 64
                for w in range(4):
                    q0 = w * 512
                    ev_pv = None
                    for j in range(8):
                        kcol = (w * 4 + j) * 128
                        bank = j % 2
                        msk = pms[j] if j < 4 else dms[j - 4]
                        ev_s = e.op("tensor", lambda t, kcol=kcol, q0=q0, hb=hb, p=p, bank=bank: t.matmul(
                            [ps_q, ps_k][bank][:, :],
                            kT[p][hb:hb + 64, kcol:kcol + 128],
                            qT[p][hb:hb + 64, q0:q0 + 512], start=True, stop=True),
                            deps=pp_done + awar[f"ps{bank}"] + awar[f"p{bank}"])
                        ev_p = e.op("scalar", lambda s, bank=bank: s.activation(
                            pstr[bank][:, :], [ps_q, ps_k][bank][:, :], AF.Exp,
                            bias=nb48[:, 0:1], scale=8.0), deps=[ev_s] + ev_con)
                        awar[f"ps{bank}"] = [ev_p]
                        ev_mk = e.op("vector", lambda v, msk=msk, bank=bank: v.tensor_tensor(
                            pstr2[bank][:, :], pstr[bank][:, :], msk[:, :], OP.mult),
                            deps=[ev_p] + ld_c + awar[f"p{bank}"])
                        vsl = v2e[:, (w * 4 + j) * 520 + hh * 65 + p * 130:
                                  (w * 4 + j) * 520 + hh * 65 + p * 130 + 65]
                        ev_pv = e.op("tensor", lambda t, vsl=vsl, bank=bank, first=(j == 0), last=(j == 7): t.matmul(
                            ps_y[0:65, :], vsl, pstr2[bank][:, :],
                            start=first, stop=last),
                            deps=[ev_mk] + v_evs + (awar["psy"] if j == 0 else []))
                        awar[f"p{bank}"] = [ev_pv]
                    ev_r = e.op("vector", lambda v: v.reciprocal(
                        rrow[0:1, :], ps_y[64:65, :]), deps=[ev_pv] + awar["rrow"])
                    ev_b = e.op("tensor", lambda t: t.matmul(
                        ps_b[0:64, :], ones64[0:1, :], rrow[0:1, :],
                        start=True, stop=True), deps=[ev_r] + ev_con + awar["psb"])
                    ev_ye = e.op("scalar", lambda s: s.copy(
                        ysc[:, :], ps_y[0:64, :]), deps=[ev_pv] + awar["ysc"])
                    awar["psy"] = [ev_ye]
                    ev_n = e.op("vector", lambda v: v.tensor_tensor(
                        ysc2[:, :], ysc[:, :], ps_b[0:64, :], OP.mult),
                        deps=[ev_ye, ev_b] + awar["ysc2"])
                    awar["rrow"] = [ev_b]
                    awar["psb"] = [ev_n]
                    awar["ysc"] = [ev_n]
                    ev_g = e.op("vector", lambda v, p=p, hb=hb, q0=q0: v.tensor_copy(
                        o2[p][hb:hb + 64, q0:q0 + 512], ysc2[:, :]), deps=[ev_n])
                    awar["ysc2"] = [ev_g]
                    at_evs.append(ev_g)

    dbg_evs = []
    if DBG_L1:
        for p in range(4):
            dbg_evs.append(e.dma("gpsimd", out=dbg_o2[p * 128:(p + 1) * 128, :],
                                 in_=o2[p][:, :], deps=at_evs))
        dbg_evs.append(e.dma("gpsimd", out=dbg_qT[:, :], in_=qT[0][:, :], deps=at_evs))
        dbg_evs.append(e.dma("gpsimd", out=dbg_kT[:, :], in_=kT[0][:, :], deps=at_evs))
        dbg_evs.append(e.dma("gpsimd", out=dbg_v[:, :], in_=v2e[:, :], deps=at_evs))
        dbg_evs.append(e.dma("gpsimd", out=dbg_qr[:, :], in_=qrot[1][:, :], deps=at_evs))
    wo_sb = [qT[i] for i in range(4)]
    woev = [e.dma("gpsimd", out=wo_sb[i][:, 0:D], in_=wo[i * 128:(i + 1) * 128, :],
                  deps=at_evs + dbg_evs) for i in range(4)]
    lnf = qf
    prev3 = []
    with c.scope():
        pb = [c.ps([128, 512]) for _ in range(2)]
        pb_war = {0: at_evs, 1: at_evs}
        tmp = [c.sb([128, 512], F32) for _ in range(2)]
        tmp_war = {0: [], 1: []}
        for ci in range(HALF // 512):
            c0 = ci * 512
            ld3 = [e.dma("sync", out=xc[i][:, :512],
                         in_=xT[i * 128:(i + 1) * 128, O0 + c0:O0 + c0 + 512],
                         deps=(ln_evs if apply_ln else []) + prev3) for i in range(NT)]
            ch3 = []

            def consume(m, evf, c0=c0, ch3=ch3, ld3=None):
                bank = m % 2
                ev = None
                for k in range(4):
                    ev = e.op("tensor", lambda t, k=k, m=m, c0=c0, bank=bank: t.matmul(
                        pb[bank][:, :], wo_sb[k][:, m * 128:(m + 1) * 128],
                        o2[k][:, c0:c0 + 512],
                        start=(k == 0), stop=(k == 3)),
                        deps=at_evs + woev + pb_war[bank])
                tb = tmp[bank]
                src = lnf[:, :] if evf is not None else xc[m][:, :512]
                ev_1 = e.op("vector", lambda v, tb=tb, bank=bank, src=src: v.tensor_tensor(
                    tb[:, :], src, pb[bank][:, :], OP.add),
                    deps=([evf] if evf is not None else (ld3 or [])) + [ev] + tmp_war[bank])
                ev_st = e.dma("sync", out=ymT[m * 128:(m + 1) * 128, c0:c0 + 512],
                              in_=tb[:, :], deps=[ev_1])
                pb_war[bank] = [ev_1]
                tmp_war[bank] = [ev_st]
                ch3 += [ev_1, ev_st]
                return [ev_1]

            if apply_ln:
                emit_layernorm_fm(c, xc, 512, None, deps=ld3 + at_evs,
                                  shared_f32=lnf, consume_fn=consume)
            else:
                for m in range(NT):
                    consume(m, None, ld3=[ld3[m]])
            prev3 = ch3
    return c.finish()


def build_l1mlp(last_layer):
    c = Ctx(f"l1mlp{int(last_layer)}")
    e = c.e
    e.serialize = SER_MLP
    HID = 2730
    NF = 22
    xmT = c.inp("xmT", [D, 1 + HALF])
    if last_layer:
        xoT = c.inp("xoT", [D, HALF])
    lg_col = c.inp("lg_col", [D, 1])
    lb_col = c.inp("lb_col", [D, 1])
    w1a = c.inp("w1a", [D, HID])
    w1g = c.inp("w1g", [D, HID])
    b1a_col = c.inp("b1a_col", [HID, 1])
    b1g_col = c.inp("b1g_col", [HID, 1])
    w2 = c.inp("w2", [HID, D])
    b2_col = c.inp("b2_col", [D, 1])
    yT = c.out("yT", [D, HALF])

    xm = [c.sb([128, 1 + HALF], F32, f"xm{i}") for i in range(NT)]
    mln = [c.sb([128, HALF], BF16, f"mln{i}") for i in range(NT)]
    lgc = [c.sb([128, 1], F32) for _ in range(NT)]
    lbc = [c.sb([128, 1], F32) for _ in range(NT)]
    b2c = [c.sb([128, 1], F32) for _ in range(NT)]
    b1ac = [c.sb([128, 1], F32) for _ in range(NF)]
    b1gc = [c.sb([128, 1], F32) for _ in range(NF)]
    w1ap = [[c.sb([128, 128], BF16) for _ in range(NT)] for _ in range(2)]
    w1gp = [[c.sb([128, 128], BF16) for _ in range(NT)] for _ in range(2)]
    w2p = [c.sb([128, D], BF16) for _ in range(2)]
    mf = [c.sb([128, 512], BF16, f"mf{j}") for j in range(NF)]
    gsc = [c.sb([128, 512], BF16) for _ in range(2)]
    asc = [c.sb([128, 512], BF16) for _ in range(2)]
    xo_c = [c.sb([128, 512], F32) for _ in range(2)]
    tmp = [c.sb([128, 512], F32) for _ in range(2)]

    ld = []
    for i in range(NT):
        ld.append(e.dma("sync", out=xm[i][:, :], in_=xmT[i * 128:(i + 1) * 128, :]))
        ld.append(e.dma("sync", out=lgc[i][:, :], in_=lg_col[i * 128:(i + 1) * 128, :]))
        ld.append(e.dma("sync", out=lbc[i][:, :], in_=lb_col[i * 128:(i + 1) * 128, :]))
        ld.append(e.dma("sync", out=b2c[i][:, :], in_=b2_col[i * 128:(i + 1) * 128, :]))
    for j in range(NF):
        r = min(128, HID - j * 128)
        ld.append(e.dma("sync", out=b1ac[j][0:r, :], in_=b1a_col[j * 128:j * 128 + r, :]))
        ld.append(e.dma("sync", out=b1gc[j][0:r, :], in_=b1g_col[j * 128:j * 128 + r, :]))

    xv = [xm[i][:, 1:1 + HALF] if i < 4 else xm[i][:, 0:HALF] for i in range(NT)]
    ln_evs = emit_layernorm_fm(c, xv, HALF, mln, deps=ld, ln_gamma=lgc, ln_beta=lbc)

    with c.scope():
        pa = c.ps([128, 512])
        pg = c.ps([128, 512])
        po = [c.ps([128, 512]) for _ in range(4)]
        pa_war, pg_war = [], []
        po_war = {i: [] for i in range(4)}
        w1_war = {0: [], 1: []}
        w2_war = {0: [], 1: []}
        tmp_war = {0: [], 1: []}
        xo_war = {0: [], 1: []}
        mf_war = []
        for ci in range(HALF // 512):
            c0 = ci * 512
            g_evs = []
            for f1 in range(NF):
                r = min(128, HID - f1 * 128)
                wp = f1 % 2
                wevA = [e.dma("gpsimd", out=w1ap[wp][k][:, 0:r],
                              in_=w1a[k * 128:(k + 1) * 128, f1 * 128:f1 * 128 + r],
                              deps=w1_war[wp]) for k in range(NT)]
                wevG = [e.dma("gpsimd", out=w1gp[wp][k][:, 0:r],
                              in_=w1g[k * 128:(k + 1) * 128, f1 * 128:f1 * 128 + r],
                              deps=w1_war[wp]) for k in range(NT)]
                evA = evG = None
                for k in range(NT):
                    evA = e.op("tensor", lambda t, k=k, c0=c0, wp=wp, r=r: t.matmul(
                        pa[0:r, :], w1ap[wp][k][:, 0:r], mln[k][:, c0:c0 + 512],
                        start=(k == 0), stop=(k == NT - 1)),
                        deps=ln_evs + wevA + pa_war)
                for k in range(NT):
                    evG = e.op("tensor", lambda t, k=k, c0=c0, wp=wp, r=r: t.matmul(
                        pg[0:r, :], w1gp[wp][k][:, 0:r], mln[k][:, c0:c0 + 512],
                        start=(k == 0), stop=(k == NT - 1)),
                        deps=ln_evs + wevG + pg_war)
                w1_war[wp] = [evA, evG]
                sb_g = gsc[f1 % 2]
                sb_a = asc[f1 % 2]
                ev_g = e.op("scalar", lambda s, f1=f1, r=r, sb_g=sb_g: s.activation(
                    sb_g[0:r, :], pg[0:r, :], AF.Gelu, bias=b1gc[f1][0:r, 0:1]),
                    deps=[evG] + ld + g_evs[-2:-1])
                pg_war = [ev_g]
                ev_a = e.op("vector", lambda v, f1=f1, r=r, sb_a=sb_a: v.tensor_scalar_add(
                    sb_a[0:r, :], pa[0:r, :], b1ac[f1][0:r, 0:1]),
                    deps=[evA] + ld + g_evs[-2:-1])
                pa_war = [ev_a]
                ev_m = e.op("vector", lambda v, f1=f1, r=r, sb_a=sb_a, sb_g=sb_g: v.tensor_tensor(
                    mf[f1][0:r, :], sb_a[0:r, :], sb_g[0:r, :], OP.mult),
                    deps=[ev_g, ev_a] + mf_war)
                g_evs.append(ev_m)
            mf_war = []
            for grp in range(2):
                ms = list(range(grp * 4, grp * 4 + 4))
                ev_k = {m: None for m in ms}
                for k in range(NF):
                    r = min(128, HID - k * 128)
                    wp = k % 2
                    wev2 = [e.dma("gpsimd", out=w2p[wp][0:r, :],
                                  in_=w2[k * 128:k * 128 + r, :],
                                  deps=w2_war[wp])]
                    for m in ms:
                        bank = m - grp * 4
                        ev_k[m] = e.op("tensor", lambda t, k=k, m=m, r=r, wp=wp, bank=bank: t.matmul(
                            po[bank][:, :], w2p[wp][0:r, m * 128:(m + 1) * 128],
                            mf[k][0:r, :], start=(k == 0), stop=(k == NF - 1)),
                            deps=g_evs + wev2 + po_war[bank])
                    w2_war[wp] = [ev_k[m] for m in ms]
                    mf_war.append(ev_k[ms[-1]])
                for m in ms:
                    bank = m - grp * 4
                    tb = tmp[m % 2]
                    ev_1 = e.op("vector", lambda v, m=m, c0=c0, tb=tb, bank=bank: v.scalar_tensor_tensor(
                        tb[:, :], po[bank][:, :], b2c[m][:, 0:1],
                        xm[m][:, 1 + c0:1 + c0 + 512], OP.add, OP.add),
                        deps=[ev_k[m]] + ld + tmp_war[m % 2])
                    if last_layer:
                        xb = xo_c[m % 2]
                        ev_x = e.dma("sync", out=xb[:, :],
                                     in_=xoT[m * 128:(m + 1) * 128, c0:c0 + 512],
                                     deps=xo_war[m % 2])
                        ev_1 = e.op("vector", lambda v, tb=tb, xb=xb: v.tensor_tensor(
                            tb[:, :], tb[:, :], xb[:, :], OP.add), deps=[ev_1, ev_x])
                        xo_war[m % 2] = [ev_1]
                    ev_st = e.dma("sync", out=yT[m * 128:(m + 1) * 128, c0:c0 + 512],
                                  in_=tb[:, :], deps=[ev_1])
                    po_war[bank] = [ev_1]
                    tmp_war[m % 2] = [ev_st]
    return c.finish()


def _l1ffn_tables(h):
    n, DIM_HEAD, WINDOW = T, 64, 512
    half_idx = np.arange(0, DIM_HEAD, 2, dtype=np.float64)
    inv_freq = 1.0 / 10000.0 ** (half_idx / DIM_HEAD)
    g0 = h * HALF - O0
    pos = np.arange(g0, g0 + TE, dtype=np.float64)
    freqs = pos[:, None] * inv_freq[None, :]
    freqs = np.concatenate([freqs, freqs], axis=-1)
    base = (half_idx + 0.4 * DIM_HEAD) / (1.4 * DIM_HEAD)
    power = (pos - n // 2) / float(WINDOW // 2)
    scale = base[None, :] ** power[:, None]
    scale = np.concatenate([scale, scale], axis=-1)
    cos, sin = np.cos(freqs), np.sin(freqs)
    tqc = (cos * scale).astype(np.float32)
    tqs = (sin * scale).astype(np.float32)
    tkc = (cos / scale).astype(np.float32)
    tks = (sin / scale).astype(np.float32)
    if h == 0:
        tqc[:O0] = tqs[:O0] = tkc[:O0] = tks[:O0] = 0.0
    def lay(tab):
        t3 = tab.reshape(NTE, 128, 64)
        return np.ascontiguousarray(t3.transpose(1, 0, 2).reshape(128, NTE * 64))
    return lay(tqc), lay(tqs), lay(tkc), lay(tks)


_L1_MASKS = None


def _l1_masks():
    global _L1_MASKS
    if _L1_MASKS is None:
        kk = np.arange(512)[:, None]
        r = np.arange(512)[None, :]
        pmask = (kk >= r).astype(np.float32)
        dmask = (kk <= r).astype(np.float32)
        _L1_MASKS = (pmask, dmask, np.zeros((512, 512), np.float32))
    return _L1_MASKS


def run_stage68(x_full, inp):
    nc_a1 = _get_prog("l1attn1", lambda: build_l1attn(True))
    nc_a0 = _get_prog("l1attn0", lambda: build_l1attn(False))
    nc_m0 = _get_prog("l1mlp0", lambda: build_l1mlp(False))
    nc_m1 = _get_prog("l1mlp1", lambda: build_l1mlp(True))
    pmask, dmask, zmask = _l1_masks()
    ident = np.eye(128, dtype=np.float32)
    x_cur = x_full
    HID = 2730
    for li in range(2):
        wqkv = np.ascontiguousarray(inp["L1_ffn_wqkv"][li])
        wo = np.ascontiguousarray(inp["L1_ffn_wo"][li])
        qs = np.ascontiguousarray(np.asarray(inp["L1_ffn_q_scale"][li]).reshape(1, 64))
        ks = np.ascontiguousarray(np.asarray(inp["L1_ffn_k_scale"][li]).reshape(1, 64))
        lg = np.ascontiguousarray(np.asarray(inp["L1_ffn_ln_g"][li]).reshape(D, 1))
        lb = np.ascontiguousarray(np.asarray(inp["L1_ffn_ln_b"][li]).reshape(D, 1))
        w1 = np.asarray(inp["L1_ffn_w1"][li])
        b1 = np.asarray(inp["L1_ffn_b1"][li])
        w2 = np.ascontiguousarray(np.asarray(inp["L1_ffn_w2"][li]))
        b2 = np.ascontiguousarray(np.asarray(inp["L1_ffn_b2"][li]).reshape(D, 1))
        w1a = np.ascontiguousarray(w1[:, :HID])
        w1g = np.ascontiguousarray(w1[:, HID:])
        b1a = np.ascontiguousarray(b1[:HID].reshape(HID, 1))
        b1g = np.ascontiguousarray(b1[HID:].reshape(HID, 1))
        in_maps = []
        for core in range(N_CORES):
            b, h = core // 2, core % 2
            own = x_cur[b, h * HALF:(h + 1) * HALF, :]
            if h == 0:
                ptail = np.zeros((O0, D), np.float32)
                pmc = np.zeros((128, 1), np.float32)
            else:
                ptail = x_cur[b, HALF - O0:HALF, :]
                pmc = np.ones((128, 1), np.float32)
            xT = np.ascontiguousarray(
                np.concatenate([ptail, own], axis=0).T.astype(np.float32))
            tqc, tqs, tkc, tks = _l1ffn_tables(h)
            in_maps.append(dict(xT=xT, wqkv=wqkv, wo=wo, qs_row=qs, ks_row=ks,
                                tqc=tqc, tqs=tqs, tkc=tkc, tks=tks,
                                pmask=pmask, dmask=dmask, pm_col=pmc,
                                ident_in=ident))
        nc_a = nc_a1 if li == 0 else nc_a0
        res = run_bass_kernel_spmd(nc_a, in_maps, core_ids=list(range(N_CORES)))
        xm = np.empty_like(x_full)
        for core in range(N_CORES):
            b, h = core // 2, core % 2
            xm[b, h * HALF:(h + 1) * HALF, :] = res.results[core]["ymT"].T
        in_maps = []
        for core in range(N_CORES):
            b, h = core // 2, core % 2
            own = xm[b, h * HALF:(h + 1) * HALF, :]
            halo = (np.zeros((1, D), np.float32) if h == 0
                    else xm[b, HALF - 1:HALF, :])
            xmT = np.ascontiguousarray(np.concatenate([halo, own], axis=0).T)
            m = dict(xmT=xmT, lg_col=lg, lb_col=lb, w1a=w1a, w1g=w1g,
                     b1a_col=b1a, b1g_col=b1g, w2=w2, b2_col=b2)
            if li == 1:
                m["xoT"] = np.ascontiguousarray(
                    x_full[b, h * HALF:(h + 1) * HALF, :].T)
            in_maps.append(m)
        nc_m = nc_m1 if li == 1 else nc_m0
        res = run_bass_kernel_spmd(nc_m, in_maps, core_ids=list(range(N_CORES)))
        x_new = np.empty_like(x_full)
        for core in range(N_CORES):
            b, h = core // 2, core % 2
            x_new[b, h * HALF:(h + 1) * HALF, :] = res.results[core]["yT"].T
        x_cur = x_new
    return x_cur


_ONES_ROW = {}


def ones_row_g(c):
    if c.name not in _ONES_ROW:
        t = c.sb([1, 128], F32, "ones_row_g")
        c.e.op("vector", lambda v: v.memset(t[:, :], 1.0))
        _ONES_ROW[c.name] = t
    return _ONES_ROW[c.name]



_PROG_CACHE = {}


def _get_prog(name, builder):
    if name not in _PROG_CACHE:
        _PROG_CACHE[name] = builder()
    return _PROG_CACHE[name]


def run_stage2(x_full, wr, wk, wv, mu_r, mu_k):
    nc = _get_prog("stage2", build_stage2)
    in_maps = []
    wrT = np.ascontiguousarray(wr.T)
    wkT = np.ascontiguousarray(wk.T)
    wvT = np.ascontiguousarray(wv.T)
    murc = np.ascontiguousarray(mu_r.reshape(D, 1))
    mukc = np.ascontiguousarray(mu_k.reshape(D, 1))
    for core in range(N_CORES):
        b, h = core // 2, core % 2
        xT = np.ascontiguousarray(x_full[b, h * HALF:(h + 1) * HALF, :].T)
        if h == 0:
            prevT = np.zeros((D, 1), np.float32)
        else:
            prevT = np.ascontiguousarray(x_full[b, h * HALF - 1, :].reshape(D, 1))
        in_maps.append(dict(xT=xT, prevT=prevT, wrT=wrT, wkT=wkT, wvT=wvT,
                            mur=murc, muk=mukc))
    res = run_bass_kernel_spmd(nc, in_maps, core_ids=list(range(N_CORES)))
    out = np.empty_like(x_full)
    for core in range(N_CORES):
        b, h = core // 2, core % 2
        out[b, h * HALF:(h + 1) * HALF, :] = res.results[core]["yT"].T
    return out


def run_stage1(x_full, wq, wk, wv, wpost):
    nc = _get_prog("stage1", build_stage1)
    in_maps = []
    for core in range(N_CORES):
        b, h = core // 2, core % 2
        own = x_full[b, h * HALF:(h + 1) * HALF, :]
        other = x_full[b, (1 - h) * HALF:(2 - h) * HALF, :]
        xT = np.ascontiguousarray(np.concatenate([own, other], axis=0).T)
        in_maps.append(dict(xT=xT, wq=wq, wk=wk, wv=wv, wpost=wpost))
    res = run_bass_kernel_spmd(nc, in_maps, core_ids=list(range(N_CORES)))
    out = np.empty_like(x_full)
    for core in range(N_CORES):
        b, h = core // 2, core % 2
        out[b, h * HALF:(h + 1) * HALF, :] = res.results[core]["yT"].T
    return out


def kernel(**inputs):
    g = {k: np.ascontiguousarray(np.asarray(v, dtype=np.float32))
         for k, v in inputs.items()}
    x = g["L0_attn_a_X"]
    x = run_stage1(x, g["L0_attn_a_Wq"], g["L0_attn_a_Wk"], g["L0_attn_a_Wv"],
                   g["L0_attn_a_Wpost"])
    x = run_stage2(x, g["L0_ffn_W_r"], g["L0_ffn_W_k"], g["L0_ffn_W_v"],
                   g["L0_ffn_mu_r"], g["L0_ffn_mu_k"])
    x = run_stage3(x, g)
    x = run_stage2(x, g["L0_ffn_W_r"], g["L0_ffn_W_k"], g["L0_ffn_W_v"],
                   g["L0_ffn_mu_r"], g["L0_ffn_mu_k"])
    x = run_stage5(x, g["L1_attn_a_W"])
    x = run_stage68(x, g)
    x = run_stage7(x, g["L1_attn_b_Wq"], g["L1_attn_b_Wk"], g["L1_attn_b_Wv"],
                   g["L1_attn_b_Wo"])
    x = run_stage68(x, g)
    return x


# version 06
# speedup vs baseline: 1.0005x; 1.0005x over previous
import numpy as np
from contextlib import ExitStack

import concourse.bass as bass
import concourse.mybir as mybir
from concourse.bass_utils import run_bass_kernel_spmd

F32 = mybir.dt.float32
BF16 = mybir.dt.bfloat16
AF = mybir.ActivationFunctionType
OP = mybir.AluOpType

N_CORES = 8
B, T, D = 4, 4096, 1024
HALF = T // 2
NT = D // 128

ENGINES = ("tensor", "vector", "scalar", "gpsimd", "sync")
DMA_QUEUES = ("sync", "gpsimd", "scalar")


class Ev:
    __slots__ = ("kind", "name", "count")

    def __init__(self, kind, name, count):
        self.kind, self.name, self.count = kind, name, count


class Emitter:

    NSEM = 8

    def __init__(self, nc):
        self.nc = nc
        self.ops = []
        self.serialize = False
        self.ecount = {e: 0 for e in ENGINES}
        self.dcount = {(q, i): 0 for q in DMA_QUEUES for i in range(self.NSEM)}
        self.didx = {q: 0 for q in DMA_QUEUES}

    def op(self, engine, fn, deps=()):
        if self.serialize:
            deps = list(self._flat(deps)) + self.barrier()
        deps = list(self._flat(deps))
        if self.ecount[engine] > 0:
            deps.append(Ev("e", engine, self.ecount[engine]))
        self.ecount[engine] += 1
        ev = Ev("e", engine, self.ecount[engine])
        self.ops.append((engine, False, fn, tuple(deps), ev))
        return ev

    def dma(self, queue, out, in_, deps=()):
        if self.serialize:
            deps = list(self._flat(deps)) + self.barrier()
        slot = self.didx[queue] % self.NSEM
        self.didx[queue] += 1
        prev = self.dcount[(queue, slot)]
        self.dcount[(queue, slot)] = prev + 16
        ev = Ev("d", (queue, slot), prev + 16)
        pre = Ev("d", (queue, slot), prev) if prev else None

        def fn(eng, out=out, in_=in_):
            return eng.dma_start(out=out, in_=in_)

        deps = list(self._flat(deps))
        if pre is not None:
            deps.append(pre)
        self.ops.append((queue, True, fn, tuple(deps), ev))
        return ev

    @staticmethod
    def _flat(deps):
        out = []
        for d in deps:
            if d is None:
                continue
            if isinstance(d, Ev):
                out.append(d)
            else:
                out.extend(x for x in d if x is not None)
        return tuple(out)

    def barrier(self):
        evs = [Ev("e", e, c) for e, c in self.ecount.items() if c > 0]
        evs += [Ev("d", qs, c) for qs, c in self.dcount.items() if c > 0]
        return evs

    def finalize(self):
        nc = self.nc
        with ExitStack() as ctx:
            esem = {e: ctx.enter_context(nc.semaphore(f"es_{e}")) for e in ENGINES}
            dsem = {}
            for q in DMA_QUEUES:
                for i in range(self.NSEM):
                    dsem[(q, i)] = ctx.enter_context(nc.semaphore(f"ds_{q}{i}"))
            block = ctx.enter_context(nc.Block())
            runs = []
            for item in self.ops:
                if runs and runs[-1][0] == item[0]:
                    runs[-1][1].append(item)
                else:
                    runs.append((item[0], [item]))
            for engine, items in runs:
                section = getattr(block, engine)

                def body(eng, items=items):
                    for _, is_dma, fn, deps, ev in items:
                        best = {}
                        for d in deps:
                            key = (d.kind, d.name if isinstance(d.name, str)
                                   else d.name)
                            if best.get(key, -1) < d.count:
                                best[key] = d.count
                        for (kind, name), count in best.items():
                            sem = esem[name] if kind == "e" else dsem[name]
                            eng.wait_ge(sem, count)
                        ins = fn(eng)
                        if ev is not None:
                            if ev.kind == "e":
                                ins.then_inc(esem[ev.name], 1)
                            else:
                                ins.then_inc(dsem[ev.name], 16)

                section(body)

    def wait_tail(self):
        evs = self.barrier()

        def fn(eng):
            return eng.nop()

        self.ops.append(("sync", False, fn, tuple(evs), None))


def cdiv(a, b):
    return -(-a // b)



class Ctx:
    def __init__(self, name):
        self.nc = bass.Bass("TRN2", target_bir_lowering=False, debug=False)
        self.e = Emitter(self.nc)
        self.stack = ExitStack()
        self.name = name
        self._uid = 0

    def uid(self):
        self._uid += 1
        return self._uid

    def scope(self):
        from contextlib import contextmanager

        @contextmanager
        def _s(self=self):
            st = ExitStack()
            old = self.stack
            self.stack = st
            try:
                yield
            finally:
                self.stack = old
                st.close()
        return _s()

    def sb(self, shape, dtype, name=None):
        name = name or f"sb{self.uid()}"
        return self.stack.enter_context(self.nc.sbuf_tensor(name, shape, dtype))

    def ps(self, shape, dtype=F32, name=None):
        name = name or f"ps{self.uid()}"
        return self.stack.enter_context(self.nc.psum_tensor(name, shape, dtype))

    def inp(self, name, shape, dtype=F32):
        return self.nc.dram_tensor(name, shape, dtype, kind="ExternalInput").ap()

    def out(self, name, shape, dtype=F32):
        return self.nc.dram_tensor(name, shape, dtype, kind="ExternalOutput").ap()

    def finish(self):
        self.e.wait_tail()
        self.e.finalize()
        self.stack.close()
        return self.nc



def load_const_tiles(c, hbm_ap, rows, cols, dtype=BF16, queue="gpsimd", name=None):
    tiles, evs = [], []
    for i in range(cdiv(rows, 128)):
        r = min(128, rows - i * 128)
        t = c.sb([r, cols], dtype, name and f"{name}{i}")
        ev = c.e.dma(queue, out=t[:, :], in_=hbm_ap[i * 128:i * 128 + r, :])
        tiles.append(t)
        evs.append(ev)
    return tiles, evs


def emit_layernorm_fm(c, x_tiles, cols, out_bf, deps, out_f32=None,
                      gamma=None, eps=1e-5, dmodel=1024, rms=False, saveAB=None,
                      col_off=0, shared_f32=None, consume_fn=None,
                      rms1_gamma=None, rms1_eps=1e-6, ln_gamma=None,
                      ln_beta=None):
    e, nc = c.e, c.nc
    n_tiles = len(x_tiles)
    if not hasattr(c, "_ln_scratch"):
        ones_col = c.sb([128, 1], F32, "ln_ones_col")
        ones_row = c.sb([1, 128], F32, "ln_ones_row")
        ev_oc = e.op("vector", lambda v: v.memset(ones_col[:, :], 1.0))
        ev_or = e.op("vector", lambda v: v.memset(ones_row[:, :], 1.0))
        c._ln_scratch = dict(
            ones_col=ones_col, ones_row=ones_row, ev_oc=ev_oc, ev_or=ev_or,
            sq=[c.sb([128, 512], F32, f"ln_sq{i}") for i in range(2)],
            r_sq=c.sb([1, 512], F32, "ln_rsq"),
            r_mean=c.sb([1, 512], F32, "ln_rmean"),
            r_tmp=c.sb([1, 512], F32, "ln_rtmp"),
            r_A=c.sb([1, 512], F32, "ln_rA"),
            r_B=c.sb([1, 512], F32, "ln_rB"))
    L = c._ln_scratch
    ones_col, ones_row = L["ones_col"], L["ones_row"]
    ev_oc, ev_or = L["ev_oc"], L["ev_or"]
    sq = L["sq"]
    r_sq, r_mean, r_tmp = L["r_sq"], L["r_mean"], L["r_tmp"]
    r_A, r_B = L["r_A"], L["r_B"]
    psstack = ExitStack()
    _oldstack, c.stack = c.stack, psstack
    ps_sum = c.ps([1, 512])
    ps_sq = c.ps([1, 512])
    ps_A = c.ps([128, 512])
    ps_B = c.ps([128, 512])
    c.stack = _oldstack
    done = []
    prev_chunk = []
    for ci, c0 in enumerate(list(range(0, cols, 512))):
        w = min(512, cols - c0)
        if not rms:
            ev_s = None
            for i, xt in enumerate(x_tiles):
                ev_s = e.op("tensor", lambda t, xt=xt, i=i, c0=c0, w=w: t.matmul(
                    ps_sum[0:1, :w], ones_col[:, :], xt[:, c0:c0 + w],
                    start=(i == 0), stop=(i == n_tiles - 1)),
                    deps=list(deps) + [ev_oc] + prev_chunk)
        ev_q = None
        sq_evs = []
        for i, xt in enumerate(x_tiles):
            buf = sq[i % 2]
            ev_w = e.op("scalar", lambda s, xt=xt, buf=buf, c0=c0, w=w: s.activation(
                buf[:, :w], xt[:, c0:c0 + w], AF.Square),
                deps=list(deps) + prev_chunk + sq_evs[-2:-1])
            ev_q = e.op("tensor", lambda t, buf=buf, i=i, w=w: t.matmul(
                ps_sq[0:1, :w], ones_col[:, :], buf[:, :w],
                start=(i == 0), stop=(i == n_tiles - 1)),
                deps=[ev_w, ev_oc] + prev_chunk)
            sq_evs.append(ev_q)
        if rms:
            def rowf(v, w=w):
                return v.tensor_scalar(r_tmp[0:1, :w], ps_sq[0:1, :w], 1.0 / dmodel,
                                       eps, OP.mult, OP.add)
            ev_r1 = e.op("vector", rowf, deps=[ev_q])
        else:
            ev_a1 = e.op("vector", lambda v, w=w: v.tensor_scalar_mul(
                r_mean[0:1, :w], ps_sum[0:1, :w], 1.0 / dmodel), deps=[ev_s, ev_q])
            ev_a2 = e.op("vector", lambda v, w=w: v.tensor_tensor(
                r_tmp[0:1, :w], r_mean[0:1, :w], r_mean[0:1, :w], OP.mult),
                deps=[ev_a1])
            ev_a3 = e.op("vector", lambda v, w=w: v.tensor_scalar(
                r_sq[0:1, :w], ps_sq[0:1, :w], 1.0 / dmodel, eps, OP.mult, OP.add),
                deps=[ev_a2])
            ev_r1 = e.op("vector", lambda v, w=w: v.tensor_tensor(
                r_tmp[0:1, :w], r_sq[0:1, :w], r_tmp[0:1, :w], OP.subtract),
                deps=[ev_a3])
        ev_r2 = e.op("scalar", lambda s, w=w: s.activation(
            r_tmp[0:1, :w], r_tmp[0:1, :w], AF.Sqrt), deps=[ev_r1])

        ev_r3 = e.op("vector", lambda v, w=w: v.reciprocal(
            r_A[0:1, :w], r_tmp[0:1, :w]), deps=[ev_r2])
        if not rms:
            ev_r3 = e.op("vector", lambda v, w=w: v.scalar_tensor_tensor(
                r_B[0:1, :w], r_mean[0:1, :w], -1.0, r_A[0:1, :w],
                OP.mult, OP.mult), deps=[ev_r3])
        if rms1_gamma is not None:
            ev_h0 = e.op("vector", lambda v, w=w: v.tensor_tensor(
                r_tmp[0:1, :w], r_A[0:1, :w], r_A[0:1, :w], OP.mult), deps=[ev_r3])
            ev_h1 = e.op("vector", lambda v, w=w: v.tensor_scalar(
                r_tmp[0:1, :w], r_tmp[0:1, :w], -eps, 1.0 + rms1_eps,
                OP.mult, OP.add), deps=[ev_h0])
            ev_h2 = e.op("scalar", lambda s, w=w: s.activation(
                r_tmp[0:1, :w], r_tmp[0:1, :w], AF.Sqrt), deps=[ev_h1])

            ev_i0 = e.op("vector", lambda v, w=w: v.reciprocal(
                r_sq[0:1, :w], r_tmp[0:1, :w]), deps=[ev_h2])
            ev_i1 = e.op("vector", lambda v, w=w: v.tensor_tensor(
                r_A[0:1, :w], r_A[0:1, :w], r_sq[0:1, :w], OP.mult), deps=[ev_i0])
            ev_r3 = e.op("vector", lambda v, w=w: v.tensor_tensor(
                r_B[0:1, :w], r_B[0:1, :w], r_sq[0:1, :w], OP.mult), deps=[ev_i1])
        ev_bA = e.op("tensor", lambda t, w=w: t.matmul(
            ps_A[:, :w], ones_row[0:1, :], r_A[0:1, :w], start=True, stop=True),
            deps=[ev_r3, ev_or])
        if not rms:
            ev_bB = e.op("tensor", lambda t, w=w: t.matmul(
                ps_B[:, :w], ones_row[0:1, :], r_B[0:1, :w], start=True, stop=True),
                deps=[ev_r3, ev_or])
        if saveAB is not None:
            rowA_s, rowB_s = saveAB
            ev_sv = e.op("vector", lambda v, w=w, o=col_off + c0: v.tensor_copy(
                rowA_s[0:1, o:o + w], r_A[0:1, :w]), deps=[ev_r3])
            if not rms:
                ev_sv2 = e.op("vector", lambda v, w=w, o=col_off + c0: v.tensor_copy(
                    rowB_s[0:1, o:o + w], r_B[0:1, :w]), deps=[ev_r3])
        chunk_evs = []
        for i, xt in enumerate(x_tiles):
            buf = sq[i % 2]
            if rms:
                if gamma is not None:
                    ev1 = e.op("vector", lambda v, xt=xt, buf=buf, c0=c0, w=w: v.tensor_tensor(
                        buf[:, :w], xt[:, c0:c0 + w], ps_A[:, :w], OP.mult),
                        deps=[ev_bA] + sq_evs)
                    ev2 = e.op("vector", lambda v, i=i, buf=buf, o=col_off + c0, w=w: v.tensor_scalar_mul(
                        out_bf[i][:, o:o + w], buf[:, :w], gamma[i][:, 0:1]),
                        deps=[ev1])
                else:
                    ev2 = e.op("vector", lambda v, xt=xt, i=i, c0=c0, o=col_off + c0, w=w: v.tensor_tensor(
                        out_bf[i][:, o:o + w], xt[:, c0:c0 + w], ps_A[:, :w], OP.mult),
                        deps=[ev_bA] + sq_evs)
                chunk_evs.append(ev2)
            else:
                ev1 = e.op("vector", lambda v, xt=xt, buf=buf, c0=c0, w=w: v.tensor_tensor(
                    buf[:, :w], xt[:, c0:c0 + w], ps_A[:, :w], OP.mult),
                    deps=[ev_bA] + sq_evs)
                if shared_f32 is not None:
                    evf = e.op("vector", lambda v, buf=buf, w=w: v.tensor_tensor(
                        shared_f32[:, :w], buf[:, :w], ps_B[:, :w], OP.add),
                        deps=[ev_bB, ev1] + getattr(c, "_lnsh_war", []))
                    c._lnsh_war = list(consume_fn(i, evf))
                    ev2 = evf
                elif out_f32 is not None:
                    evf = e.op("vector", lambda v, i=i, buf=buf, o=col_off + c0, w=w: v.tensor_tensor(
                        out_f32[i][:, o:o + w], buf[:, :w], ps_B[:, :w], OP.add),
                        deps=[ev_bB, ev1])
                    if out_bf is None:
                        ev2 = evf
                    else:
                        ev2 = e.op("vector", lambda v, i=i, o=col_off + c0, w=w: v.tensor_copy(
                            out_bf[i][:, o:o + w], out_f32[i][:, o:o + w]),
                            deps=[evf])
                elif rms1_gamma is not None:
                    ev15 = e.op("vector", lambda v, buf=buf, w=w: v.tensor_tensor(
                        buf[:, :w], buf[:, :w], ps_B[:, :w], OP.add),
                        deps=[ev_bB, ev1])
                    ev2 = e.op("vector", lambda v, i=i, buf=buf, o=col_off + c0, w=w: v.tensor_scalar_mul(
                        out_bf[i][:, o:o + w], buf[:, :w], rms1_gamma[i][:, 0:1]),
                        deps=[ev15])
                elif ln_gamma is not None:
                    ev15 = e.op("vector", lambda v, buf=buf, w=w: v.tensor_tensor(
                        buf[:, :w], buf[:, :w], ps_B[:, :w], OP.add),
                        deps=[ev_bB, ev1])
                    ev2 = e.op("vector", lambda v, i=i, buf=buf, o=col_off + c0, w=w: v.scalar_tensor_tensor(
                        out_bf[i][:, o:o + w], buf[:, :w], ln_gamma[i][:, 0:1],
                        ln_beta[i][:, 0:1].broadcast_to([128, w]),
                        OP.mult, OP.add), deps=[ev15])
                else:
                    ev2 = e.op("vector", lambda v, i=i, buf=buf, o=col_off + c0, w=w: v.tensor_tensor(
                        out_bf[i][:, o:o + w], buf[:, :w], ps_B[:, :w], OP.add),
                        deps=[ev_bB, ev1])
                chunk_evs.append(ev2)
        prev_chunk = chunk_evs
        done.extend(chunk_evs)
    psstack.close()
    return done


def emit_proj_fm(c, w_tiles, w_evs, xin_bf, cols, out_psum_fn, m_tiles, deps,
                 n_k=8, nw=512):
    e = c.e
    out = {}
    for m in m_tiles:
        for ci, c0 in enumerate(range(0, cols, nw)):
            w_ = min(nw, cols - c0)
            pb = out_psum_fn(m, ci)
            ev = None
            for k in range(n_k):
                ev = e.op("tensor", lambda t, k=k, m=m, c0=c0, w_=w_, pb=pb: t.matmul(
                    pb[:, :w_], w_tiles(k, m), xin_bf[k][:, c0:c0 + w_],
                    start=(k == 0), stop=(k == n_k - 1)),
                    deps=list(deps) + list(w_evs))
            out[(m, ci)] = ev
    return out



def emit_stage2_body(c, x, xT_hbm, yT_hbm, wr_sb, wk_sb, wv_sb, wev,
                     mur_sb, muk_sb, x_deps, cols=HALF):
    e = c.e
    EC = 1 + cols
    xln = [c.sb([128, EC], BF16, f"s2xln{i}") for i in range(NT)]
    ln_done = emit_layernorm_fm(c, x, EC, xln, deps=x_deps)

    nchunk = cols // 512
    xr = [c.sb([128, 512], BF16) for _ in range(NT)]
    xk = [c.sb([128, 512], BF16) for _ in range(NT)]
    rb = [c.sb([128, 512], BF16) for _ in range(NT)]
    ka = [c.sb([128, 512], BF16) for _ in range(NT)]
    dsc = [c.sb([128, 512], BF16) for _ in range(2)]
    ksc = [c.sb([128, 512], F32) for _ in range(2)]
    tmp = [c.sb([128, 512], F32) for _ in range(2)]
    st_all = []
    with c.scope():
        pb = [c.ps([128, 512]) for _ in range(4)]
        pb_war = {i: [] for i in range(4)}
        buf_war = {0: []}
        for ci in range(nchunk):
            c0 = ci * 512
            p = 0
            xrp, xkp, rbp, kap = xr, xk, rb, ka
            mix_done = []
            for i in range(NT):
                cur = xln[i][:, 1 + c0:1 + c0 + 512]
                prv = xln[i][:, c0:c0 + 512]
                dv = dsc[p]
                ev_d = e.op("vector", lambda v, cur=cur, prv=prv, dv=dv: v.tensor_tensor(
                    dv[:, :], cur, prv, OP.subtract),
                    deps=ln_done + buf_war[p])
                ev_r = e.op("vector", lambda v, i=i, prv=prv, dv=dv: v.scalar_tensor_tensor(
                    xrp[i][:, :], dv[:, :], mur_sb[i][:, 0:1], prv, OP.mult, OP.add),
                    deps=[ev_d])
                ev_k = e.op("vector", lambda v, i=i, prv=prv, dv=dv: v.scalar_tensor_tensor(
                    xkp[i][:, :], dv[:, :], muk_sb[i][:, 0:1], prv, OP.mult, OP.add),
                    deps=[ev_d])
                mix_done += [ev_r, ev_k]
            rk_done = []
            for m in range(NT):
                bank = m % 2
                pbank = pb[bank]
                ev = None
                for k in range(NT):
                    ev = e.op("tensor", lambda t, k=k, m=m, pbank=pbank: t.matmul(
                        pbank[:, :], wr_sb[k][:, m * 128:(m + 1) * 128], xrp[k][:, :],
                        start=(k == 0), stop=(k == NT - 1)),
                        deps=mix_done + wev + pb_war[bank])
                ev_e = e.op("scalar", lambda s, m=m, pbank=pbank: s.activation(
                    rbp[m][:, :], pbank[:, :], AF.Sigmoid), deps=[ev])
                pb_war[bank] = [ev_e]
                bank2 = 2 + m % 2
                pbank2 = pb[bank2]
                ev = None
                for k in range(NT):
                    ev = e.op("tensor", lambda t, k=k, m=m, pbank2=pbank2: t.matmul(
                        pbank2[:, :], wk_sb[k][:, m * 128:(m + 1) * 128], xkp[k][:, :],
                        start=(k == 0), stop=(k == NT - 1)),
                        deps=mix_done + wev + pb_war[bank2])
                kscb = ksc[m % 2]
                ev_r1 = e.op("vector", lambda v, pbank2=pbank2, kscb=kscb: v.tensor_scalar_max(
                    kscb[:, :], pbank2[:, :], 0.0),
                    deps=[ev] + ([rk_done[-2]] if len(rk_done) >= 2 else []))
                pb_war[bank2] = [ev_r1]
                ev_sq = e.op("scalar", lambda s, m=m, kscb=kscb: s.activation(
                    kap[m][:, :], kscb[:, :], AF.Square), deps=[ev_r1])
                rk_done.append(ev_sq)
            ch_done = []
            for m in range(NT):
                bank = m % 2
                pbank = pb[bank]
                ev = None
                for k in range(NT):
                    ev = e.op("tensor", lambda t, k=k, m=m, pbank=pbank: t.matmul(
                        pbank[:, :], wv_sb[k][:, m * 128:(m + 1) * 128], kap[k][:, :],
                        start=(k == 0), stop=(k == NT - 1)),
                        deps=rk_done + pb_war[bank])
                tb = tmp[m % 2]
                ev_m = e.op("vector", lambda v, m=m, pbank=pbank, tb=tb: v.tensor_tensor(
                    tb[:, :], pbank[:, :], rbp[m][:, :], OP.mult),
                    deps=[ev] + ([ch_done[-2]] if len(ch_done) >= 2 else []))
                pb_war[bank] = [ev_m]
                ev_a = e.op("vector", lambda v, m=m, c0=c0, tb=tb: v.tensor_tensor(
                    tb[:, :], tb[:, :], x[m][:, 1 + c0:1 + c0 + 512], OP.add),
                    deps=[ev_m])
                ev_st = e.dma("sync", out=yT_hbm[m * 128:(m + 1) * 128, c0:c0 + 512],
                              in_=tb[:, :], deps=[ev_a])
                ch_done.append(ev_st)
            buf_war[p] = ch_done
            st_all += ch_done
    return st_all


def build_stage2():
    c = Ctx("stage2")
    e = c.e
    xT = c.inp("xT", [D, HALF])
    prevT = c.inp("prevT", [D, 1])
    wrT = c.inp("wrT", [D, D])
    wkT = c.inp("wkT", [D, D])
    wvT = c.inp("wvT", [D, D])
    mur = c.inp("mur", [D, 1])
    muk = c.inp("muk", [D, 1])
    yT = c.out("yT", [D, HALF])

    x = [c.sb([128, 1 + HALF], F32, f"x{i}") for i in range(NT)]
    mur_sb = [c.sb([128, 1], F32) for _ in range(NT)]
    muk_sb = [c.sb([128, 1], F32) for _ in range(NT)]
    wr_sb = [c.sb([128, D], BF16, f"wr{i}") for i in range(NT)]
    wk_sb = [c.sb([128, D], BF16, f"wk{i}") for i in range(NT)]
    wv_sb = [c.sb([128, D], BF16, f"wv{i}") for i in range(NT)]

    ld = []
    for i in range(NT):
        ld.append(e.dma("sync", out=x[i][:, 1:1 + HALF], in_=xT[i * 128:(i + 1) * 128, :]))
        ld.append(e.dma("sync", out=x[i][:, 0:1], in_=prevT[i * 128:(i + 1) * 128, :]))
        ld.append(e.dma("sync", out=mur_sb[i][:, :], in_=mur[i * 128:(i + 1) * 128, :]))
        ld.append(e.dma("sync", out=muk_sb[i][:, :], in_=muk[i * 128:(i + 1) * 128, :]))
    wev = []
    for i in range(NT):
        wev.append(e.dma("gpsimd", out=wr_sb[i][:, :], in_=wrT[i * 128:(i + 1) * 128, :]))
        wev.append(e.dma("gpsimd", out=wk_sb[i][:, :], in_=wkT[i * 128:(i + 1) * 128, :]))
        wev.append(e.dma("gpsimd", out=wv_sb[i][:, :], in_=wvT[i * 128:(i + 1) * 128, :]))

    emit_stage2_body(c, x, xT, yT, wr_sb, wk_sb, wv_sb, wev, mur_sb, muk_sb, ld)
    return c.finish()



def emit_full_attention(c, xln, NQ, NK, wq_hbm, wk_hbm, wv_hbm, a2_bf,
                        xln_deps, bound, apply_gelu, qk_scale=0.125,
                        wq_col=None, wk_col=None, wv_col=None):
    e = c.e
    wq_col = wq_col or (lambda hp: hp * 128)
    wk_col = wk_col or (lambda hp: hp * 128)
    wv_col = wv_col or (lambda hp: hp * 128)
    k2 = c.sb([128, NK], BF16, "fa_k2")
    q2 = c.sb([128, NQ], BF16, "fa_q2")
    v2e = c.sb([128, (NK // 128) * 130], BF16, "fa_v2e")
    wqp = [[c.sb([128, 128], BF16) for _ in range(NT)] for _ in range(2)]
    wkp = [[c.sb([128, 128], BF16) for _ in range(NT)] for _ in range(2)]
    wvp = [[c.sb([128, 128], BF16) for _ in range(NT)] for _ in range(2)]
    pstr = [c.sb([128, 512], BF16) for _ in range(4)]
    ysc = c.sb([64, 512], F32, "fa_ysc")
    ysc2 = c.sb([64, 512], F32, "fa_ysc2")
    rrow = c.sb([1, 512], F32, "fa_rrow")
    ones64 = c.sb([1, 64], F32, "fa_ones64")
    ev_o = e.op("vector", lambda v: v.memset(ones64[:, :], 1.0))
    nbias = c.sb([128, 1], F32, "fa_nbias")
    ev_nb = e.op("vector", lambda v: v.memset(nbias[:, :], -float(bound)))
    nkt = NK // 128
    v2e_3d = v2e[:, :].rearrange("p (t c) -> p t c", c=130)
    ones_evs = [
        e.op("vector", lambda v: v.memset(v2e_3d[:, :, 64:65], 1.0)),
        e.op("vector", lambda v: v.memset(v2e_3d[:, :, 129:130], 1.0)),
    ]
    out_evs = []
    with c.scope():
        ps_s = [c.ps([128, 512]) for _ in range(4)]
        ps_y = c.ps([128, 512], name="fa_psy")
        ps_b = c.ps([128, 512], name="fa_psb")
        war = {"k2": [], "q2": [], "v2e": ones_evs, "ps0": [], "ps1": [],
               "ps2": [], "ps3": [], "psy": [], "psb": [], "p0": [], "p1": [],
               "p2": [], "p3": [], "ysc": [], "ysc2": [],
               "rrow": [], "w0": [], "w1": []}
        for hp in range(8):
            wp = hp % 2
            wev = []
            for k in range(NT):
                wev.append(e.dma("gpsimd", out=wqp[wp][k][:, :],
                                 in_=wq_hbm[k * 128:(k + 1) * 128,
                                            wq_col(hp):wq_col(hp) + 128],
                                 deps=war[f"w{wp}"]))
                wev.append(e.dma("gpsimd", out=wkp[wp][k][:, :],
                                 in_=wk_hbm[k * 128:(k + 1) * 128,
                                            wk_col(hp):wk_col(hp) + 128],
                                 deps=war[f"w{wp}"]))
                wev.append(e.dma("gpsimd", out=wvp[wp][k][:, :],
                                 in_=wv_hbm[k * 128:(k + 1) * 128,
                                            wv_col(hp):wv_col(hp) + 128],
                                 deps=war[f"w{wp}"]))
            kq_evs = []
            kqv_mm = []
            for ci in range(NK // 512):
                c0 = ci * 512
                bank = ci % 2
                ev = None
                for k in range(NT):
                    ev = e.op("tensor", lambda t, k=k, c0=c0, wp=wp, bank=bank: t.matmul(
                        ps_s[bank][:, :], wkp[wp][k][:, :], xln[k][:, c0:c0 + 512],
                        start=(k == 0), stop=(k == NT - 1)),
                        deps=list(xln_deps) + wev + war[f"ps{bank}"])
                kqv_mm.append(ev)
                ev_e = e.op("vector", lambda v, c0=c0, bank=bank: v.tensor_copy(
                    k2[:, c0:c0 + 512], ps_s[bank][:, :]),
                    deps=[ev] + war["k2"])
                war[f"ps{bank}"] = [ev_e]
                kq_evs.append(ev_e)
            for ci in range(NQ // 512):
                c0 = ci * 512
                bank = ci % 2
                ev = None
                for k in range(NT):
                    ev = e.op("tensor", lambda t, k=k, c0=c0, wp=wp, bank=bank: t.matmul(
                        ps_s[bank][:, :], wqp[wp][k][:, :], xln[k][:, c0:c0 + 512],
                        start=(k == 0), stop=(k == NT - 1)),
                        deps=list(xln_deps) + wev + war[f"ps{bank}"])
                kqv_mm.append(ev)
                ev_e = e.op("vector", lambda v, c0=c0, bank=bank: v.tensor_copy(
                    q2[:, c0:c0 + 512], ps_s[bank][:, :]),
                    deps=[ev] + war["q2"])
                war[f"ps{bank}"] = [ev_e]
                kq_evs.append(ev_e)
            v_evs = []
            for t0 in range(nkt):
                bank = t0 % 2
                ev = None
                for k in range(NT):
                    ev = e.op("tensor", lambda t, k=k, t0=t0, wp=wp, bank=bank: t.matmul(
                        ps_s[bank][:, 0:128], xln[k][:, t0 * 128:(t0 + 1) * 128],
                        wvp[wp][k][:, :],
                        start=(k == 0), stop=(k == NT - 1)),
                        deps=list(xln_deps) + wev + war[f"ps{bank}"])
                kqv_mm.append(ev)

                b0 = t0 * 130
                ev_e1 = e.op("vector", lambda v, b0=b0, bank=bank: v.tensor_copy(
                    v2e[:, b0:b0 + 64], ps_s[bank][:, 0:64]),
                    deps=[ev] + war["v2e"])
                ev_e = e.op("vector", lambda v, b0=b0, bank=bank: v.tensor_copy(
                    v2e[:, b0 + 65:b0 + 129], ps_s[bank][:, 64:128]),
                    deps=[ev_e1])
                war[f"ps{bank}"] = [ev_e]
                v_evs.append(ev_e)
            war[f"w{wp}"] = kqv_mm
            head_evs = []
            for hh in range(2):
                hb = hh * 64
                for qc in range(NQ // 512):
                    q0 = qc * 512
                    ev_pv = None
                    for ks in range(nkt):
                        bank = ks % 4
                        ev_s = e.op("tensor", lambda t, ks=ks, q0=q0, hb=hb, bank=bank: t.matmul(
                            ps_s[bank][:, :], k2[hb:hb + 64, ks * 128:(ks + 1) * 128],
                            q2[hb:hb + 64, q0:q0 + 512],
                            start=True, stop=True),
                            deps=kq_evs + war[f"ps{bank}"] + war[f"p{bank}"])
                        ev_p = e.op("scalar", lambda s, bank=bank: s.activation(
                            pstr[bank][:, :], ps_s[bank][:, :], AF.Exp,
                            bias=nbias[:, 0:1], scale=qk_scale),
                            deps=[ev_s, ev_nb])
                        war[f"ps{bank}"] = [ev_p]
                        ev_pv = e.op("tensor", lambda t, ks=ks, hh=hh, bank=bank: t.matmul(
                            ps_y[0:65, :], v2e[:, ks * 130 + hh * 65:ks * 130 + hh * 65 + 65],
                            pstr[bank][:, :],
                            start=(ks == 0), stop=(ks == nkt - 1)),
                            deps=[ev_p] + v_evs + war["psy"])
                        war[f"p{bank}"] = [ev_pv]
                    ev_r = e.op("vector", lambda v: v.reciprocal(
                        rrow[0:1, :], ps_y[64:65, :]), deps=[ev_pv] + war["rrow"])
                    ev_b = e.op("tensor", lambda t: t.matmul(
                        ps_b[0:64, :], ones64[0:1, :], rrow[0:1, :],
                        start=True, stop=True), deps=[ev_r, ev_o] + war["psb"])
                    ev_ye = e.op("scalar", lambda s: s.copy(
                        ysc[:, :], ps_y[0:64, :]), deps=[ev_pv] + war["ysc"])
                    war["psy"] = [ev_ye]
                    ev_n = e.op("vector", lambda v: v.tensor_tensor(
                        ysc2[:, :], ysc[:, :], ps_b[0:64, :], OP.mult),
                        deps=[ev_ye, ev_b] + war["ysc2"])
                    war["rrow"] = [ev_b]
                    war["psb"] = [ev_n]
                    war["ysc"] = [ev_n]
                    tile, rb = hp, hb
                    if apply_gelu:
                        ev_g = e.op("scalar", lambda s, tile=tile, rb=rb, q0=q0: s.activation(
                            a2_bf[tile][rb:rb + 64, q0:q0 + 512], ysc2[:, :], AF.Gelu),
                            deps=[ev_n])
                    else:
                        ev_g = e.op("vector", lambda v, tile=tile, rb=rb, q0=q0: v.tensor_copy(
                            a2_bf[tile][rb:rb + 64, q0:q0 + 512], ysc2[:, :]),
                            deps=[ev_n])
                    war["ysc2"] = [ev_g]
                    head_evs.append(ev_g)
            war["k2"] = head_evs
            war["q2"] = head_evs
            war["v2e"] = head_evs
            out_evs.extend(head_evs)
    return out_evs


def build_stage1():
    c = Ctx("stage1")
    e = c.e
    xT = c.inp("xT", [D, T])
    wq = c.inp("wq", [D, D])
    wk = c.inp("wk", [D, D])
    wv = c.inp("wv", [D, D])
    wpost = c.inp("wpost", [D, D])
    yT = c.out("yT", [D, HALF])

    xln = [c.sb([128, T], BF16, f"xln{i}") for i in range(NT)]
    a2 = [c.sb([128, HALF], BF16, f"a2_{i}") for i in range(NT)]
    xc = [c.sb([128, 512], F32) for _ in range(NT)]

    ln_evs = []
    prev = []
    for ci in range(T // 512):
        c0 = ci * 512
        ld = [e.dma("sync", out=xc[i][:, :], in_=xT[i * 128:(i + 1) * 128, c0:c0 + 512],
                    deps=prev) for i in range(NT)]
        evs = emit_layernorm_fm(c, xc, 512, [t for t in xln], deps=ld, col_off=c0)
        prev = evs
        ln_evs.extend(evs)

    at_evs = emit_full_attention(c, xln, HALF, T, wq, wk, wv,
                                 a2, ln_evs, bound=16.0, apply_gelu=True)

    wp_sb = [c.sb([128, D], BF16, f"wp{i}") for i in range(NT)]
    wpev = [e.dma("gpsimd", out=wp_sb[i][:, :], in_=wpost[i * 128:(i + 1) * 128, :])
            for i in range(NT)]
    lnf = c.sb([128, 512], F32, "lnf")
    prev3 = []
    with c.scope():
        pb = [c.ps([128, 512]) for _ in range(2)]
        pb_war = {0: at_evs, 1: at_evs}
        tmp = [c.sb([128, 512], F32) for _ in range(2)]
        tmp_war = {0: [], 1: []}
        st3 = {"prev": []}

        for ci in range(HALF // 512):
            c0 = ci * 512
            ld3 = [e.dma("sync", out=xc[i][:, :],
                         in_=xT[i * 128:(i + 1) * 128, c0:c0 + 512],
                         deps=ln_evs + prev3) for i in range(NT)]
            ch3 = []

            def consume(m, evf, c0=c0, ch3=ch3, ld3=ld3):
                bank = m % 2
                ev = None
                for k in range(NT):
                    ev = e.op("tensor", lambda t, k=k, m=m, c0=c0, bank=bank: t.matmul(
                        pb[bank][:, :], wp_sb[k][:, m * 128:(m + 1) * 128],
                        a2[k][:, c0:c0 + 512],
                        start=(k == 0), stop=(k == NT - 1)),
                        deps=at_evs + wpev + pb_war[bank])
                tb = tmp[bank]
                ev_1 = e.op("vector", lambda v, m=m, tb=tb: v.tensor_tensor(
                    tb[:, :], xc[m][:, :], lnf[:, :], OP.add),
                    deps=[evf] + tmp_war[bank])
                ev_2 = e.op("vector", lambda v, tb=tb, bank=bank: v.tensor_tensor(
                    tb[:, :], tb[:, :], pb[bank][:, :], OP.add), deps=[ev_1, ev])
                ev_st = e.dma("sync", out=yT[m * 128:(m + 1) * 128, c0:c0 + 512],
                              in_=tb[:, :], deps=[ev_2])
                pb_war[bank] = [ev_2]
                tmp_war[bank] = [ev_st]
                ch3.append(ev_2)
                ch3.append(ev_st)
                return [ev_1]

            emit_layernorm_fm(c, xc, 512, None, deps=ld3 + at_evs,
                              shared_f32=lnf, consume_fn=consume)
            prev3 = ch3
    return c.finish()


def build_stage5():
    c = Ctx("stage5")
    e = c.e
    xT = c.inp("xT", [D, T])
    wqs = c.inp("wqs", [D, D])
    wks = c.inp("wks", [D, D])
    wvs = c.inp("wvs", [D, D])
    yT = c.out("yT", [D, HALF])

    xln = [c.sb([128, T], BF16, f"xln{i}") for i in range(NT)]
    a2 = [c.sb([128, HALF], F32, f"a2_{i}") for i in range(NT)]
    xc = [c.sb([128, 512], F32) for _ in range(NT)]

    ln_evs = []
    prev = []
    for ci in range(T // 512):
        c0 = ci * 512
        ld = [e.dma("sync", out=xc[i][:, :], in_=xT[i * 128:(i + 1) * 128, c0:c0 + 512],
                    deps=prev) for i in range(NT)]
        evs = emit_layernorm_fm(c, xc, 512, xln, deps=ld, col_off=c0)
        prev = evs
        ln_evs.extend(evs)

    at_evs = emit_full_attention(c, xln, HALF, T, wqs, wks, wvs,
                                 a2, ln_evs, bound=24.0, apply_gelu=False)

    prev3 = []
    tmp = [c.sb([128, 512], F32) for _ in range(2)]
    tmp_war = {0: [], 1: []}
    for ci in range(HALF // 512):
        c0 = ci * 512
        ch3 = []
        for m in range(NT):
            bank = m % 2
            xb = xc[m]
            ev_x = e.dma("sync", out=xb[:, :],
                         in_=xT[m * 128:(m + 1) * 128, c0:c0 + 512],
                         deps=ln_evs + prev3)
            tb = tmp[bank]
            ev_1 = e.op("vector", lambda v, m=m, c0=c0, xb=xb, tb=tb: v.tensor_tensor(
                tb[:, :], xb[:, :], a2[m][:, c0:c0 + 512], OP.add),
                deps=[ev_x] + at_evs + tmp_war[bank])
            ev_st = e.dma("sync", out=yT[m * 128:(m + 1) * 128, c0:c0 + 512],
                          in_=tb[:, :], deps=[ev_1])
            tmp_war[bank] = [ev_st]
            ch3 += [ev_1, ev_st]
        prev3 = ch3
    return c.finish()


def run_stage5(x_full, W):
    nc = _get_prog("stage5", build_stage5)
    Wr = np.ascontiguousarray(W.reshape(D, 16, 192))
    wqs = np.ascontiguousarray(Wr[:, :, 0:64].reshape(D, 1024))
    wks = np.ascontiguousarray(Wr[:, :, 64:128].reshape(D, 1024))
    wvs = np.ascontiguousarray(Wr[:, :, 128:192].reshape(D, 1024))
    in_maps = []
    for core in range(N_CORES):
        b, h = core // 2, core % 2
        own = x_full[b, h * HALF:(h + 1) * HALF, :]
        other = x_full[b, (1 - h) * HALF:(2 - h) * HALF, :]
        xT = np.ascontiguousarray(np.concatenate([own, other], axis=0).T)
        in_maps.append(dict(xT=xT, wqs=wqs, wks=wks, wvs=wvs))
    res = run_bass_kernel_spmd(nc, in_maps, core_ids=list(range(N_CORES)))
    out = np.empty_like(x_full)
    for core in range(N_CORES):
        b, h = core // 2, core % 2
        out[b, h * HALF:(h + 1) * HALF, :] = res.results[core]["yT"].T
    return out


def build_stage7():
    c = Ctx("stage7")
    e = c.e
    Q0 = HALF
    xT = c.inp("xT", [D, T])
    wq = c.inp("wq", [D, 256])
    wk = c.inp("wk", [D, 256])
    wv = c.inp("wv", [D, D])
    wo = c.inp("wo", [D, D])
    pmcol = c.inp("pmcol", [128, 1])
    dmask = c.inp("dmask", [512, 512])
    hmask = c.inp("hmask", [512, 512])
    yT = c.out("yT", [D, HALF])

    SQI = 1.0 / np.sqrt(32.0)
    SQB = float(np.sqrt(2.0) / 2.0)

    xln = [c.sb([128, T], BF16, f"xln{i}") for i in range(NT)]
    a2 = [c.sb([128, HALF], BF16, f"a2_{i}") for i in range(NT)]
    xc = [c.sb([128, 512], F32) for _ in range(NT)]
    wq_sb = [c.sb([128, 256], BF16) for _ in range(NT)]
    wk_sb = [c.sb([128, 256], BF16) for _ in range(NT)]
    wv_sb = [c.sb([128, D], BF16, f"wv{i}") for i in range(NT)]
    qa = [c.sb([16, HALF], BF16) for _ in range(2)]
    ka = [c.sb([16, T], BF16) for _ in range(2)]
    v2e = c.sb([128, (T // 128) * 130], BF16, "t7_v2e")
    pstr = [c.sb([128, 512], BF16) for _ in range(2)]
    half_c = c.sb([128, 512], BF16, "half_c")
    dm = [c.sb([128, 512], BF16) for _ in range(4)]
    hm = [c.sb([128, 512], BF16) for _ in range(4)]
    pm_sb = c.sb([128, 1], F32)
    ysc = c.sb([64, 512], F32)
    ysc2 = c.sb([64, 512], F32)
    rrow = c.sb([1, 512], F32)
    zrow = c.sb([1, 512], F32)
    ones64 = c.sb([1, 64], F32)
    sqb_col = c.sb([128, 1], F32)
    ev_con = [
        e.op("vector", lambda v: v.memset(half_c[:, :], 0.5)),
        e.op("vector", lambda v: v.memset(ones64[:, :], 1.0)),
        e.op("vector", lambda v: v.memset(sqb_col[:, :], SQB)),
    ]
    ld_c = [e.dma("sync", out=pm_sb[:, :], in_=pmcol[:, :])]
    for j in range(4):
        ld_c.append(e.dma("gpsimd", out=dm[j][:, :], in_=dmask[j * 128:(j + 1) * 128, :]))
        ld_c.append(e.dma("gpsimd", out=hm[j][:, :], in_=hmask[j * 128:(j + 1) * 128, :]))
    wev = []
    for i in range(NT):
        wev.append(e.dma("gpsimd", out=wq_sb[i][:, :], in_=wq[i * 128:(i + 1) * 128, :]))
        wev.append(e.dma("gpsimd", out=wk_sb[i][:, :], in_=wk[i * 128:(i + 1) * 128, :]))
        wev.append(e.dma("gpsimd", out=wv_sb[i][:, :], in_=wv[i * 128:(i + 1) * 128, :]))

    ln_evs = []
    prev = []
    for ci in range(T // 512):
        c0 = ci * 512
        ld = [e.dma("sync", out=xc[i][:, :], in_=xT[i * 128:(i + 1) * 128, c0:c0 + 512],
                    deps=prev) for i in range(NT)]
        evs = emit_layernorm_fm(c, xc, 512, xln, deps=ld, col_off=c0)
        prev = evs
        ln_evs.extend(evs)

    NKT = T // 128
    with c.scope():
        ps_s = [c.ps([128, 512]) for _ in range(2)]
        ps_y = c.ps([128, 512], name="t7_psy")
        ps_b = c.ps([128, 512], name="t7_psb")
        war = {"ps0": [], "ps1": [], "psy": [], "psb": [], "p0": [], "p1": [],
               "qk": [], "v2e": [], "ysc": [], "ysc2": [], "rrow": []}
        out_evs = []
        for hp in range(8):
            kq_evs = []
            for hh in range(2):
                h = hp * 2 + hh
                for ci in range(HALF // 512):
                    c0 = ci * 512
                    bank = ci % 2
                    ev = None
                    for k in range(NT):
                        ev = e.op("tensor", lambda t, k=k, h=h, c0=c0, bank=bank: t.matmul(
                            ps_s[bank][0:16, :], wq_sb[k][:, h * 16:h * 16 + 16],
                            xln[k][:, Q0 + c0:Q0 + c0 + 512],
                            start=(k == 0), stop=(k == NT - 1)),
                            deps=ln_evs + wev + war[f"ps{bank}"])
                    ev_e = e.op("vector", lambda v, hh=hh, c0=c0, bank=bank: v.tensor_copy(
                        qa[hh][:, c0:c0 + 512], ps_s[bank][0:16, :]),
                        deps=[ev] + war["qk"])
                    war[f"ps{bank}"] = [ev_e]
                    kq_evs.append(ev_e)
                for ci in range(T // 512):
                    c0 = ci * 512
                    bank = ci % 2
                    ev = None
                    for k in range(NT):
                        ev = e.op("tensor", lambda t, k=k, h=h, c0=c0, bank=bank: t.matmul(
                            ps_s[bank][0:16, :], wk_sb[k][:, h * 16:h * 16 + 16],
                            xln[k][:, c0:c0 + 512],
                            start=(k == 0), stop=(k == NT - 1)),
                            deps=ln_evs + wev + war[f"ps{bank}"])
                    ev_e = e.op("vector", lambda v, hh=hh, c0=c0, bank=bank: v.tensor_copy(
                        ka[hh][:, c0:c0 + 512], ps_s[bank][0:16, :]),
                        deps=[ev] + war["qk"])
                    war[f"ps{bank}"] = [ev_e]
                    kq_evs.append(ev_e)
            v_evs = []
            for t0 in range(NKT):
                bank = t0 % 2
                ev = None
                for k in range(NT):
                    ev = e.op("tensor", lambda t, k=k, t0=t0, hp=hp, bank=bank: t.matmul(
                        ps_s[bank][:, 0:128], xln[k][:, t0 * 128:(t0 + 1) * 128],
                        wv_sb[k][:, hp * 128:(hp + 1) * 128],
                        start=(k == 0), stop=(k == NT - 1)),
                        deps=ln_evs + wev + war[f"ps{bank}"])

                b0 = t0 * 130
                ev_e1 = e.op("vector", lambda v, b0=b0, bank=bank: v.tensor_copy(
                    v2e[:, b0:b0 + 64], ps_s[bank][:, 0:64]),
                    deps=[ev] + war["v2e"])
                ev_e = e.op("vector", lambda v, b0=b0, bank=bank: v.tensor_copy(
                    v2e[:, b0 + 65:b0 + 129], ps_s[bank][:, 64:128]),
                    deps=[ev_e1])
                if t0 < NKT // 2:
                    ev_o1 = e.op("vector", lambda v, b0=b0: v.tensor_copy(
                        v2e[:, b0 + 64:b0 + 65], pm_sb[:, 0:1]),
                        deps=[ev_e] + ld_c)
                    ev_o = e.op("vector", lambda v, b0=b0: v.tensor_copy(
                        v2e[:, b0 + 129:b0 + 130], pm_sb[:, 0:1]), deps=[ev_o1])
                else:
                    ev_o1 = e.op("vector", lambda v, b0=b0: v.memset(
                        v2e[:, b0 + 64:b0 + 65], 1.0), deps=[ev_e] + ld_c)
                    ev_o = e.op("vector", lambda v, b0=b0: v.memset(
                        v2e[:, b0 + 129:b0 + 130], 1.0), deps=[ev_o1])
                war[f"ps{bank}"] = [ev_e]
                v_evs.append(ev_o)
                v_evs.append(ev_e)
            head_evs = []
            for hh in range(2):
                qh, kh = qa[hh], ka[hh]
                for qc in range(HALF // 512):
                    q0 = qc * 512
                    nfull = (HALF // 128) + qc * 4
                    strips = list(range(nfull + 4))
                    ev_pv = None
                    for ks in strips:
                        bank = ks % 2
                        diag = ks >= nfull
                        ev_s = e.op("tensor", lambda t, ks=ks, q0=q0, qh=qh, kh=kh, bank=bank: t.matmul(
                            ps_s[bank][:, :], kh[:, ks * 128:(ks + 1) * 128],
                            qh[:, q0:q0 + 512],
                            start=True, stop=True),
                            deps=kq_evs + war[f"ps{bank}"] + war[f"p{bank}"])
                        ev_p = e.op("scalar", lambda s, bank=bank: s.activation(
                            pstr[bank][:, :], ps_s[bank][:, :], AF.Square,
                            bias=sqb_col[:, 0:1], scale=SQI),
                            deps=[ev_s] + ev_con)
                        war[f"ps{bank}"] = [ev_p]
                        if diag:
                            j = ks - nfull
                            ev_p = e.op("vector", lambda v, j=j, bank=bank: v.tensor_tensor(
                                pstr[bank][:, :], pstr[bank][:, :], dm[j][:, :],
                                OP.mult), deps=[ev_p] + ld_c)
                            war[f"ps{bank}"][0] = ev_p
                        vsl = v2e[:, ks * 130 + hh * 65:ks * 130 + hh * 65 + 65]
                        ev_pv = e.op("tensor", lambda t, vsl=vsl, bank=bank, first=(ks == 0): t.matmul(
                            ps_y[0:65, :], vsl, pstr[bank][:, :],
                            start=first, stop=False),
                            deps=[ev_p] + v_evs + (war["psy"] if ks == 0 else []))
                        war[f"p{bank}"] = [ev_pv]
                        rhs_h = hm[ks - nfull] if diag else half_c
                        ev_pv = e.op("tensor", lambda t, vsl=vsl, rhs_h=rhs_h, last=(ks == strips[-1]): t.matmul(
                            ps_y[0:65, :], vsl, rhs_h[:, :],
                            start=False, stop=last),
                            deps=[ev_pv] + ev_con + ld_c)
                    ev_z = e.op("vector", lambda v: v.tensor_scalar_add(
                        zrow[0:1, :], ps_y[64:65, :], 1e-12), deps=[ev_pv] + war["rrow"])
                    ev_r = e.op("vector", lambda v: v.reciprocal(
                        rrow[0:1, :], zrow[0:1, :]), deps=[ev_z])
                    ev_b = e.op("tensor", lambda t: t.matmul(
                        ps_b[0:64, :], ones64[0:1, :], rrow[0:1, :],
                        start=True, stop=True), deps=[ev_r] + ev_con + war["psb"])
                    ev_ye = e.op("scalar", lambda s: s.copy(
                        ysc[:, :], ps_y[0:64, :]), deps=[ev_pv] + war["ysc"])
                    war["psy"] = [ev_ye]
                    ev_n = e.op("vector", lambda v: v.tensor_tensor(
                        ysc2[:, :], ysc[:, :], ps_b[0:64, :], OP.mult),
                        deps=[ev_ye, ev_b] + war["ysc2"])
                    war["rrow"] = [ev_b]
                    war["psb"] = [ev_n]
                    war["ysc"] = [ev_n]
                    hb = hh * 64
                    ev_g = e.op("vector", lambda v, hp=hp, hb=hb, q0=q0: v.tensor_copy(
                        a2[hp][hb:hb + 64, q0:q0 + 512], ysc2[:, :]), deps=[ev_n])
                    war["ysc2"] = [ev_g]
                    head_evs.append(ev_g)
            war["qk"] = head_evs
            war["v2e"] = head_evs
            out_evs.extend(head_evs)
    at_evs = out_evs

    wo_sb = wv_sb
    woev = [e.dma("gpsimd", out=wo_sb[i][:, :], in_=wo[i * 128:(i + 1) * 128, :],
                  deps=at_evs) for i in range(NT)]
    prev3 = []
    with c.scope():
        pb = [c.ps([128, 512]) for _ in range(2)]
        pb_war = {0: at_evs, 1: at_evs}
        tmp = [c.sb([128, 512], F32) for _ in range(2)]
        tmp_war = {0: [], 1: []}
        for ci in range(HALF // 512):
            c0 = ci * 512
            for m in range(NT):
                bank = m % 2
                ev = None
                for k in range(NT):
                    ev = e.op("tensor", lambda t, k=k, m=m, c0=c0, bank=bank: t.matmul(
                        pb[bank][:, :], wo_sb[k][:, m * 128:(m + 1) * 128],
                        a2[k][:, c0:c0 + 512],
                        start=(k == 0), stop=(k == NT - 1)),
                        deps=at_evs + woev + pb_war[bank])
                xb = xc[m]
                ev_x = e.dma("sync", out=xb[:, :],
                             in_=xT[m * 128:(m + 1) * 128, Q0 + c0:Q0 + c0 + 512],
                             deps=ln_evs + prev3)
                tb = tmp[bank]
                ev_1 = e.op("vector", lambda v, xb=xb, tb=tb, bank=bank: v.tensor_tensor(
                    tb[:, :], xb[:, :], pb[bank][:, :], OP.add),
                    deps=[ev, ev_x] + tmp_war[bank])
                ev_st = e.dma("sync", out=yT[m * 128:(m + 1) * 128, c0:c0 + 512],
                              in_=tb[:, :], deps=[ev_1])
                pb_war[bank] = [ev_1]
                tmp_war[bank] = [ev_st]
                prev3 = [ev_1, ev_st]
    return c.finish()


def run_stage7(x_full, Wq, Wk, Wv, Wo):
    nc = _get_prog("stage7", build_stage7)
    r = np.arange(512)
    kk = np.arange(512)
    dmask = (kk[:, None] <= r[None, :]).astype(np.float32)
    hmask = 0.5 * dmask
    in_maps = []
    for core in range(N_CORES):
        b, h = core // 2, core % 2
        own = x_full[b, h * HALF:(h + 1) * HALF, :]
        if h == 0:
            prev = np.zeros_like(own)
            pm = np.zeros((128, 1), np.float32)
        else:
            prev = x_full[b, 0:HALF, :]
            pm = np.ones((128, 1), np.float32)
        xT = np.ascontiguousarray(np.concatenate([prev, own], axis=0).T)
        in_maps.append(dict(xT=xT, wq=Wq, wk=Wk, wv=Wv, wo=Wo, pmcol=pm,
                            dmask=dmask, hmask=hmask))
    res = run_bass_kernel_spmd(nc, in_maps, core_ids=list(range(N_CORES)))
    out = np.empty_like(x_full)
    for core in range(N_CORES):
        b, h = core // 2, core % 2
        out[b, h * HALF:(h + 1) * HALF, :] = res.results[core]["yT"].T
    return out


def build_stage3a():
    c = Ctx("stage3a")
    e = c.e
    Q0 = HALF
    xT = c.inp("xT", [D, T])
    qkvw = c.inp("qkvw", [D, 3072])
    kvb_row = c.inp("kvb_row", [1, 2048])
    qb_col = c.inp("qb_col", [D, 1])
    kb_col = c.inp("kb_col", [D, 1])
    ob_col = c.inp("ob_col", [D, 1])
    n1_col = c.inp("n1_col", [D, 1])
    ow = c.inp("ow", [D, D])
    cosw_row = c.inp("cosw_row", [1, T])
    sinw_row = c.inp("sinw_row", [1, T])
    coswc = c.inp("coswc", [128, T // 128])
    sinwc = c.inp("sinwc", [128, T // 128])
    dmask = c.inp("dmask", [512, 512])
    hT = c.out("hT", [D, HALF])
    NKT = T // 128

    xc = [c.sb([128, 512], F32) for _ in range(NT)]
    a2 = [c.sb([128, HALF], BF16, f"a2_{i}") for i in range(NT)]
    cos_b = c.sb([128, HALF], BF16, "cos_b")
    sin_b = c.sb([128, HALF], BF16, "sin_b")
    cwc = c.sb([128, NKT], F32)
    swc = c.sb([128, NKT], F32)
    kb_r = c.sb([1, 128], BF16)
    vb_r = c.sb([1, 128], BF16)
    qb = [c.sb([128, 1], F32) for _ in range(NT)]
    obc = [c.sb([128, 1], F32) for _ in range(NT)]
    n1c = [c.sb([128, 1], F32) for _ in range(NT)]
    dm = [c.sb([128, 512], BF16) for _ in range(4)]
    ones1 = c.sb([1, 128], BF16)
    onesr = c.sb([1, 128], F32)
    ones64 = c.sb([1, 64], F32)
    crow = c.sb([1, 512], F32)
    ld_c = [e.dma("sync", out=cwc[:, :], in_=coswc[:, :]),
            e.dma("sync", out=swc[:, :], in_=sinwc[:, :])]
    kbc = [c.sb([128, 1], F32) for _ in range(NT)]
    for i in range(NT):
        ld_c.append(e.dma("sync", out=qb[i][:, :], in_=qb_col[i * 128:(i + 1) * 128, :]))
        ld_c.append(e.dma("sync", out=kbc[i][:, :], in_=kb_col[i * 128:(i + 1) * 128, :]))
        ld_c.append(e.dma("sync", out=obc[i][:, :], in_=ob_col[i * 128:(i + 1) * 128, :]))
        ld_c.append(e.dma("sync", out=n1c[i][:, :], in_=n1_col[i * 128:(i + 1) * 128, :]))
    for j in range(4):
        ld_c.append(e.dma("gpsimd", out=dm[j][:, :], in_=dmask[j * 128:(j + 1) * 128, :]))
    ev_con = [e.op("vector", lambda v: v.memset(ones1[:, :], 1.0)),
              e.op("vector", lambda v: v.memset(onesr[:, :], 1.0)),
              e.op("vector", lambda v: v.memset(ones64[:, :], 1.0))]
    with c.scope():
        ps_t = c.ps([128, 512])
        war_t = []
        for src_row, dst in ((cosw_row, cos_b), (sinw_row, sin_b)):
            for ci in range(HALF // 512):
                c0 = ci * 512
                ev_l = e.dma("sync", out=crow[0:1, :], in_=src_row[0:1, Q0 + c0:Q0 + c0 + 512],
                             deps=war_t)
                ev_m = e.op("tensor", lambda t, c0=c0: t.matmul(
                    ps_t[:, :], onesr[0:1, :], crow[0:1, :], start=True, stop=True),
                    deps=[ev_l] + ev_con + war_t)
                ev_e = e.op("vector", lambda v, c0=c0, dst=dst: v.tensor_copy(
                    dst[:, c0:c0 + 512], ps_t[:, :]), deps=[ev_m])
                war_t = [ev_e]
        tab_evs = [ev_e]

    rms1 = [c.sb([128, T], BF16, f"rms1_{i}") for i in range(NT)]
    ln_evs = []
    prev = []
    for ci in range(T // 512):
        c0 = ci * 512
        ld = [e.dma("sync", out=xc[i][:, :], in_=xT[i * 128:(i + 1) * 128, c0:c0 + 512],
                    deps=prev) for i in range(NT)]
        evs = emit_layernorm_fm(c, xc, 512, rms1, deps=ld + ld_c, col_off=c0,
                                rms1_gamma=n1c, rms1_eps=1e-6)
        prev = evs
        ln_evs.extend(evs)

    with c.scope():
        wqp = [c.sb([128, 128], BF16) for _ in range(NT)]
        wkp = [c.sb([128, 128], BF16) for _ in range(NT)]
        wvp = [c.sb([128, 128], BF16) for _ in range(NT)]
        qcp = c.sb([128, HALF], BF16, "qcp")
        qsp = c.sb([128, HALF], BF16, "qsp")
        kfc = c.sb([128, HALF], BF16, "kfc")
        kfs = c.sb([128, HALF], BF16, "kfs")
        ktc = c.sb([128, NKT * 128], BF16, "ktc")
        kts = c.sb([128, NKT * 128], BF16, "kts")
        v2e = c.sb([128, NKT * 130], BF16, "c_v2e")
        qrel = c.sb([128, 512], F32)
        sbf = [c.sb([128, 512], BF16) for _ in range(2)]
        stc_bf = c.sb([128, 65], BF16)
        sts_bf = c.sb([128, 65], BF16)
        stc_lo = c.sb([128, 65], BF16)
        sts_lo = c.sb([128, 65], BF16)
        ysc = c.sb([64, 512], F32)
        ysc2 = c.sb([64, 512], F32)
        rrow = c.sb([1, 512], F32)
        zrow = c.sb([1, 512], F32)
        ps_s = [c.ps([128, 512]) for _ in range(2)]
        ps_y = c.ps([128, 512])
        ps_b = c.ps([128, 512])
        ps_stc = c.ps([64, 65])
        ps_sts = c.ps([64, 65])
        war = {"ps0": [], "ps1": [], "psy": [], "psb": [], "s0": [], "s1": [],
               "qk": [], "v2e": [], "ysc": [], "ysc2": [], "rrow": [],
               "w": [], "qrel": [], "st": [], "stbf": []}
        v2e_3d = v2e[:, :].rearrange("p (t c) -> p t c", c=130)
        ev_vo = [e.op("vector", lambda v: v.memset(v2e_3d[:, :, 64:65], 1.0)),
                 e.op("vector", lambda v: v.memset(v2e_3d[:, :, 129:130], 1.0))]
        out_evs = []
        for hp in range(8):
            wev = [e.dma("gpsimd", out=kb_r[:, :],
                         in_=kvb_row[0:1, hp * 128:hp * 128 + 128], deps=war["w"]),
                   e.dma("gpsimd", out=vb_r[:, :],
                         in_=kvb_row[0:1, 1024 + hp * 128:1024 + hp * 128 + 128],
                         deps=war["w"])]
            for k in range(NT):
                wev.append(e.dma("gpsimd", out=wqp[k][:, :],
                                 in_=qkvw[k * 128:(k + 1) * 128, hp * 128:hp * 128 + 128],
                                 deps=war["w"]))
                wev.append(e.dma("gpsimd", out=wkp[k][:, :],
                                 in_=qkvw[k * 128:(k + 1) * 128, 1024 + hp * 128:1024 + hp * 128 + 128],
                                 deps=war["w"]))
                wev.append(e.dma("gpsimd", out=wvp[k][:, :],
                                 in_=qkvw[k * 128:(k + 1) * 128, 2048 + hp * 128:2048 + hp * 128 + 128],
                                 deps=war["w"]))
            proj_mm = []
            kq_evs = []
            for ci in range(HALF // 512):
                c0 = ci * 512
                bank = ci % 2
                ev = None
                for k in range(NT):
                    ev = e.op("tensor", lambda t, k=k, c0=c0, bank=bank: t.matmul(
                        ps_s[bank][:, :], wqp[k][:, :], rms1[k][:, Q0 + c0:Q0 + c0 + 512],
                        start=(k == 0), stop=(k == NT - 1)),
                        deps=ln_evs + wev + war[f"ps{bank}"])
                proj_mm.append(ev)
                ev_e = e.op("scalar", lambda s, hp=hp, bank=bank: s.activation(
                    qrel[:, :], ps_s[bank][:, :], AF.Relu, bias=qb[hp][:, 0:1]),
                    deps=[ev] + war["qrel"])
                war[f"ps{bank}"] = [ev_e]
                evq = []
                evq.append(e.op("vector", lambda v, c0=c0: v.tensor_tensor(
                    qcp[:, c0:c0 + 512], qrel[:, :],
                    cos_b[:, c0:c0 + 512], OP.mult),
                    deps=[ev_e] + tab_evs + war["qk"]))
                evq.append(e.op("vector", lambda v, c0=c0: v.tensor_tensor(
                    qsp[:, c0:c0 + 512], qrel[:, :],
                    sin_b[:, c0:c0 + 512], OP.mult),
                    deps=[ev_e] + tab_evs + war["qk"]))
                war["qrel"] = evq
                kq_evs += evq
            for ci in range(HALF // 512):
                c0 = ci * 512
                bank = ci % 2
                ev = None
                for k in range(NT):
                    ev = e.op("tensor", lambda t, k=k, c0=c0, bank=bank: t.matmul(
                        ps_s[bank][:, :], wkp[k][:, :], rms1[k][:, Q0 + c0:Q0 + c0 + 512],
                        start=(k == 0), stop=(k == NT - 1)),
                        deps=ln_evs + wev + war[f"ps{bank}"])
                proj_mm.append(ev)
                ev_e = e.op("scalar", lambda s, hp=hp, bank=bank: s.activation(
                    qrel[:, :], ps_s[bank][:, :], AF.Relu, bias=kbc[hp][:, 0:1]),
                    deps=[ev] + war["qrel"] + ld_c)
                war[f"ps{bank}"] = [ev_e]
                ev1 = e.op("vector", lambda v, c0=c0: v.tensor_tensor(
                    kfc[:, c0:c0 + 512], qrel[:, :], cos_b[:, c0:c0 + 512], OP.mult),
                    deps=[ev_e] + tab_evs + war["qk"])
                ev2 = e.op("vector", lambda v, c0=c0: v.tensor_tensor(
                    kfs[:, c0:c0 + 512], qrel[:, :], sin_b[:, c0:c0 + 512], OP.mult),
                    deps=[ev_e] + tab_evs + war["qk"])
                war["qrel"] = [ev1, ev2]
                kq_evs += [ev1, ev2]
            v_evs = []
            for t0 in range(NKT):
                bank = t0 % 2
                ev = None
                for k in range(NT):
                    ev = e.op("tensor", lambda t, k=k, t0=t0, bank=bank: t.matmul(
                        ps_s[bank][:, 0:128], rms1[k][:, t0 * 128:(t0 + 1) * 128],
                        wkp[k][:, :], start=(k == 0), stop=False),
                        deps=ln_evs + wev + war[f"ps{bank}"])
                ev = e.op("tensor", lambda t, bank=bank: t.matmul(
                    ps_s[bank][:, 0:128], ones1[0:1, :], kb_r[0:1, :],
                    start=False, stop=True),
                    deps=[ev] + wev + ev_con)
                proj_mm.append(ev)
                ev_e = e.op("scalar", lambda s, bank=bank: s.activation(
                    qrel[:, 0:128], ps_s[bank][:, 0:128], AF.Relu),
                    deps=[ev] + war["qrel"])
                war[f"ps{bank}"] = [ev_e]
                ev1 = e.op("vector", lambda v, t0=t0: v.tensor_scalar_mul(
                    ktc[:, t0 * 128:(t0 + 1) * 128], qrel[:, 0:128], cwc[:, t0:t0 + 1]),
                    deps=[ev_e] + ld_c + war["qk"])
                ev2 = e.op("vector", lambda v, t0=t0: v.tensor_scalar_mul(
                    kts[:, t0 * 128:(t0 + 1) * 128], qrel[:, 0:128], swc[:, t0:t0 + 1]),
                    deps=[ev_e] + ld_c + war["qk"])
                war["qrel"] = [ev1, ev2]
                kq_evs += [ev1, ev2]
                bank = (t0 + 1) % 2
                ev = None
                for k in range(NT):
                    ev = e.op("tensor", lambda t, k=k, t0=t0, bank=bank: t.matmul(
                        ps_s[bank][:, 0:128], rms1[k][:, t0 * 128:(t0 + 1) * 128],
                        wvp[k][:, :], start=(k == 0), stop=False),
                        deps=ln_evs + wev + war[f"ps{bank}"])
                ev = e.op("tensor", lambda t, bank=bank: t.matmul(
                    ps_s[bank][:, 0:128], ones1[0:1, :], vb_r[0:1, :],
                    start=False, stop=True),
                    deps=[ev] + wev + ev_con)
                proj_mm.append(ev)

                b0 = t0 * 130
                ev_e1 = e.op("vector", lambda v, b0=b0, bank=bank: v.tensor_copy(
                    v2e[:, b0:b0 + 64], ps_s[bank][:, 0:64]),
                    deps=[ev] + war["v2e"] + ev_vo)
                ev_e = e.op("vector", lambda v, b0=b0, bank=bank: v.tensor_copy(
                    v2e[:, b0 + 65:b0 + 129], ps_s[bank][:, 64:128]),
                    deps=[ev_e1])
                war[f"ps{bank}"] = [ev_e]
                v_evs.append(ev_e)
            war["w"] = proj_mm
            head_evs = []
            for hh in range(2):
                hb = hh * 64
                ev_su = None
                NPS = HALF // 128
                for t0 in range(NPS):
                    ev_su = e.op("tensor", lambda t, t0=t0, hb=hb, hh=hh, first=(t0 == 0), lst=(t0 == NPS - 1): t.matmul(
                        ps_stc[:, :], ktc[:, t0 * 128 + hb:t0 * 128 + hb + 64],
                        v2e[:, t0 * 130 + hh * 65:t0 * 130 + hh * 65 + 65],
                        start=first, stop=lst, skip_group_check=True),
                        deps=kq_evs + v_evs + war["st"])
                    ev_su = e.op("tensor", lambda t, t0=t0, hb=hb, hh=hh, first=(t0 == 0), lst=(t0 == NPS - 1): t.matmul(
                        ps_sts[:, :], kts[:, t0 * 128 + hb:t0 * 128 + hb + 64],
                        v2e[:, t0 * 130 + hh * 65:t0 * 130 + hh * 65 + 65],
                        start=first, stop=lst, skip_group_check=True),
                        deps=kq_evs + v_evs + war["st"])
                for qc in range(HALF // 512):
                    q0 = qc * 512
                    ev_sc = e.op("scalar", lambda s, hb=hb: s.copy(
                        stc_bf[hb:hb + 64, :], ps_stc[:, :]),
                        deps=[ev_su] + war["stbf"])
                    ev_ss = e.op("scalar", lambda s, hb=hb: s.copy(
                        sts_bf[hb:hb + 64, :], ps_sts[:, :]),
                        deps=[ev_su] + war["stbf"])
                    ev_sc = e.op("vector", lambda v, hb=hb: v.tensor_tensor(
                        stc_lo[hb:hb + 64, :], ps_stc[:, :], stc_bf[hb:hb + 64, :],
                        OP.subtract), deps=[ev_sc])
                    ev_ss = e.op("vector", lambda v, hb=hb: v.tensor_tensor(
                        sts_lo[hb:hb + 64, :], ps_sts[:, :], sts_bf[hb:hb + 64, :],
                        OP.subtract), deps=[ev_ss])
                    ev_pv = None
                    for j in range(4):
                        ks = (Q0 + q0) // 128 + j
                        bank = j % 2
                        ko = ks * 128 - Q0
                        ev_s1 = e.op("tensor", lambda t, ko=ko, q0=q0, hb=hb, bank=bank: t.matmul(
                            ps_s[bank][:, :], kfc[hb:hb + 64, ko:ko + 128],
                            qcp[hb:hb + 64, q0:q0 + 512], start=True, stop=False),
                            deps=kq_evs + war[f"ps{bank}"] + war[f"s{bank}"])
                        ev_s2 = e.op("tensor", lambda t, ko=ko, q0=q0, hb=hb, bank=bank: t.matmul(
                            ps_s[bank][:, :], kfs[hb:hb + 64, ko:ko + 128],
                            qsp[hb:hb + 64, q0:q0 + 512], start=False, stop=True),
                            deps=[ev_s1])
                        ev_m = e.op("vector", lambda v, j=j, bank=bank: v.tensor_tensor(
                            sbf[bank][:, :], ps_s[bank][:, :], dm[j][:, :], OP.mult),
                            deps=[ev_s2] + ld_c + war[f"s{bank}"])
                        war[f"ps{bank}"] = [ev_m]
                        ev_pv = e.op("tensor", lambda t, ks=ks, hh=hh, bank=bank, first=(j == 0): t.matmul(
                            ps_y[0:65, :], v2e[:, ks * 130 + hh * 65:ks * 130 + hh * 65 + 65],
                            sbf[bank][:, :], start=first, stop=False),
                            deps=[ev_m] + v_evs + (war["psy"] if j == 0 else []))
                        war[f"s{bank}"] = [ev_pv]
                    ev_pv = e.op("tensor", lambda t, q0=q0, hb=hb: t.matmul(
                        ps_y[0:65, :], stc_bf[hb:hb + 64, :], qcp[hb:hb + 64, q0:q0 + 512],
                        start=False, stop=False), deps=[ev_pv, ev_sc])
                    ev_pv = e.op("tensor", lambda t, q0=q0, hb=hb: t.matmul(
                        ps_y[0:65, :], stc_lo[hb:hb + 64, :], qcp[hb:hb + 64, q0:q0 + 512],
                        start=False, stop=False), deps=[ev_pv, ev_sc])
                    ev_pv = e.op("tensor", lambda t, q0=q0, hb=hb: t.matmul(
                        ps_y[0:65, :], sts_bf[hb:hb + 64, :], qsp[hb:hb + 64, q0:q0 + 512],
                        start=False, stop=False), deps=[ev_pv, ev_ss])
                    ev_pv = e.op("tensor", lambda t, q0=q0, hb=hb: t.matmul(
                        ps_y[0:65, :], sts_lo[hb:hb + 64, :], qsp[hb:hb + 64, q0:q0 + 512],
                        start=False, stop=True), deps=[ev_pv, ev_ss])
                    war["stbf"] = [ev_pv]
                    ev_z = e.op("vector", lambda v: v.tensor_scalar_add(
                        zrow[0:1, :], ps_y[64:65, :], 1e-6), deps=[ev_pv] + war["rrow"])
                    ev_r = e.op("vector", lambda v: v.reciprocal(
                        rrow[0:1, :], zrow[0:1, :]), deps=[ev_z])
                    ev_b = e.op("tensor", lambda t: t.matmul(
                        ps_b[0:64, :], ones64[0:1, :], rrow[0:1, :],
                        start=True, stop=True), deps=[ev_r] + ev_con + war["psb"])
                    ev_ye = e.op("scalar", lambda s: s.copy(
                        ysc[:, :], ps_y[0:64, :]), deps=[ev_pv] + war["ysc"])
                    war["psy"] = [ev_ye]
                    ev_n = e.op("vector", lambda v: v.tensor_tensor(
                        ysc2[:, :], ysc[:, :], ps_b[0:64, :], OP.mult),
                        deps=[ev_ye, ev_b] + war["ysc2"])
                    war["rrow"] = [ev_b]
                    war["psb"] = [ev_n]
                    war["ysc"] = [ev_n]
                    ev_g = e.op("vector", lambda v, hp=hp, hb=hb, q0=q0: v.tensor_copy(
                        a2[hp][hb:hb + 64, q0:q0 + 512], ysc2[:, :]), deps=[ev_n])
                    war["ysc2"] = [ev_g]
                    head_evs.append(ev_g)
                    for j in range(4):
                        ks = (Q0 + q0) // 128 + j
                        stp = j == 3
                        ev_su = e.op("tensor", lambda t, ks=ks, hb=hb, hh=hh, stp=stp: t.matmul(
                            ps_stc[:, :], ktc[:, ks * 128 + hb:ks * 128 + hb + 64],
                            v2e[:, ks * 130 + hh * 65:ks * 130 + hh * 65 + 65],
                            start=False, stop=stp, skip_group_check=True),
                            deps=[ev_sc, ev_ss] + kq_evs + v_evs)
                        ev_su = e.op("tensor", lambda t, ks=ks, hb=hb, hh=hh, stp=stp: t.matmul(
                            ps_sts[:, :], kts[:, ks * 128 + hb:ks * 128 + hb + 64],
                            v2e[:, ks * 130 + hh * 65:ks * 130 + hh * 65 + 65],
                            start=False, stop=stp, skip_group_check=True),
                            deps=[ev_sc, ev_ss] + kq_evs + v_evs)
                war["st"] = [ev_sc, ev_ss, ev_su]
            war["qk"] = head_evs
            war["v2e"] = head_evs
            out_evs.extend(head_evs)
    at_evs = out_evs

    ow_sb = [c.sb([128, D], BF16, f"ow{i}") for i in range(NT)]
    owev = [e.dma("gpsimd", out=ow_sb[i][:, :], in_=ow[i * 128:(i + 1) * 128, :],
                  deps=at_evs) for i in range(NT)]
    lnf = c.sb([128, 512], F32, "lnf3")
    prev3 = []
    with c.scope():
        pb = [c.ps([128, 512]) for _ in range(2)]
        pb_war = {0: at_evs, 1: at_evs}
        tmp = [c.sb([128, 512], F32) for _ in range(2)]
        tmp_war = {0: [], 1: []}
        for ci in range(HALF // 512):
            c0 = ci * 512
            ld3 = [e.dma("sync", out=xc[i][:, :],
                         in_=xT[i * 128:(i + 1) * 128, Q0 + c0:Q0 + c0 + 512],
                         deps=ln_evs + prev3) for i in range(NT)]
            ch3 = []

            def consume(m, evf, c0=c0, ch3=ch3):
                bank = m % 2
                ev = None
                for k in range(NT):
                    ev = e.op("tensor", lambda t, k=k, m=m, c0=c0, bank=bank: t.matmul(
                        pb[bank][:, :], ow_sb[k][:, m * 128:(m + 1) * 128],
                        a2[k][:, c0:c0 + 512],
                        start=(k == 0), stop=(k == NT - 1)),
                        deps=at_evs + owev + pb_war[bank])
                tb = tmp[bank]
                ev_1 = e.op("vector", lambda v, m=m, tb=tb, bank=bank: v.scalar_tensor_tensor(
                    tb[:, :], pb[bank][:, :], obc[m][:, 0:1], lnf[:, :],
                    OP.add, OP.add), deps=[evf, ev] + tmp_war[bank] + ld_c)
                ev_st = e.dma("sync", out=hT[m * 128:(m + 1) * 128, c0:c0 + 512],
                              in_=tb[:, :], deps=[ev_1])
                pb_war[bank] = [ev_1]
                tmp_war[bank] = [ev_st]
                ch3 += [ev_1, ev_st]
                return [ev_1]

            emit_layernorm_fm(c, xc, 512, None, deps=ld3 + at_evs,
                              shared_f32=lnf, consume_fn=consume)
            prev3 = ch3
    return c.finish()


def build_stage3b():
    c = Ctx("stage3b")
    e = c.e
    xT = c.inp("xT", [D, HALF])
    hT = c.inp("hT", [D, HALF])
    n2_col = c.inp("n2_col", [D, 1])
    w1 = c.inp("w1", [D, 4096])
    b1_col = c.inp("b1_col", [4096, 1])
    w2 = c.inp("w2", [4096, D])
    b2_col = c.inp("b2_col", [D, 1])
    yT = c.out("yT", [D, HALF])

    hx = [c.sb([128, HALF], F32, f"h{i}") for i in range(NT)]
    mi = [c.sb([128, HALF], BF16, f"mi{i}") for i in range(NT)]
    h1g = [c.sb([128, 512], BF16, f"h1g{i}") for i in range(32)]
    xc2 = [c.sb([128, 512], F32) for _ in range(2)]
    n2c = [c.sb([128, 1], F32) for _ in range(NT)]
    b1c = [c.sb([128, 1], F32) for _ in range(32)]
    b2c = [c.sb([128, 1], F32) for _ in range(NT)]
    w1p = [[c.sb([128, 128], BF16) for _ in range(NT)] for _ in range(2)]
    w2p = [c.sb([128, D], BF16) for _ in range(2)]

    ld = []
    for i in range(NT):
        ld.append(e.dma("sync", out=hx[i][:, :], in_=hT[i * 128:(i + 1) * 128, :]))
        ld.append(e.dma("sync", out=n2c[i][:, :], in_=n2_col[i * 128:(i + 1) * 128, :]))
        ld.append(e.dma("sync", out=b2c[i][:, :], in_=b2_col[i * 128:(i + 1) * 128, :]))
    for j in range(32):
        ld.append(e.dma("sync", out=b1c[j][:, :], in_=b1_col[j * 128:(j + 1) * 128, :]))

    rms_evs = emit_layernorm_fm(c, hx, HALF, mi, deps=ld, gamma=n2c, eps=1e-6,
                                rms=True)

    with c.scope():
        pbank = [c.ps([128, 512]) for _ in range(2)]
        obank = [c.ps([128, 512]) for _ in range(4)]
        pb_war = {0: [], 1: []}
        w1_war = {0: [], 1: []}
        w2_war = {0: [], 1: []}
        ob_war = {i: [] for i in range(4)}
        tmp = [c.sb([128, 512], F32) for _ in range(2)]
        tmp_war = {0: [], 1: []}
        xcw = {0: [], 1: []}
        for ci in range(HALF // 512):
            c0 = ci * 512
            g_evs = []
            for f1 in range(32):
                wp = f1 % 2
                wev = [e.dma("gpsimd", out=w1p[wp][k][:, :],
                             in_=w1[k * 128:(k + 1) * 128, f1 * 128:(f1 + 1) * 128],
                             deps=w1_war[wp]) for k in range(NT)]
                bank = f1 % 2
                ev = None
                for k in range(NT):
                    ev = e.op("tensor", lambda t, k=k, c0=c0, wp=wp, bank=bank: t.matmul(
                        pbank[bank][:, :], w1p[wp][k][:, :], mi[k][:, c0:c0 + 512],
                        start=(k == 0), stop=(k == NT - 1)),
                        deps=rms_evs + wev + pb_war[bank])
                w1_war[wp] = [ev]
                ev_g = e.op("scalar", lambda s, f1=f1, bank=bank: s.activation(
                    h1g[f1][:, :], pbank[bank][:, :], AF.Gelu_apprx_tanh,
                    bias=b1c[f1][:, 0:1]), deps=[ev] + ld + g_evs[-2:-1])
                pb_war[bank] = [ev_g]
                g_evs.append(ev_g)
            for grp in range(2):
                ms = list(range(grp * 4, grp * 4 + 4))
                ev_k = {m: None for m in ms}
                for k in range(32):
                    wp = k % 2
                    wev2 = [e.dma("gpsimd", out=w2p[wp][:, :],
                                  in_=w2[k * 128:(k + 1) * 128, :],
                                  deps=w2_war[wp])]
                    for m in ms:
                        bank = m - grp * 4
                        ev_k[m] = e.op("tensor", lambda t, k=k, m=m, wp=wp, bank=bank: t.matmul(
                            obank[bank][:, :], w2p[wp][:, m * 128:(m + 1) * 128],
                            h1g[k][:, :], start=(k == 0), stop=(k == 31)),
                            deps=g_evs + wev2 + ob_war[bank])
                    w2_war[wp] = [ev_k[m] for m in ms]
                for m in ms:
                    bank = m - grp * 4
                    tb = tmp[m % 2]
                    xb = xc2[m % 2]
                    ev_x = e.dma("sync", out=xb[:, :],
                                 in_=xT[m * 128:(m + 1) * 128, c0:c0 + 512],
                                 deps=xcw[m % 2])
                    ev_1 = e.op("vector", lambda v, m=m, c0=c0, tb=tb, bank=bank: v.scalar_tensor_tensor(
                        tb[:, :], obank[bank][:, :], b2c[m][:, 0:1],
                        hx[m][:, c0:c0 + 512], OP.add, OP.add),
                        deps=[ev_k[m]] + tmp_war[m % 2] + ld)
                    ev_2 = e.op("vector", lambda v, tb=tb, xb=xb: v.tensor_tensor(
                        tb[:, :], tb[:, :], xb[:, :], OP.add), deps=[ev_1, ev_x])
                    ev_st = e.dma("sync", out=yT[m * 128:(m + 1) * 128, c0:c0 + 512],
                                  in_=tb[:, :], deps=[ev_2])
                    ob_war[bank] = [ev_1]
                    tmp_war[m % 2] = [ev_st]
                    xcw[m % 2] = [ev_2]
    return c.finish()


def run_stage3(x_full, inp):
    nc_a = _get_prog("stage3a", build_stage3a)
    nc_b = _get_prog("stage3b", build_stage3b)
    qkvw = np.asarray(inp["L0_attn_b_qkv_w"])
    qkvb = np.asarray(inp["L0_attn_b_qkv_b"])
    ow = np.asarray(inp["L0_attn_b_out_w"])
    ob = np.asarray(inp["L0_attn_b_out_b"])
    n1 = np.asarray(inp["L0_attn_b_norm1_w"])
    n2 = np.asarray(inp["L0_attn_b_norm2_w"])
    w1 = np.asarray(inp["L0_attn_b_mlp_w1"])
    b1 = np.asarray(inp["L0_attn_b_mlp_b1"])
    w2 = np.asarray(inp["L0_attn_b_mlp_w2"])
    b2 = np.asarray(inp["L0_attn_b_mlp_b2"])
    pos = np.arange(T, dtype=np.float64)
    cw_g = np.cos(np.pi / 2 * pos / T).astype(np.float32)
    sw_g = np.sin(np.pi / 2 * pos / T).astype(np.float32)
    r = np.arange(512)
    kk = np.arange(512)
    dmask = (kk[:, None] <= r[None, :]).astype(np.float32)
    in_maps = []
    for core in range(N_CORES):
        b, h = core // 2, core % 2
        own = x_full[b, h * HALF:(h + 1) * HALF, :]
        if h == 0:
            prev = np.zeros_like(own)
            cw = np.concatenate([np.zeros(HALF, np.float32), cw_g[0:HALF]])
            sw = np.concatenate([np.zeros(HALF, np.float32), sw_g[0:HALF]])
        else:
            prev = x_full[b, 0:HALF, :]
            cw, sw = cw_g, sw_g
        xT = np.ascontiguousarray(np.concatenate([prev, own], axis=0).T)
        in_maps.append(dict(
            xT=xT, qkvw=qkvw,
            kvb_row=np.ascontiguousarray(qkvb[1024:3072].reshape(1, 2048)),
            qb_col=np.ascontiguousarray(qkvb[0:1024].reshape(D, 1)),
            kb_col=np.ascontiguousarray(qkvb[1024:2048].reshape(D, 1)),
            ob_col=np.ascontiguousarray(ob.reshape(D, 1)),
            n1_col=np.ascontiguousarray(n1.reshape(D, 1)),
            ow=ow,
            cosw_row=np.ascontiguousarray(cw.reshape(1, T)),
            sinw_row=np.ascontiguousarray(sw.reshape(1, T)),
            coswc=np.ascontiguousarray(cw.reshape(T // 128, 128).T),
            sinwc=np.ascontiguousarray(sw.reshape(T // 128, 128).T),
            dmask=dmask))
    res = run_bass_kernel_spmd(nc_a, in_maps, core_ids=list(range(N_CORES)))
    in_maps_b = []
    for core in range(N_CORES):
        b, h = core // 2, core % 2
        own = x_full[b, h * HALF:(h + 1) * HALF, :]
        in_maps_b.append(dict(
            xT=np.ascontiguousarray(own.T),
            hT=np.ascontiguousarray(res.results[core]["hT"]),
            n2_col=np.ascontiguousarray(n2.reshape(D, 1)),
            w1=w1, b1_col=np.ascontiguousarray(b1.reshape(4096, 1)),
            w2=w2, b2_col=np.ascontiguousarray(b2.reshape(D, 1))))
    res_b = run_bass_kernel_spmd(nc_b, in_maps_b, core_ids=list(range(N_CORES)))
    out = np.empty_like(x_full)
    for core in range(N_CORES):
        b, h = core // 2, core % 2
        out[b, h * HALF:(h + 1) * HALF, :] = res_b.results[core]["yT"].T
    return out


DBG_L1 = False
SER_MLP = False
SER_ATTN = False
TE = 2688
O0 = 640
NTE = TE // 128


def build_l1attn(apply_ln):
    c = Ctx(f"l1attn{int(apply_ln)}")
    e = c.e
    e.serialize = SER_ATTN
    xT = c.inp("xT", [D, TE])
    wqkv = c.inp("wqkv", [D, 1536])
    wo = c.inp("wo", [512, D])
    qs_row = c.inp("qs_row", [1, 64])
    ks_row = c.inp("ks_row", [1, 64])
    tqc_i = c.inp("tqc", [128, NTE * 64])
    tqs_i = c.inp("tqs", [128, NTE * 64])
    tkc_i = c.inp("tkc", [128, NTE * 64])
    tks_i = c.inp("tks", [128, NTE * 64])
    pmask = c.inp("pmask", [512, 512])
    dmask = c.inp("dmask", [512, 512])
    pm_col = c.inp("pm_col", [128, 1])
    id_in = c.inp("ident_in", [128, 128])
    ymT = c.out("ymT", [D, HALF])
    if DBG_L1:
        dbg_qf = c.out("dbg_qf", [128, 512])
        dbg_qn = c.out("dbg_qn", [128, 512])
        dbg_ns = c.out("dbg_ns", [128, 8])
        dbg_qr = c.out("dbg_qr", [128, 512])
        dbg_o2 = c.out("dbg_o2", [512, HALF])
        dbg_qT = c.out("dbg_qT", [128, HALF])
        dbg_kT = c.out("dbg_kT", [128, (NTE - 1) * 128])
        dbg_v = c.out("dbg_v", [128, (NTE - 1) * 520])

    xin = [c.sb([128, TE], BF16, f"xin{i}") for i in range(NT)]
    xc = [c.sb([128, 512], F32) for _ in range(NT)]
    wq_sb = [c.sb([128, 1536], BF16, f"wq{i}") for i in range(NT)]

    tqc = c.sb([128, 16 * 64], BF16, "tqc_s")
    tqs = c.sb([128, 16 * 64], BF16, "tqs_s")
    tkc = c.sb([128, 20 * 64], BF16, "tkc_s")
    tks = c.sb([128, 20 * 64], BF16, "tks_s")
    pms = [c.sb([128, 512], BF16) for _ in range(4)]
    dms = [c.sb([128, 512], BF16) for _ in range(4)]
    pm_sb = c.sb([128, 1], F32)
    qsb = c.sb([128, 64], BF16, "qsb")
    ksb = c.sb([128, 64], BF16, "ksb")
    qT = [c.sb([128, HALF], BF16, f"qT{p}") for p in range(4)]
    kT = [c.sb([128, (NTE - 1) * 128], BF16, f"kT{p}") for p in range(4)]
    v2e = c.sb([128, (NTE - 1) * 520], BF16, "v2e")
    o2 = [c.sb([128, HALF], BF16, f"o2_{p}") for p in range(4)]
    ident = c.sb([128, 128], BF16, "ident")
    qrow = c.sb([1, 64], F32)
    nb48 = c.sb([128, 1], F32)
    ones64 = c.sb([1, 64], F32)
    pstr = [c.sb([128, 512], BF16) for _ in range(2)]
    pstr2 = [c.sb([128, 512], BF16) for _ in range(2)]
    ysc = c.sb([64, 512], BF16)
    ysc2 = c.sb([64, 512], BF16)
    rrow = c.sb([1, 512], F32)
    qf = c.sb([128, 512], F32)
    qsq = c.sb([128, 512], BF16)
    nsum = c.sb([128, 8], F32)
    nsum2 = c.sb([128, 8], F32)
    qn = c.sb([128, 512], F32)
    qrot = [c.sb([128, 512], BF16) for _ in range(2)]
    rhs_scr = c.sb([128, 512], BF16)

    ev_con = [e.op("vector", lambda v: v.memset(nb48[:, :], -48.0)),
              e.op("vector", lambda v: v.memset(ones64[:, :], 1.0))]
    ld_c = [e.dma("gpsimd", out=ident[:, :], in_=id_in[:, :])]
    for t, ti, t0w in ((tqc, tqc_i, (5, 21)), (tqs, tqs_i, (5, 21)),
                       (tkc, tkc_i, (1, 21)), (tks, tks_i, (1, 21))):
        ld_c.append(e.dma("gpsimd", out=t[:, :], in_=ti[:, t0w[0] * 64:t0w[1] * 64]))
    ld_c.append(e.dma("sync", out=pm_sb[:, :], in_=pm_col[:, :]))
    for j in range(4):
        ld_c.append(e.dma("gpsimd", out=pms[j][:, :], in_=pmask[j * 128:(j + 1) * 128, :]))
        ld_c.append(e.dma("gpsimd", out=dms[j][:, :], in_=dmask[j * 128:(j + 1) * 128, :]))
    wev = []
    for i in range(NT):
        wev.append(e.dma("gpsimd", out=wq_sb[i][:, :], in_=wqkv[i * 128:(i + 1) * 128, :]))

    onesr_l = c.sb([1, 128], F32, "onesr_l")
    ev_orl = e.op("vector", lambda v: v.memset(onesr_l[:, :], 1.0))
    with c.scope():
        ps_t = c.ps([128, 64])
        ev_l = e.dma("sync", out=qrow[0:1, :], in_=qs_row[0:1, :])
        ev_m = e.op("tensor", lambda t: t.matmul(
            ps_t[:, :], onesr_l[0:1, :], qrow[0:1, :], start=True, stop=True),
            deps=[ev_l, ev_orl] + ev_con)
        ev_q1 = e.op("vector", lambda v: v.tensor_copy(qsb[:, :], ps_t[:, :]),
                     deps=[ev_m])
        ev_l2 = e.dma("sync", out=qrow[0:1, :], in_=ks_row[0:1, :], deps=[ev_m])
        ev_m2 = e.op("tensor", lambda t: t.matmul(
            ps_t[:, :], onesr_l[0:1, :], qrow[0:1, :], start=True, stop=True),
            deps=[ev_l2, ev_q1, ev_orl])
        ev_q2 = e.op("vector", lambda v: v.tensor_copy(ksb[:, :], ps_t[:, :]),
                     deps=[ev_m2])
        qsk_evs = [ev_q1, ev_q2]

    ln_evs = []
    if apply_ln:
        prev = []
        for ci in range(6):
            c0 = ci * 512
            w = min(512, TE - c0)
            ld = [e.dma("sync", out=xc[i][:, :w], in_=xT[i * 128:(i + 1) * 128, c0:c0 + w],
                        deps=prev) for i in range(NT)]
            evs = emit_layernorm_fm(c, xc, w, xin, deps=ld, col_off=c0)
            prev = evs
            ln_evs.extend(evs)
    else:
        for i in range(NT):
            ln_evs.append(e.dma("gpsimd", out=xin[i][:, :], in_=xT[i * 128:(i + 1) * 128, :]))

    def emit_side(t0, psum, tab_c, tab_s, scale_b, is_q, deps, war):
        out = qrot[0 if is_q else 1]
        t0 = t0 - (5 if is_q else 1)
        prev_scr = list(war["scr"])
        ev0 = e.op("vector", lambda v: v.tensor_copy(qf[:, :], psum[:, :]),
                   deps=deps + prev_scr)
        ev1 = e.op("scalar", lambda s: s.activation(qsq[:, :], qf[:, :], AF.Square),
                   deps=[ev0] + prev_scr)
        red = [ev1]
        for h in range(8):
            red.append(e.op("vector", lambda v, h=h: v.reduce_sum(
                nsum[:, h:h + 1], qsq[:, h * 64:(h + 1) * 64],
                mybir.AxisListType.X), deps=red[-1:]))
        ev3 = e.op("vector", lambda v: v.tensor_scalar_max(
            nsum2[:, :], nsum[:, :], 1e-24), deps=red[-1:] + prev_scr)
        ev2 = e.op("scalar", lambda s: s.activation(nsum2[:, :], nsum2[:, :], AF.Ln),
                   deps=[ev3])
        ev4 = e.op("scalar", lambda s: s.activation(nsum2[:, :], nsum2[:, :], AF.Exp,
                                                    scale=-0.5), deps=[ev2])
        qf3 = qf[:, :].rearrange("p (h d) -> p h d", d=64)
        qn3 = qn[:, :].rearrange("p (h d) -> p h d", d=64)
        scr3 = qsq[:, :].rearrange("p (h d) -> p h d", d=64)
        rstd_b = nsum2[:, 0:8].unsqueeze(2).broadcast_to([128, 8, 64])
        ev5 = e.op("vector", lambda v: v.tensor_tensor(qn3, qf3, rstd_b, OP.mult),
                   deps=[ev4])
        sc_b = scale_b[:, 0:64].unsqueeze(1).broadcast_to([128, 8, 64])
        ev6 = e.op("vector", lambda v: v.tensor_tensor(qn3, qn3, sc_b, OP.mult),
                   deps=[ev5] + qsk_evs)
        tc = tab_c[:, t0 * 64:(t0 + 1) * 64].unsqueeze(1).broadcast_to([128, 8, 64])
        ev7 = e.op("vector", lambda v: v.tensor_tensor(
            rhs_scr[:, :].rearrange("p (h d) -> p h d", d=64), qn3, tc, OP.mult),
            deps=[ev6] + ld_c)
        ts_lo = tab_s[:, t0 * 64:t0 * 64 + 32].unsqueeze(1).broadcast_to([128, 8, 32])
        ts_hi = tab_s[:, t0 * 64 + 32:t0 * 64 + 64].unsqueeze(1).broadcast_to([128, 8, 32])
        ev8 = e.op("vector", lambda v: v.scalar_tensor_tensor(
            scr3[:, :, 0:32], qn3[:, :, 32:64], -1.0, ts_lo, OP.mult, OP.mult),
            deps=[ev7])
        ev9 = e.op("vector", lambda v: v.tensor_tensor(
            scr3[:, :, 32:64], qn3[:, :, 0:32], ts_hi, OP.mult), deps=[ev8])
        ev10 = e.op("vector", lambda v, out=out: v.tensor_tensor(
            out[:, :], rhs_scr[:, :], qsq[:, :], OP.add),
            deps=[ev9] + war["rot" + ("q" if is_q else "k")])
        war["scr"] = [ev1, ev4, ev10]
        return ev10

    qT_evs, kT_evs, v_evs = [], [], []
    with c.scope():
        ps_q = c.ps([128, 512])
        ps_k = c.ps([128, 512])
        ps_v = c.ps([128, 512])
        ps_tr = c.ps([128, 128], BF16)
        war = {"q": [], "k": [], "v": [], "tr": [], "scr": [], "rotq": [],
               "rotk": []}
        v2e_3d = v2e[:, :].rearrange("p (t c) -> p t c", c=65)
        ev_vo = [e.op("vector", lambda v: v.memset(v2e_3d[:, :, 64:65], 1.0))]
        ev_vo.append(e.op("vector", lambda v: v.tensor_copy(
            v2e_3d[:, 0:32, 64:65],
            pm_sb[:, 0:1].unsqueeze(1).broadcast_to([128, 32, 1])),
            deps=ev_vo + ld_c))
        for t0 in range(1, NTE):
            own = t0 >= 5
            sides = [("k", ps_k, 512), ("v", ps_v, 1024)]
            if own:
                sides = [("q", ps_q, 0)] + sides
            mm_evs = {}
            for nm, ps, col0 in sides:
                ev = None
                for k in range(NT):
                    if k < 4:
                        lo = t0 * 128
                    else:
                        lo = t0 * 128 - 1
                    ev = e.op("tensor", lambda t, k=k, ps=ps, col0=col0, lo=lo: t.matmul(
                        ps[:, :], xin[k][:, lo:lo + 128], wq_sb[k][:, col0:col0 + 512],
                        start=(k == 0), stop=(k == NT - 1)),
                        deps=ln_evs + wev + war[nm])
                mm_evs[nm] = ev
            b0v = (t0 - 1) * 520
            ev_v = None
            for h8 in range(8):
                ev_v = e.op("vector", lambda v, b0=b0v, h=h8: v.tensor_copy(
                    v2e[:, b0 + h * 65:b0 + h * 65 + 64],
                    ps_v[:, h * 64:(h + 1) * 64]),
                    deps=([mm_evs["v"]] + ev_vo + v_evs[-1:]) if h8 == 0 else [ev_v])
            war["v"] = [ev_v]
            v_evs.append(ev_v)
            ev_k = emit_side(t0, ps_k, tkc, tks, ksb, False, [mm_evs["k"]], war)
            war["k"] = [ev_k]
            if DBG_L1 and t0 == NTE - 1:
                dd = [e.dma("gpsimd", out=dbg_qf[:, :], in_=qf[:, :], deps=[ev_k]),
                      e.dma("gpsimd", out=dbg_qn[:, :], in_=qn[:, :], deps=[ev_k]),
                      e.dma("gpsimd", out=dbg_ns[:, :], in_=nsum2[:, :], deps=[ev_k])]
                war["scr"] = war["scr"] + dd
            if own:
                ev_q = emit_side(t0, ps_q, tqc, tqs, qsb, True, [mm_evs["q"]], war)
                war["q"] = [ev_q]
            else:
                ev_q = None
            tr_evs = []
            for p in range(4):
                ev_t = e.op("tensor", lambda t, p=p: t.transpose(
                    ps_tr[:, :], qrot[1][:, p * 128:(p + 1) * 128], ident[:, :]),
                    deps=[ev_k, ld_c[0]] + war["tr"])
                ev_te = e.op("scalar", lambda sc, p=p, t0=t0: sc.copy(
                    kT[p][:, (t0 - 1) * 128:t0 * 128], ps_tr[:, :]), deps=[ev_t])
                war["tr"] = [ev_te]
                tr_evs.append(ev_te)
                if own:
                    ev_t2 = e.op("tensor", lambda t, p=p: t.transpose(
                        ps_tr[:, :], qrot[0][:, p * 128:(p + 1) * 128], ident[:, :]),
                        deps=[ev_q, ld_c[0]] + war["tr"])
                    ev_te2 = e.op("scalar", lambda sc, p=p, t0=t0: sc.copy(
                        qT[p][:, (t0 - 5) * 128:(t0 - 4) * 128], ps_tr[:, :]),
                        deps=[ev_t2])
                    war["tr"] = [ev_te2]
                    tr_evs.append(ev_te2)
            war["rotk"] = tr_evs
            war["rotq"] = tr_evs
            kT_evs += tr_evs
        pp_done = kT_evs + v_evs

        ps_y = c.ps([128, 512])
        ps_b = c.ps([128, 512])
        awar = {"ps0": [], "ps1": [], "psy": [], "psb": [], "p0": [], "p1": [],
                "ysc": [], "ysc2": [], "rrow": [], "o2": []}
        at_evs = []
        for p in range(4):
            for hh in range(2):
                hb = hh * 64
                for w in range(4):
                    q0 = w * 512
                    ev_pv = None
                    for j in range(8):
                        kcol = (w * 4 + j) * 128
                        bank = j % 2
                        msk = pms[j] if j < 4 else dms[j - 4]
                        ev_s = e.op("tensor", lambda t, kcol=kcol, q0=q0, hb=hb, p=p, bank=bank: t.matmul(
                            [ps_q, ps_k][bank][:, :],
                            kT[p][hb:hb + 64, kcol:kcol + 128],
                            qT[p][hb:hb + 64, q0:q0 + 512], start=True, stop=True),
                            deps=pp_done + awar[f"ps{bank}"] + awar[f"p{bank}"])
                        ev_p = e.op("scalar", lambda s, bank=bank: s.activation(
                            pstr[bank][:, :], [ps_q, ps_k][bank][:, :], AF.Exp,
                            bias=nb48[:, 0:1], scale=8.0), deps=[ev_s] + ev_con)
                        awar[f"ps{bank}"] = [ev_p]
                        ev_mk = e.op("vector", lambda v, msk=msk, bank=bank: v.tensor_tensor(
                            pstr2[bank][:, :], pstr[bank][:, :], msk[:, :], OP.mult),
                            deps=[ev_p] + ld_c + awar[f"p{bank}"])
                        vsl = v2e[:, (w * 4 + j) * 520 + hh * 65 + p * 130:
                                  (w * 4 + j) * 520 + hh * 65 + p * 130 + 65]
                        ev_pv = e.op("tensor", lambda t, vsl=vsl, bank=bank, first=(j == 0), last=(j == 7): t.matmul(
                            ps_y[0:65, :], vsl, pstr2[bank][:, :],
                            start=first, stop=last),
                            deps=[ev_mk] + v_evs + (awar["psy"] if j == 0 else []))
                        awar[f"p{bank}"] = [ev_pv]
                    ev_r = e.op("vector", lambda v: v.reciprocal(
                        rrow[0:1, :], ps_y[64:65, :]), deps=[ev_pv] + awar["rrow"])
                    ev_b = e.op("tensor", lambda t: t.matmul(
                        ps_b[0:64, :], ones64[0:1, :], rrow[0:1, :],
                        start=True, stop=True), deps=[ev_r] + ev_con + awar["psb"])
                    ev_ye = e.op("scalar", lambda s: s.copy(
                        ysc[:, :], ps_y[0:64, :]), deps=[ev_pv] + awar["ysc"])
                    awar["psy"] = [ev_ye]
                    ev_n = e.op("vector", lambda v: v.tensor_tensor(
                        ysc2[:, :], ysc[:, :], ps_b[0:64, :], OP.mult),
                        deps=[ev_ye, ev_b] + awar["ysc2"])
                    awar["rrow"] = [ev_b]
                    awar["psb"] = [ev_n]
                    awar["ysc"] = [ev_n]
                    ev_g = e.op("vector", lambda v, p=p, hb=hb, q0=q0: v.tensor_copy(
                        o2[p][hb:hb + 64, q0:q0 + 512], ysc2[:, :]), deps=[ev_n])
                    awar["ysc2"] = [ev_g]
                    at_evs.append(ev_g)

    dbg_evs = []
    if DBG_L1:
        for p in range(4):
            dbg_evs.append(e.dma("gpsimd", out=dbg_o2[p * 128:(p + 1) * 128, :],
                                 in_=o2[p][:, :], deps=at_evs))
        dbg_evs.append(e.dma("gpsimd", out=dbg_qT[:, :], in_=qT[0][:, :], deps=at_evs))
        dbg_evs.append(e.dma("gpsimd", out=dbg_kT[:, :], in_=kT[0][:, :], deps=at_evs))
        dbg_evs.append(e.dma("gpsimd", out=dbg_v[:, :], in_=v2e[:, :], deps=at_evs))
        dbg_evs.append(e.dma("gpsimd", out=dbg_qr[:, :], in_=qrot[1][:, :], deps=at_evs))
    wo_sb = [qT[i] for i in range(4)]
    woev = [e.dma("gpsimd", out=wo_sb[i][:, 0:D], in_=wo[i * 128:(i + 1) * 128, :],
                  deps=at_evs + dbg_evs) for i in range(4)]
    lnf = qf
    prev3 = []
    with c.scope():
        pb = [c.ps([128, 512]) for _ in range(2)]
        pb_war = {0: at_evs, 1: at_evs}
        tmp = [c.sb([128, 512], F32) for _ in range(2)]
        tmp_war = {0: [], 1: []}
        for ci in range(HALF // 512):
            c0 = ci * 512
            ld3 = [e.dma("sync", out=xc[i][:, :512],
                         in_=xT[i * 128:(i + 1) * 128, O0 + c0:O0 + c0 + 512],
                         deps=(ln_evs if apply_ln else []) + prev3) for i in range(NT)]
            ch3 = []

            def consume(m, evf, c0=c0, ch3=ch3, ld3=None):
                bank = m % 2
                ev = None
                for k in range(4):
                    ev = e.op("tensor", lambda t, k=k, m=m, c0=c0, bank=bank: t.matmul(
                        pb[bank][:, :], wo_sb[k][:, m * 128:(m + 1) * 128],
                        o2[k][:, c0:c0 + 512],
                        start=(k == 0), stop=(k == 3)),
                        deps=at_evs + woev + pb_war[bank])
                tb = tmp[bank]
                src = lnf[:, :] if evf is not None else xc[m][:, :512]
                ev_1 = e.op("vector", lambda v, tb=tb, bank=bank, src=src: v.tensor_tensor(
                    tb[:, :], src, pb[bank][:, :], OP.add),
                    deps=([evf] if evf is not None else (ld3 or [])) + [ev] + tmp_war[bank])
                ev_st = e.dma("sync", out=ymT[m * 128:(m + 1) * 128, c0:c0 + 512],
                              in_=tb[:, :], deps=[ev_1])
                pb_war[bank] = [ev_1]
                tmp_war[bank] = [ev_st]
                ch3 += [ev_1, ev_st]
                return [ev_1]

            if apply_ln:
                emit_layernorm_fm(c, xc, 512, None, deps=ld3 + at_evs,
                                  shared_f32=lnf, consume_fn=consume)
            else:
                for m in range(NT):
                    consume(m, None, ld3=[ld3[m]])
            prev3 = ch3
    return c.finish()


def build_l1mlp(last_layer):
    c = Ctx(f"l1mlp{int(last_layer)}")
    e = c.e
    e.serialize = SER_MLP
    HID = 2730
    NF = 22
    xmT = c.inp("xmT", [D, 1 + HALF])
    if last_layer:
        xoT = c.inp("xoT", [D, HALF])
    lg_col = c.inp("lg_col", [D, 1])
    lb_col = c.inp("lb_col", [D, 1])
    w1a = c.inp("w1a", [D, HID])
    w1g = c.inp("w1g", [D, HID])
    b1a_col = c.inp("b1a_col", [HID, 1])
    b1g_col = c.inp("b1g_col", [HID, 1])
    w2 = c.inp("w2", [HID, D])
    b2_col = c.inp("b2_col", [D, 1])
    yT = c.out("yT", [D, HALF])

    xm = [c.sb([128, 1 + HALF], F32, f"xm{i}") for i in range(NT)]
    mln = [c.sb([128, HALF], BF16, f"mln{i}") for i in range(NT)]
    lgc = [c.sb([128, 1], F32) for _ in range(NT)]
    lbc = [c.sb([128, 1], F32) for _ in range(NT)]
    b2c = [c.sb([128, 1], F32) for _ in range(NT)]
    b1ac = [c.sb([128, 1], F32) for _ in range(NF)]
    b1gc = [c.sb([128, 1], F32) for _ in range(NF)]
    w1ap = [[c.sb([128, 128], BF16) for _ in range(NT)] for _ in range(2)]
    w1gp = [[c.sb([128, 128], BF16) for _ in range(NT)] for _ in range(2)]
    w2p = [c.sb([128, D], BF16) for _ in range(2)]
    mf = [c.sb([128, 512], BF16, f"mf{j}") for j in range(NF)]
    gsc = [c.sb([128, 512], BF16) for _ in range(2)]
    asc = [c.sb([128, 512], BF16) for _ in range(2)]
    xo_c = [c.sb([128, 512], F32) for _ in range(2)]
    tmp = [c.sb([128, 512], F32) for _ in range(2)]

    ld = []
    for i in range(NT):
        ld.append(e.dma("sync", out=xm[i][:, :], in_=xmT[i * 128:(i + 1) * 128, :]))
        ld.append(e.dma("sync", out=lgc[i][:, :], in_=lg_col[i * 128:(i + 1) * 128, :]))
        ld.append(e.dma("sync", out=lbc[i][:, :], in_=lb_col[i * 128:(i + 1) * 128, :]))
        ld.append(e.dma("sync", out=b2c[i][:, :], in_=b2_col[i * 128:(i + 1) * 128, :]))
    for j in range(NF):
        r = min(128, HID - j * 128)
        ld.append(e.dma("sync", out=b1ac[j][0:r, :], in_=b1a_col[j * 128:j * 128 + r, :]))
        ld.append(e.dma("sync", out=b1gc[j][0:r, :], in_=b1g_col[j * 128:j * 128 + r, :]))

    xv = [xm[i][:, 1:1 + HALF] if i < 4 else xm[i][:, 0:HALF] for i in range(NT)]
    ln_evs = emit_layernorm_fm(c, xv, HALF, mln, deps=ld, ln_gamma=lgc, ln_beta=lbc)

    with c.scope():
        pa = c.ps([128, 512])
        pg = c.ps([128, 512])
        po = [c.ps([128, 512]) for _ in range(4)]
        pa_war, pg_war = [], []
        po_war = {i: [] for i in range(4)}
        w1_war = {0: [], 1: []}
        w2_war = {0: [], 1: []}
        tmp_war = {0: [], 1: []}
        xo_war = {0: [], 1: []}
        mf_war = []
        for ci in range(HALF // 512):
            c0 = ci * 512
            g_evs = []
            for f1 in range(NF):
                r = min(128, HID - f1 * 128)
                wp = f1 % 2
                wevA = [e.dma("gpsimd", out=w1ap[wp][k][:, 0:r],
                              in_=w1a[k * 128:(k + 1) * 128, f1 * 128:f1 * 128 + r],
                              deps=w1_war[wp]) for k in range(NT)]
                wevG = [e.dma("gpsimd", out=w1gp[wp][k][:, 0:r],
                              in_=w1g[k * 128:(k + 1) * 128, f1 * 128:f1 * 128 + r],
                              deps=w1_war[wp]) for k in range(NT)]
                evA = evG = None
                for k in range(NT):
                    evA = e.op("tensor", lambda t, k=k, c0=c0, wp=wp, r=r: t.matmul(
                        pa[0:r, :], w1ap[wp][k][:, 0:r], mln[k][:, c0:c0 + 512],
                        start=(k == 0), stop=(k == NT - 1)),
                        deps=ln_evs + wevA + pa_war)
                for k in range(NT):
                    evG = e.op("tensor", lambda t, k=k, c0=c0, wp=wp, r=r: t.matmul(
                        pg[0:r, :], w1gp[wp][k][:, 0:r], mln[k][:, c0:c0 + 512],
                        start=(k == 0), stop=(k == NT - 1)),
                        deps=ln_evs + wevG + pg_war)
                w1_war[wp] = [evA, evG]
                sb_g = gsc[f1 % 2]
                sb_a = asc[f1 % 2]
                ev_g = e.op("scalar", lambda s, f1=f1, r=r, sb_g=sb_g: s.activation(
                    sb_g[0:r, :], pg[0:r, :], AF.Gelu, bias=b1gc[f1][0:r, 0:1]),
                    deps=[evG] + ld + g_evs[-2:-1])
                pg_war = [ev_g]
                ev_a = e.op("vector", lambda v, f1=f1, r=r, sb_a=sb_a: v.tensor_scalar_add(
                    sb_a[0:r, :], pa[0:r, :], b1ac[f1][0:r, 0:1]),
                    deps=[evA] + ld + g_evs[-2:-1])
                pa_war = [ev_a]
                ev_m = e.op("vector", lambda v, f1=f1, r=r, sb_a=sb_a, sb_g=sb_g: v.tensor_tensor(
                    mf[f1][0:r, :], sb_a[0:r, :], sb_g[0:r, :], OP.mult),
                    deps=[ev_g, ev_a] + mf_war)
                g_evs.append(ev_m)
            mf_war = []
            for grp in range(2):
                ms = list(range(grp * 4, grp * 4 + 4))
                ev_k = {m: None for m in ms}
                for k in range(NF):
                    r = min(128, HID - k * 128)
                    wp = k % 2
                    wev2 = [e.dma("gpsimd", out=w2p[wp][0:r, :],
                                  in_=w2[k * 128:k * 128 + r, :],
                                  deps=w2_war[wp])]
                    for m in ms:
                        bank = m - grp * 4
                        ev_k[m] = e.op("tensor", lambda t, k=k, m=m, r=r, wp=wp, bank=bank: t.matmul(
                            po[bank][:, :], w2p[wp][0:r, m * 128:(m + 1) * 128],
                            mf[k][0:r, :], start=(k == 0), stop=(k == NF - 1)),
                            deps=g_evs + wev2 + po_war[bank])
                    w2_war[wp] = [ev_k[m] for m in ms]
                    mf_war.append(ev_k[ms[-1]])
                for m in ms:
                    bank = m - grp * 4
                    tb = tmp[m % 2]
                    ev_1 = e.op("vector", lambda v, m=m, c0=c0, tb=tb, bank=bank: v.scalar_tensor_tensor(
                        tb[:, :], po[bank][:, :], b2c[m][:, 0:1],
                        xm[m][:, 1 + c0:1 + c0 + 512], OP.add, OP.add),
                        deps=[ev_k[m]] + ld + tmp_war[m % 2])
                    if last_layer:
                        xb = xo_c[m % 2]
                        ev_x = e.dma("sync", out=xb[:, :],
                                     in_=xoT[m * 128:(m + 1) * 128, c0:c0 + 512],
                                     deps=xo_war[m % 2])
                        ev_1 = e.op("vector", lambda v, tb=tb, xb=xb: v.tensor_tensor(
                            tb[:, :], tb[:, :], xb[:, :], OP.add), deps=[ev_1, ev_x])
                        xo_war[m % 2] = [ev_1]
                    ev_st = e.dma("sync", out=yT[m * 128:(m + 1) * 128, c0:c0 + 512],
                                  in_=tb[:, :], deps=[ev_1])
                    po_war[bank] = [ev_1]
                    tmp_war[m % 2] = [ev_st]
    return c.finish()


def _l1ffn_tables(h):
    n, DIM_HEAD, WINDOW = T, 64, 512
    half_idx = np.arange(0, DIM_HEAD, 2, dtype=np.float64)
    inv_freq = 1.0 / 10000.0 ** (half_idx / DIM_HEAD)
    g0 = h * HALF - O0
    pos = np.arange(g0, g0 + TE, dtype=np.float64)
    freqs = pos[:, None] * inv_freq[None, :]
    freqs = np.concatenate([freqs, freqs], axis=-1)
    base = (half_idx + 0.4 * DIM_HEAD) / (1.4 * DIM_HEAD)
    power = (pos - n // 2) / float(WINDOW // 2)
    scale = base[None, :] ** power[:, None]
    scale = np.concatenate([scale, scale], axis=-1)
    cos, sin = np.cos(freqs), np.sin(freqs)
    tqc = (cos * scale).astype(np.float32)
    tqs = (sin * scale).astype(np.float32)
    tkc = (cos / scale).astype(np.float32)
    tks = (sin / scale).astype(np.float32)
    if h == 0:
        tqc[:O0] = tqs[:O0] = tkc[:O0] = tks[:O0] = 0.0
    def lay(tab):
        t3 = tab.reshape(NTE, 128, 64)
        return np.ascontiguousarray(t3.transpose(1, 0, 2).reshape(128, NTE * 64))
    return lay(tqc), lay(tqs), lay(tkc), lay(tks)


_L1_MASKS = None


def _l1_masks():
    global _L1_MASKS
    if _L1_MASKS is None:
        kk = np.arange(512)[:, None]
        r = np.arange(512)[None, :]
        pmask = (kk >= r).astype(np.float32)
        dmask = (kk <= r).astype(np.float32)
        _L1_MASKS = (pmask, dmask, np.zeros((512, 512), np.float32))
    return _L1_MASKS


def run_stage68(x_full, inp):
    nc_a1 = _get_prog("l1attn1", lambda: build_l1attn(True))
    nc_a0 = _get_prog("l1attn0", lambda: build_l1attn(False))
    nc_m0 = _get_prog("l1mlp0", lambda: build_l1mlp(False))
    nc_m1 = _get_prog("l1mlp1", lambda: build_l1mlp(True))
    pmask, dmask, zmask = _l1_masks()
    ident = np.eye(128, dtype=np.float32)
    x_cur = x_full
    HID = 2730
    for li in range(2):
        wqkv = np.ascontiguousarray(inp["L1_ffn_wqkv"][li])
        wo = np.ascontiguousarray(inp["L1_ffn_wo"][li])
        qs = np.ascontiguousarray(np.asarray(inp["L1_ffn_q_scale"][li]).reshape(1, 64))
        ks = np.ascontiguousarray(np.asarray(inp["L1_ffn_k_scale"][li]).reshape(1, 64))
        lg = np.ascontiguousarray(np.asarray(inp["L1_ffn_ln_g"][li]).reshape(D, 1))
        lb = np.ascontiguousarray(np.asarray(inp["L1_ffn_ln_b"][li]).reshape(D, 1))
        w1 = np.asarray(inp["L1_ffn_w1"][li])
        b1 = np.asarray(inp["L1_ffn_b1"][li])
        w2 = np.ascontiguousarray(np.asarray(inp["L1_ffn_w2"][li]))
        b2 = np.ascontiguousarray(np.asarray(inp["L1_ffn_b2"][li]).reshape(D, 1))
        w1a = np.ascontiguousarray(w1[:, :HID])
        w1g = np.ascontiguousarray(w1[:, HID:])
        b1a = np.ascontiguousarray(b1[:HID].reshape(HID, 1))
        b1g = np.ascontiguousarray(b1[HID:].reshape(HID, 1))
        in_maps = []
        for core in range(N_CORES):
            b, h = core // 2, core % 2
            own = x_cur[b, h * HALF:(h + 1) * HALF, :]
            if h == 0:
                ptail = np.zeros((O0, D), np.float32)
                pmc = np.zeros((128, 1), np.float32)
            else:
                ptail = x_cur[b, HALF - O0:HALF, :]
                pmc = np.ones((128, 1), np.float32)
            xT = np.ascontiguousarray(
                np.concatenate([ptail, own], axis=0).T.astype(np.float32))
            tqc, tqs, tkc, tks = _l1ffn_tables(h)
            in_maps.append(dict(xT=xT, wqkv=wqkv, wo=wo, qs_row=qs, ks_row=ks,
                                tqc=tqc, tqs=tqs, tkc=tkc, tks=tks,
                                pmask=pmask, dmask=dmask, pm_col=pmc,
                                ident_in=ident))
        nc_a = nc_a1 if li == 0 else nc_a0
        res = run_bass_kernel_spmd(nc_a, in_maps, core_ids=list(range(N_CORES)))
        xm = np.empty_like(x_full)
        for core in range(N_CORES):
            b, h = core // 2, core % 2
            xm[b, h * HALF:(h + 1) * HALF, :] = res.results[core]["ymT"].T
        in_maps = []
        for core in range(N_CORES):
            b, h = core // 2, core % 2
            own = xm[b, h * HALF:(h + 1) * HALF, :]
            halo = (np.zeros((1, D), np.float32) if h == 0
                    else xm[b, HALF - 1:HALF, :])
            xmT = np.ascontiguousarray(np.concatenate([halo, own], axis=0).T)
            m = dict(xmT=xmT, lg_col=lg, lb_col=lb, w1a=w1a, w1g=w1g,
                     b1a_col=b1a, b1g_col=b1g, w2=w2, b2_col=b2)
            if li == 1:
                m["xoT"] = np.ascontiguousarray(
                    x_full[b, h * HALF:(h + 1) * HALF, :].T)
            in_maps.append(m)
        nc_m = nc_m1 if li == 1 else nc_m0
        res = run_bass_kernel_spmd(nc_m, in_maps, core_ids=list(range(N_CORES)))
        x_new = np.empty_like(x_full)
        for core in range(N_CORES):
            b, h = core // 2, core % 2
            x_new[b, h * HALF:(h + 1) * HALF, :] = res.results[core]["yT"].T
        x_cur = x_new
    return x_cur


_ONES_ROW = {}


def ones_row_g(c):
    if c.name not in _ONES_ROW:
        t = c.sb([1, 128], F32, "ones_row_g")
        c.e.op("vector", lambda v: v.memset(t[:, :], 1.0))
        _ONES_ROW[c.name] = t
    return _ONES_ROW[c.name]



_PROG_CACHE = {}


def _get_prog(name, builder):
    if name not in _PROG_CACHE:
        _PROG_CACHE[name] = builder()
    return _PROG_CACHE[name]


def run_stage2(x_full, wr, wk, wv, mu_r, mu_k):
    nc = _get_prog("stage2", build_stage2)
    in_maps = []
    wrT = np.ascontiguousarray(wr.T)
    wkT = np.ascontiguousarray(wk.T)
    wvT = np.ascontiguousarray(wv.T)
    murc = np.ascontiguousarray(mu_r.reshape(D, 1))
    mukc = np.ascontiguousarray(mu_k.reshape(D, 1))
    for core in range(N_CORES):
        b, h = core // 2, core % 2
        xT = np.ascontiguousarray(x_full[b, h * HALF:(h + 1) * HALF, :].T)
        if h == 0:
            prevT = np.zeros((D, 1), np.float32)
        else:
            prevT = np.ascontiguousarray(x_full[b, h * HALF - 1, :].reshape(D, 1))
        in_maps.append(dict(xT=xT, prevT=prevT, wrT=wrT, wkT=wkT, wvT=wvT,
                            mur=murc, muk=mukc))
    res = run_bass_kernel_spmd(nc, in_maps, core_ids=list(range(N_CORES)))
    out = np.empty_like(x_full)
    for core in range(N_CORES):
        b, h = core // 2, core % 2
        out[b, h * HALF:(h + 1) * HALF, :] = res.results[core]["yT"].T
    return out


def run_stage1(x_full, wq, wk, wv, wpost):
    nc = _get_prog("stage1", build_stage1)
    in_maps = []
    for core in range(N_CORES):
        b, h = core // 2, core % 2
        own = x_full[b, h * HALF:(h + 1) * HALF, :]
        other = x_full[b, (1 - h) * HALF:(2 - h) * HALF, :]
        xT = np.ascontiguousarray(np.concatenate([own, other], axis=0).T)
        in_maps.append(dict(xT=xT, wq=wq, wk=wk, wv=wv, wpost=wpost))
    res = run_bass_kernel_spmd(nc, in_maps, core_ids=list(range(N_CORES)))
    out = np.empty_like(x_full)
    for core in range(N_CORES):
        b, h = core // 2, core % 2
        out[b, h * HALF:(h + 1) * HALF, :] = res.results[core]["yT"].T
    return out


def kernel(**inputs):
    g = {k: np.ascontiguousarray(np.asarray(v, dtype=np.float32))
         for k, v in inputs.items()}
    x = g["L0_attn_a_X"]
    x = run_stage1(x, g["L0_attn_a_Wq"], g["L0_attn_a_Wk"], g["L0_attn_a_Wv"],
                   g["L0_attn_a_Wpost"])
    x = run_stage2(x, g["L0_ffn_W_r"], g["L0_ffn_W_k"], g["L0_ffn_W_v"],
                   g["L0_ffn_mu_r"], g["L0_ffn_mu_k"])
    x = run_stage3(x, g)
    x = run_stage2(x, g["L0_ffn_W_r"], g["L0_ffn_W_k"], g["L0_ffn_W_v"],
                   g["L0_ffn_mu_r"], g["L0_ffn_mu_k"])
    x = run_stage5(x, g["L1_attn_a_W"])
    x = run_stage68(x, g)
    x = run_stage7(x, g["L1_attn_b_Wq"], g["L1_attn_b_Wk"], g["L1_attn_b_Wv"],
                   g["L1_attn_b_Wo"])
    x = run_stage68(x, g)
    return x


# version 07 (final)
# speedup vs baseline: 1.6097x; 1.6090x over previous
import numpy as np
from contextlib import ExitStack

import concourse.bass as bass
import concourse.mybir as mybir
from concourse.bass_utils import run_bass_kernel_spmd

F32 = mybir.dt.float32
BF16 = mybir.dt.bfloat16
AF = mybir.ActivationFunctionType
OP = mybir.AluOpType

N_CORES = 8
B, T, D = 4, 4096, 1024
HALF = T // 2
NT = D // 128

ENGINES = ("tensor", "vector", "scalar", "gpsimd", "sync")
DMA_QUEUES = ("sync", "gpsimd", "scalar")


class Ev:
    __slots__ = ("kind", "name", "count")

    def __init__(self, kind, name, count):
        self.kind, self.name, self.count = kind, name, count


class Emitter:

    NSEM = 8

    def __init__(self, nc):
        self.nc = nc
        self.ops = []
        self.serialize = False
        self.ecount = {e: 0 for e in ENGINES}
        self.dcount = {(q, i): 0 for q in DMA_QUEUES for i in range(self.NSEM)}
        self.didx = {q: 0 for q in DMA_QUEUES}

    def op(self, engine, fn, deps=()):
        if self.serialize:
            deps = list(self._flat(deps)) + self.barrier()
        deps = list(self._flat(deps))
        if engine != "tensor" and self.ecount[engine] > 0:
            deps.append(Ev("e", engine, self.ecount[engine]))
        self.ecount[engine] += 1
        ev = Ev("e", engine, self.ecount[engine])
        self.ops.append((engine, False, fn, tuple(deps), ev))
        return ev

    def dma(self, queue, out, in_, deps=()):
        if self.serialize:
            deps = list(self._flat(deps)) + self.barrier()
        slot = self.didx[queue] % self.NSEM
        self.didx[queue] += 1
        prev = self.dcount[(queue, slot)]
        self.dcount[(queue, slot)] = prev + 16
        ev = Ev("d", (queue, slot), prev + 16)
        pre = Ev("d", (queue, slot), prev) if prev else None

        def fn(eng, out=out, in_=in_):
            return eng.dma_start(out=out, in_=in_)

        deps = list(self._flat(deps))
        if pre is not None:
            deps.append(pre)
        self.ops.append((queue, True, fn, tuple(deps), ev))
        return ev

    @staticmethod
    def _flat(deps):
        out = []
        for d in deps:
            if d is None:
                continue
            if isinstance(d, Ev):
                out.append(d)
            else:
                out.extend(x for x in d if x is not None)
        return tuple(out)

    def barrier(self):
        evs = [Ev("e", e, c) for e, c in self.ecount.items() if c > 0]
        evs += [Ev("d", qs, c) for qs, c in self.dcount.items() if c > 0]
        return evs

    def finalize(self):
        nc = self.nc
        with ExitStack() as ctx:
            esem = {e: ctx.enter_context(nc.semaphore(f"es_{e}")) for e in ENGINES}
            dsem = {}
            for q in DMA_QUEUES:
                for i in range(self.NSEM):
                    dsem[(q, i)] = ctx.enter_context(nc.semaphore(f"ds_{q}{i}"))
            block = ctx.enter_context(nc.Block())
            wmarks = {}
            runs = []
            for item in self.ops:
                if runs and runs[-1][0] == item[0]:
                    runs[-1][1].append(item)
                else:
                    runs.append((item[0], [item]))
            for engine, items in runs:
                section = getattr(block, engine)

                def body(eng, items=items, engine=engine):
                    wm = wmarks.setdefault(engine, {})
                    for _, is_dma, fn, deps, ev in items:
                        best = {}
                        for d in deps:
                            key = (d.kind, d.name)
                            if best.get(key, -1) < d.count:
                                best[key] = d.count
                        for (kind, name), count in best.items():
                            if wm.get((kind, name), -1) >= count:
                                continue
                            wm[(kind, name)] = count
                            sem = esem[name] if kind == "e" else dsem[name]
                            eng.wait_ge(sem, count)
                        ins = fn(eng)
                        if ev is not None:
                            if ev.kind == "e":
                                ins.then_inc(esem[ev.name], 1)
                            else:
                                ins.then_inc(dsem[ev.name], 16)

                section(body)

    def wait_tail(self):
        evs = self.barrier()

        def fn(eng):
            return eng.nop()

        self.ops.append(("sync", False, fn, tuple(evs), None))


def cdiv(a, b):
    return -(-a // b)



class Ctx:
    def __init__(self, name):
        self.nc = bass.Bass("TRN2", target_bir_lowering=False, debug=False)
        self.e = Emitter(self.nc)
        self.stack = ExitStack()
        self.name = name
        self._uid = 0

    def uid(self):
        self._uid += 1
        return self._uid

    def scope(self):
        from contextlib import contextmanager

        @contextmanager
        def _s(self=self):
            st = ExitStack()
            old = self.stack
            self.stack = st
            try:
                yield
            finally:
                self.stack = old
                st.close()
        return _s()

    def sb(self, shape, dtype, name=None):
        name = name or f"sb{self.uid()}"
        return self.stack.enter_context(self.nc.sbuf_tensor(name, shape, dtype))

    def ps(self, shape, dtype=F32, name=None):
        name = name or f"ps{self.uid()}"
        return self.stack.enter_context(self.nc.psum_tensor(name, shape, dtype))

    def inp(self, name, shape, dtype=F32):
        return self.nc.dram_tensor(name, shape, dtype, kind="ExternalInput").ap()

    def out(self, name, shape, dtype=F32):
        return self.nc.dram_tensor(name, shape, dtype, kind="ExternalOutput").ap()

    def finish(self):
        self.e.wait_tail()
        self.e.finalize()
        self.stack.close()
        return self.nc



def load_const_tiles(c, hbm_ap, rows, cols, dtype=BF16, queue="gpsimd", name=None):
    tiles, evs = [], []
    for i in range(cdiv(rows, 128)):
        r = min(128, rows - i * 128)
        t = c.sb([r, cols], dtype, name and f"{name}{i}")
        ev = c.e.dma(queue, out=t[:, :], in_=hbm_ap[i * 128:i * 128 + r, :])
        tiles.append(t)
        evs.append(ev)
    return tiles, evs


def emit_layernorm_fm(c, x_tiles, cols, out_bf, deps, out_f32=None,
                      gamma=None, eps=1e-5, dmodel=1024, rms=False, saveAB=None,
                      col_off=0, shared_f32=None, consume_fn=None,
                      rms1_gamma=None, rms1_eps=1e-6, ln_gamma=None,
                      ln_beta=None):
    e, nc = c.e, c.nc
    n_tiles = len(x_tiles)
    if not hasattr(c, "_ln_scratch"):
        ones_col = c.sb([128, 1], F32, "ln_ones_col")
        ones_row = c.sb([1, 128], F32, "ln_ones_row")
        ev_oc = e.op("vector", lambda v: v.memset(ones_col[:, :], 1.0))
        ev_or = e.op("vector", lambda v: v.memset(ones_row[:, :], 1.0))
        c._ln_scratch = dict(
            ones_col=ones_col, ones_row=ones_row, ev_oc=ev_oc, ev_or=ev_or,
            sq=[c.sb([128, 512], F32, f"ln_sq{i}") for i in range(2)],
            r_sq=c.sb([1, 512], F32, "ln_rsq"),
            r_mean=c.sb([1, 512], F32, "ln_rmean"),
            r_tmp=c.sb([1, 512], F32, "ln_rtmp"),
            r_A=c.sb([1, 512], F32, "ln_rA"),
            r_B=c.sb([1, 512], F32, "ln_rB"))
    L = c._ln_scratch
    ones_col, ones_row = L["ones_col"], L["ones_row"]
    ev_oc, ev_or = L["ev_oc"], L["ev_or"]
    sq = L["sq"]
    r_sq, r_mean, r_tmp = L["r_sq"], L["r_mean"], L["r_tmp"]
    r_A, r_B = L["r_A"], L["r_B"]
    psstack = ExitStack()
    _oldstack, c.stack = c.stack, psstack
    ps_sum = c.ps([1, 512])
    ps_sq = c.ps([1, 512])
    ps_A = c.ps([128, 512])
    ps_B = c.ps([128, 512])
    c.stack = _oldstack
    done = []
    prev_chunk = []
    for ci, c0 in enumerate(list(range(0, cols, 512))):
        w = min(512, cols - c0)
        if not rms:
            ev_s = None
            for i, xt in enumerate(x_tiles):
                ev_s = e.op("tensor", lambda t, xt=xt, i=i, c0=c0, w=w: t.matmul(
                    ps_sum[0:1, :w], ones_col[:, :], xt[:, c0:c0 + w],
                    start=(i == 0), stop=(i == n_tiles - 1)),
                    deps=list(deps) + [ev_oc] + prev_chunk)
        ev_q = None
        sq_evs = []
        for i, xt in enumerate(x_tiles):
            buf = sq[i % 2]
            ev_w = e.op("scalar", lambda s, xt=xt, buf=buf, c0=c0, w=w: s.activation(
                buf[:, :w], xt[:, c0:c0 + w], AF.Square),
                deps=list(deps) + prev_chunk + sq_evs[-2:-1])
            ev_q = e.op("tensor", lambda t, buf=buf, i=i, w=w: t.matmul(
                ps_sq[0:1, :w], ones_col[:, :], buf[:, :w],
                start=(i == 0), stop=(i == n_tiles - 1)),
                deps=[ev_w, ev_oc] + prev_chunk)
            sq_evs.append(ev_q)
        if rms:
            def rowf(v, w=w):
                return v.tensor_scalar(r_tmp[0:1, :w], ps_sq[0:1, :w], 1.0 / dmodel,
                                       eps, OP.mult, OP.add)
            ev_r1 = e.op("vector", rowf, deps=[ev_q])
        else:
            ev_a1 = e.op("vector", lambda v, w=w: v.tensor_scalar_mul(
                r_mean[0:1, :w], ps_sum[0:1, :w], 1.0 / dmodel), deps=[ev_s, ev_q])
            ev_a2 = e.op("vector", lambda v, w=w: v.tensor_tensor(
                r_tmp[0:1, :w], r_mean[0:1, :w], r_mean[0:1, :w], OP.mult),
                deps=[ev_a1])
            ev_a3 = e.op("vector", lambda v, w=w: v.tensor_scalar(
                r_sq[0:1, :w], ps_sq[0:1, :w], 1.0 / dmodel, eps, OP.mult, OP.add),
                deps=[ev_a2])
            ev_r1 = e.op("vector", lambda v, w=w: v.tensor_tensor(
                r_tmp[0:1, :w], r_sq[0:1, :w], r_tmp[0:1, :w], OP.subtract),
                deps=[ev_a3])
        ev_r2 = e.op("scalar", lambda s, w=w: s.activation(
            r_tmp[0:1, :w], r_tmp[0:1, :w], AF.Sqrt), deps=[ev_r1])

        ev_r3 = e.op("vector", lambda v, w=w: v.reciprocal(
            r_A[0:1, :w], r_tmp[0:1, :w]), deps=[ev_r2])
        if not rms:
            ev_r3 = e.op("vector", lambda v, w=w: v.scalar_tensor_tensor(
                r_B[0:1, :w], r_mean[0:1, :w], -1.0, r_A[0:1, :w],
                OP.mult, OP.mult), deps=[ev_r3])
        if rms1_gamma is not None:
            ev_h0 = e.op("vector", lambda v, w=w: v.tensor_tensor(
                r_tmp[0:1, :w], r_A[0:1, :w], r_A[0:1, :w], OP.mult), deps=[ev_r3])
            ev_h1 = e.op("vector", lambda v, w=w: v.tensor_scalar(
                r_tmp[0:1, :w], r_tmp[0:1, :w], -eps, 1.0 + rms1_eps,
                OP.mult, OP.add), deps=[ev_h0])
            ev_h2 = e.op("scalar", lambda s, w=w: s.activation(
                r_tmp[0:1, :w], r_tmp[0:1, :w], AF.Sqrt), deps=[ev_h1])

            ev_i0 = e.op("vector", lambda v, w=w: v.reciprocal(
                r_sq[0:1, :w], r_tmp[0:1, :w]), deps=[ev_h2])
            ev_i1 = e.op("vector", lambda v, w=w: v.tensor_tensor(
                r_A[0:1, :w], r_A[0:1, :w], r_sq[0:1, :w], OP.mult), deps=[ev_i0])
            ev_r3 = e.op("vector", lambda v, w=w: v.tensor_tensor(
                r_B[0:1, :w], r_B[0:1, :w], r_sq[0:1, :w], OP.mult), deps=[ev_i1])
        ev_bA = e.op("tensor", lambda t, w=w: t.matmul(
            ps_A[:, :w], ones_row[0:1, :], r_A[0:1, :w], start=True, stop=True),
            deps=[ev_r3, ev_or])
        if not rms:
            ev_bB = e.op("tensor", lambda t, w=w: t.matmul(
                ps_B[:, :w], ones_row[0:1, :], r_B[0:1, :w], start=True, stop=True),
                deps=[ev_r3, ev_or])
        if saveAB is not None:
            rowA_s, rowB_s = saveAB
            ev_sv = e.op("vector", lambda v, w=w, o=col_off + c0: v.tensor_copy(
                rowA_s[0:1, o:o + w], r_A[0:1, :w]), deps=[ev_r3])
            if not rms:
                ev_sv2 = e.op("vector", lambda v, w=w, o=col_off + c0: v.tensor_copy(
                    rowB_s[0:1, o:o + w], r_B[0:1, :w]), deps=[ev_r3])
        chunk_evs = []
        for i, xt in enumerate(x_tiles):
            buf = sq[i % 2]
            if rms:
                if gamma is not None:
                    ev1 = e.op("vector", lambda v, xt=xt, buf=buf, c0=c0, w=w: v.tensor_tensor(
                        buf[:, :w], xt[:, c0:c0 + w], ps_A[:, :w], OP.mult),
                        deps=[ev_bA] + sq_evs)
                    ev2 = e.op("vector", lambda v, i=i, buf=buf, o=col_off + c0, w=w: v.tensor_scalar_mul(
                        out_bf[i][:, o:o + w], buf[:, :w], gamma[i][:, 0:1]),
                        deps=[ev1])
                else:
                    ev2 = e.op("vector", lambda v, xt=xt, i=i, c0=c0, o=col_off + c0, w=w: v.tensor_tensor(
                        out_bf[i][:, o:o + w], xt[:, c0:c0 + w], ps_A[:, :w], OP.mult),
                        deps=[ev_bA] + sq_evs)
                chunk_evs.append(ev2)
            else:
                ev1 = e.op("vector", lambda v, xt=xt, buf=buf, c0=c0, w=w: v.tensor_tensor(
                    buf[:, :w], xt[:, c0:c0 + w], ps_A[:, :w], OP.mult),
                    deps=[ev_bA] + sq_evs)
                if shared_f32 is not None:
                    evf = e.op("vector", lambda v, buf=buf, w=w: v.tensor_tensor(
                        shared_f32[:, :w], buf[:, :w], ps_B[:, :w], OP.add),
                        deps=[ev_bB, ev1] + getattr(c, "_lnsh_war", []))
                    c._lnsh_war = list(consume_fn(i, evf))
                    ev2 = evf
                elif out_f32 is not None:
                    evf = e.op("vector", lambda v, i=i, buf=buf, o=col_off + c0, w=w: v.tensor_tensor(
                        out_f32[i][:, o:o + w], buf[:, :w], ps_B[:, :w], OP.add),
                        deps=[ev_bB, ev1])
                    if out_bf is None:
                        ev2 = evf
                    else:
                        ev2 = e.op("vector", lambda v, i=i, o=col_off + c0, w=w: v.tensor_copy(
                            out_bf[i][:, o:o + w], out_f32[i][:, o:o + w]),
                            deps=[evf])
                elif rms1_gamma is not None:
                    ev15 = e.op("vector", lambda v, buf=buf, w=w: v.tensor_tensor(
                        buf[:, :w], buf[:, :w], ps_B[:, :w], OP.add),
                        deps=[ev_bB, ev1])
                    ev2 = e.op("vector", lambda v, i=i, buf=buf, o=col_off + c0, w=w: v.tensor_scalar_mul(
                        out_bf[i][:, o:o + w], buf[:, :w], rms1_gamma[i][:, 0:1]),
                        deps=[ev15])
                elif ln_gamma is not None:
                    ev15 = e.op("vector", lambda v, buf=buf, w=w: v.tensor_tensor(
                        buf[:, :w], buf[:, :w], ps_B[:, :w], OP.add),
                        deps=[ev_bB, ev1])
                    ev2 = e.op("vector", lambda v, i=i, buf=buf, o=col_off + c0, w=w: v.scalar_tensor_tensor(
                        out_bf[i][:, o:o + w], buf[:, :w], ln_gamma[i][:, 0:1],
                        ln_beta[i][:, 0:1].broadcast_to([128, w]),
                        OP.mult, OP.add), deps=[ev15])
                else:
                    ev2 = e.op("vector", lambda v, i=i, buf=buf, o=col_off + c0, w=w: v.tensor_tensor(
                        out_bf[i][:, o:o + w], buf[:, :w], ps_B[:, :w], OP.add),
                        deps=[ev_bB, ev1])
                chunk_evs.append(ev2)
        prev_chunk = chunk_evs
        done.extend(chunk_evs)
    psstack.close()
    return done


def emit_proj_fm(c, w_tiles, w_evs, xin_bf, cols, out_psum_fn, m_tiles, deps,
                 n_k=8, nw=512):
    e = c.e
    out = {}
    for m in m_tiles:
        for ci, c0 in enumerate(range(0, cols, nw)):
            w_ = min(nw, cols - c0)
            pb = out_psum_fn(m, ci)
            ev = None
            for k in range(n_k):
                ev = e.op("tensor", lambda t, k=k, m=m, c0=c0, w_=w_, pb=pb: t.matmul(
                    pb[:, :w_], w_tiles(k, m), xin_bf[k][:, c0:c0 + w_],
                    start=(k == 0), stop=(k == n_k - 1)),
                    deps=list(deps) + list(w_evs))
            out[(m, ci)] = ev
    return out



def emit_stage2_body(c, x, xT_hbm, yT_hbm, wr_sb, wk_sb, wv_sb, wev,
                     mur_sb, muk_sb, x_deps, cols=HALF):
    e = c.e
    EC = 1 + cols
    xln = [c.sb([128, EC], BF16, f"s2xln{i}") for i in range(NT)]
    ln_done = emit_layernorm_fm(c, x, EC, xln, deps=x_deps)

    nchunk = cols // 512
    xr = [c.sb([128, 512], BF16) for _ in range(NT)]
    xk = [c.sb([128, 512], BF16) for _ in range(NT)]
    rb = [c.sb([128, 512], BF16) for _ in range(NT)]
    ka = [c.sb([128, 512], BF16) for _ in range(NT)]
    dsc = [c.sb([128, 512], BF16) for _ in range(2)]
    ksc = [c.sb([128, 512], F32) for _ in range(2)]
    tmp = [c.sb([128, 512], F32) for _ in range(2)]
    st_all = []
    with c.scope():
        pb = [c.ps([128, 512]) for _ in range(4)]
        pb_war = {i: [] for i in range(4)}
        buf_war = {0: []}
        for ci in range(nchunk):
            c0 = ci * 512
            p = 0
            xrp, xkp, rbp, kap = xr, xk, rb, ka
            mix_done = []
            for i in range(NT):
                cur = xln[i][:, 1 + c0:1 + c0 + 512]
                prv = xln[i][:, c0:c0 + 512]
                dv = dsc[p]
                ev_d = e.op("vector", lambda v, cur=cur, prv=prv, dv=dv: v.tensor_tensor(
                    dv[:, :], cur, prv, OP.subtract),
                    deps=ln_done + buf_war[p])
                ev_r = e.op("vector", lambda v, i=i, prv=prv, dv=dv: v.scalar_tensor_tensor(
                    xrp[i][:, :], dv[:, :], mur_sb[i][:, 0:1], prv, OP.mult, OP.add),
                    deps=[ev_d])
                ev_k = e.op("vector", lambda v, i=i, prv=prv, dv=dv: v.scalar_tensor_tensor(
                    xkp[i][:, :], dv[:, :], muk_sb[i][:, 0:1], prv, OP.mult, OP.add),
                    deps=[ev_d])
                mix_done += [ev_r, ev_k]
            rk_done = []
            for m in range(NT):
                bank = m % 2
                pbank = pb[bank]
                ev = None
                for k in range(NT):
                    ev = e.op("tensor", lambda t, k=k, m=m, pbank=pbank: t.matmul(
                        pbank[:, :], wr_sb[k][:, m * 128:(m + 1) * 128], xrp[k][:, :],
                        start=(k == 0), stop=(k == NT - 1)),
                        deps=mix_done + wev + pb_war[bank])
                ev_e = e.op("scalar", lambda s, m=m, pbank=pbank: s.activation(
                    rbp[m][:, :], pbank[:, :], AF.Sigmoid), deps=[ev])
                pb_war[bank] = [ev_e]
                bank2 = 2 + m % 2
                pbank2 = pb[bank2]
                ev = None
                for k in range(NT):
                    ev = e.op("tensor", lambda t, k=k, m=m, pbank2=pbank2: t.matmul(
                        pbank2[:, :], wk_sb[k][:, m * 128:(m + 1) * 128], xkp[k][:, :],
                        start=(k == 0), stop=(k == NT - 1)),
                        deps=mix_done + wev + pb_war[bank2])
                kscb = ksc[m % 2]
                ev_r1 = e.op("vector", lambda v, pbank2=pbank2, kscb=kscb: v.tensor_scalar_max(
                    kscb[:, :], pbank2[:, :], 0.0),
                    deps=[ev] + ([rk_done[-2]] if len(rk_done) >= 2 else []))
                pb_war[bank2] = [ev_r1]
                ev_sq = e.op("scalar", lambda s, m=m, kscb=kscb: s.activation(
                    kap[m][:, :], kscb[:, :], AF.Square), deps=[ev_r1])
                rk_done.append(ev_sq)
            ch_done = []
            for m in range(NT):
                bank = m % 2
                pbank = pb[bank]
                ev = None
                for k in range(NT):
                    ev = e.op("tensor", lambda t, k=k, m=m, pbank=pbank: t.matmul(
                        pbank[:, :], wv_sb[k][:, m * 128:(m + 1) * 128], kap[k][:, :],
                        start=(k == 0), stop=(k == NT - 1)),
                        deps=rk_done + pb_war[bank])
                tb = tmp[m % 2]
                ev_m = e.op("vector", lambda v, m=m, pbank=pbank, tb=tb: v.tensor_tensor(
                    tb[:, :], pbank[:, :], rbp[m][:, :], OP.mult),
                    deps=[ev] + ([ch_done[-2]] if len(ch_done) >= 2 else []))
                pb_war[bank] = [ev_m]
                ev_a = e.op("vector", lambda v, m=m, c0=c0, tb=tb: v.tensor_tensor(
                    tb[:, :], tb[:, :], x[m][:, 1 + c0:1 + c0 + 512], OP.add),
                    deps=[ev_m])
                ev_st = e.dma("sync", out=yT_hbm[m * 128:(m + 1) * 128, c0:c0 + 512],
                              in_=tb[:, :], deps=[ev_a])
                ch_done.append(ev_st)
            buf_war[p] = ch_done
            st_all += ch_done
    return st_all


def build_stage2():
    c = Ctx("stage2")
    e = c.e
    xT = c.inp("xT", [D, HALF])
    prevT = c.inp("prevT", [D, 1])
    wrT = c.inp("wrT", [D, D])
    wkT = c.inp("wkT", [D, D])
    wvT = c.inp("wvT", [D, D])
    mur = c.inp("mur", [D, 1])
    muk = c.inp("muk", [D, 1])
    yT = c.out("yT", [D, HALF])

    x = [c.sb([128, 1 + HALF], F32, f"x{i}") for i in range(NT)]
    mur_sb = [c.sb([128, 1], F32) for _ in range(NT)]
    muk_sb = [c.sb([128, 1], F32) for _ in range(NT)]
    wr_sb = [c.sb([128, D], BF16, f"wr{i}") for i in range(NT)]
    wk_sb = [c.sb([128, D], BF16, f"wk{i}") for i in range(NT)]
    wv_sb = [c.sb([128, D], BF16, f"wv{i}") for i in range(NT)]

    ld = []
    for i in range(NT):
        ld.append(e.dma("sync", out=x[i][:, 1:1 + HALF], in_=xT[i * 128:(i + 1) * 128, :]))
        ld.append(e.dma("sync", out=x[i][:, 0:1], in_=prevT[i * 128:(i + 1) * 128, :]))
        ld.append(e.dma("sync", out=mur_sb[i][:, :], in_=mur[i * 128:(i + 1) * 128, :]))
        ld.append(e.dma("sync", out=muk_sb[i][:, :], in_=muk[i * 128:(i + 1) * 128, :]))
    wev = []
    for i in range(NT):
        wev.append(e.dma("gpsimd", out=wr_sb[i][:, :], in_=wrT[i * 128:(i + 1) * 128, :]))
        wev.append(e.dma("gpsimd", out=wk_sb[i][:, :], in_=wkT[i * 128:(i + 1) * 128, :]))
        wev.append(e.dma("gpsimd", out=wv_sb[i][:, :], in_=wvT[i * 128:(i + 1) * 128, :]))

    emit_stage2_body(c, x, xT, yT, wr_sb, wk_sb, wv_sb, wev, mur_sb, muk_sb, ld)
    return c.finish()



def emit_full_attention(c, xln, NQ, NK, wq_hbm, wk_hbm, wv_hbm, a2_bf,
                        xln_deps, bound, apply_gelu, qk_scale=0.125,
                        wq_col=None, wk_col=None, wv_col=None):
    e = c.e
    wq_col = wq_col or (lambda hp: hp * 128)
    wk_col = wk_col or (lambda hp: hp * 128)
    wv_col = wv_col or (lambda hp: hp * 128)
    k2 = c.sb([128, NK], BF16, "fa_k2")
    q2 = c.sb([128, NQ], BF16, "fa_q2")
    v2e = c.sb([128, (NK // 128) * 130], BF16, "fa_v2e")
    wqp = [[c.sb([128, 128], BF16) for _ in range(NT)] for _ in range(2)]
    wkp = [[c.sb([128, 128], BF16) for _ in range(NT)] for _ in range(2)]
    wvp = [[c.sb([128, 128], BF16) for _ in range(NT)] for _ in range(2)]
    pstr = [c.sb([128, 512], BF16) for _ in range(4)]
    ysc = c.sb([64, 512], F32, "fa_ysc")
    ysc2 = c.sb([64, 512], F32, "fa_ysc2")
    rrow = c.sb([1, 512], F32, "fa_rrow")
    ones64 = c.sb([1, 64], F32, "fa_ones64")
    ev_o = e.op("vector", lambda v: v.memset(ones64[:, :], 1.0))
    nbias = c.sb([128, 1], F32, "fa_nbias")
    ev_nb = e.op("vector", lambda v: v.memset(nbias[:, :], -float(bound)))
    nkt = NK // 128
    v2e_3d = v2e[:, :].rearrange("p (t c) -> p t c", c=130)
    ones_evs = [
        e.op("vector", lambda v: v.memset(v2e_3d[:, :, 64:65], 1.0)),
        e.op("vector", lambda v: v.memset(v2e_3d[:, :, 129:130], 1.0)),
    ]
    out_evs = []
    with c.scope():
        ps_s = [c.ps([128, 512]) for _ in range(4)]
        ps_y = c.ps([128, 512], name="fa_psy")
        ps_b = c.ps([128, 512], name="fa_psb")
        war = {"k2": [], "q2": [], "v2e": ones_evs, "ps0": [], "ps1": [],
               "ps2": [], "ps3": [], "psy": [], "psb": [], "p0": [], "p1": [],
               "p2": [], "p3": [], "ysc": [], "ysc2": [],
               "rrow": [], "w0": [], "w1": []}
        for hp in range(8):
            wp = hp % 2
            wev = []
            for k in range(NT):
                wev.append(e.dma("gpsimd", out=wqp[wp][k][:, :],
                                 in_=wq_hbm[k * 128:(k + 1) * 128,
                                            wq_col(hp):wq_col(hp) + 128],
                                 deps=war[f"w{wp}"]))
                wev.append(e.dma("gpsimd", out=wkp[wp][k][:, :],
                                 in_=wk_hbm[k * 128:(k + 1) * 128,
                                            wk_col(hp):wk_col(hp) + 128],
                                 deps=war[f"w{wp}"]))
                wev.append(e.dma("gpsimd", out=wvp[wp][k][:, :],
                                 in_=wv_hbm[k * 128:(k + 1) * 128,
                                            wv_col(hp):wv_col(hp) + 128],
                                 deps=war[f"w{wp}"]))
            kq_evs = []
            kqv_mm = []
            for ci in range(NK // 512):
                c0 = ci * 512
                bank = ci % 2
                ev = None
                for k in range(NT):
                    ev = e.op("tensor", lambda t, k=k, c0=c0, wp=wp, bank=bank: t.matmul(
                        ps_s[bank][:, :], wkp[wp][k][:, :], xln[k][:, c0:c0 + 512],
                        start=(k == 0), stop=(k == NT - 1)),
                        deps=list(xln_deps) + wev + war[f"ps{bank}"])
                kqv_mm.append(ev)
                ev_e = e.op("vector", lambda v, c0=c0, bank=bank: v.tensor_copy(
                    k2[:, c0:c0 + 512], ps_s[bank][:, :]),
                    deps=[ev] + war["k2"])
                war[f"ps{bank}"] = [ev_e]
                kq_evs.append(ev_e)
            for ci in range(NQ // 512):
                c0 = ci * 512
                bank = ci % 2
                ev = None
                for k in range(NT):
                    ev = e.op("tensor", lambda t, k=k, c0=c0, wp=wp, bank=bank: t.matmul(
                        ps_s[bank][:, :], wqp[wp][k][:, :], xln[k][:, c0:c0 + 512],
                        start=(k == 0), stop=(k == NT - 1)),
                        deps=list(xln_deps) + wev + war[f"ps{bank}"])
                kqv_mm.append(ev)
                ev_e = e.op("vector", lambda v, c0=c0, bank=bank: v.tensor_copy(
                    q2[:, c0:c0 + 512], ps_s[bank][:, :]),
                    deps=[ev] + war["q2"])
                war[f"ps{bank}"] = [ev_e]
                kq_evs.append(ev_e)
            v_evs = []
            for t0 in range(nkt):
                bank = t0 % 2
                ev = None
                for k in range(NT):
                    ev = e.op("tensor", lambda t, k=k, t0=t0, wp=wp, bank=bank: t.matmul(
                        ps_s[bank][:, 0:128], xln[k][:, t0 * 128:(t0 + 1) * 128],
                        wvp[wp][k][:, :],
                        start=(k == 0), stop=(k == NT - 1)),
                        deps=list(xln_deps) + wev + war[f"ps{bank}"])
                kqv_mm.append(ev)

                b0 = t0 * 130
                ev_e1 = e.op("vector", lambda v, b0=b0, bank=bank: v.tensor_copy(
                    v2e[:, b0:b0 + 64], ps_s[bank][:, 0:64]),
                    deps=[ev] + war["v2e"])
                ev_e = e.op("vector", lambda v, b0=b0, bank=bank: v.tensor_copy(
                    v2e[:, b0 + 65:b0 + 129], ps_s[bank][:, 64:128]),
                    deps=[ev_e1])
                war[f"ps{bank}"] = [ev_e]
                v_evs.append(ev_e)
            war[f"w{wp}"] = kqv_mm
            head_evs = []
            for hh in range(2):
                hb = hh * 64
                for qc in range(NQ // 512):
                    q0 = qc * 512
                    ev_pv = None
                    for ks in range(nkt):
                        bank = ks % 4
                        ev_s = e.op("tensor", lambda t, ks=ks, q0=q0, hb=hb, bank=bank: t.matmul(
                            ps_s[bank][:, :], k2[hb:hb + 64, ks * 128:(ks + 1) * 128],
                            q2[hb:hb + 64, q0:q0 + 512],
                            start=True, stop=True),
                            deps=kq_evs + war[f"ps{bank}"] + war[f"p{bank}"])
                        ev_p = e.op("scalar", lambda s, bank=bank: s.activation(
                            pstr[bank][:, :], ps_s[bank][:, :], AF.Exp,
                            bias=nbias[:, 0:1], scale=qk_scale),
                            deps=[ev_s, ev_nb])
                        war[f"ps{bank}"] = [ev_p]
                        ev_pv = e.op("tensor", lambda t, ks=ks, hh=hh, bank=bank: t.matmul(
                            ps_y[0:65, :], v2e[:, ks * 130 + hh * 65:ks * 130 + hh * 65 + 65],
                            pstr[bank][:, :],
                            start=(ks == 0), stop=(ks == nkt - 1)),
                            deps=[ev_p] + v_evs + war["psy"])
                        war[f"p{bank}"] = [ev_pv]
                    ev_r = e.op("vector", lambda v: v.reciprocal(
                        rrow[0:1, :], ps_y[64:65, :]), deps=[ev_pv] + war["rrow"])
                    ev_b = e.op("tensor", lambda t: t.matmul(
                        ps_b[0:64, :], ones64[0:1, :], rrow[0:1, :],
                        start=True, stop=True), deps=[ev_r, ev_o] + war["psb"])
                    ev_ye = e.op("scalar", lambda s: s.copy(
                        ysc[:, :], ps_y[0:64, :]), deps=[ev_pv] + war["ysc"])
                    war["psy"] = [ev_ye]
                    ev_n = e.op("vector", lambda v: v.tensor_tensor(
                        ysc2[:, :], ysc[:, :], ps_b[0:64, :], OP.mult),
                        deps=[ev_ye, ev_b] + war["ysc2"])
                    war["rrow"] = [ev_b]
                    war["psb"] = [ev_n]
                    war["ysc"] = [ev_n]
                    tile, rb = hp, hb
                    if apply_gelu:
                        ev_g = e.op("scalar", lambda s, tile=tile, rb=rb, q0=q0: s.activation(
                            a2_bf[tile][rb:rb + 64, q0:q0 + 512], ysc2[:, :], AF.Gelu),
                            deps=[ev_n])
                    else:
                        ev_g = e.op("vector", lambda v, tile=tile, rb=rb, q0=q0: v.tensor_copy(
                            a2_bf[tile][rb:rb + 64, q0:q0 + 512], ysc2[:, :]),
                            deps=[ev_n])
                    war["ysc2"] = [ev_g]
                    head_evs.append(ev_g)
            war["k2"] = head_evs
            war["q2"] = head_evs
            war["v2e"] = head_evs
            out_evs.extend(head_evs)
    return out_evs


def build_stage1():
    c = Ctx("stage1")
    e = c.e
    xT = c.inp("xT", [D, T])
    wq = c.inp("wq", [D, D])
    wk = c.inp("wk", [D, D])
    wv = c.inp("wv", [D, D])
    wpost = c.inp("wpost", [D, D])
    yT = c.out("yT", [D, HALF])

    xln = [c.sb([128, T], BF16, f"xln{i}") for i in range(NT)]
    a2 = [c.sb([128, HALF], BF16, f"a2_{i}") for i in range(NT)]
    xc = [c.sb([128, 512], F32) for _ in range(NT)]

    ln_evs = []
    prev = []
    for ci in range(T // 512):
        c0 = ci * 512
        ld = [e.dma("sync", out=xc[i][:, :], in_=xT[i * 128:(i + 1) * 128, c0:c0 + 512],
                    deps=prev) for i in range(NT)]
        evs = emit_layernorm_fm(c, xc, 512, [t for t in xln], deps=ld, col_off=c0)
        prev = evs
        ln_evs.extend(evs)

    at_evs = emit_full_attention(c, xln, HALF, T, wq, wk, wv,
                                 a2, ln_evs, bound=16.0, apply_gelu=True)

    wp_sb = [c.sb([128, D], BF16, f"wp{i}") for i in range(NT)]
    wpev = [e.dma("gpsimd", out=wp_sb[i][:, :], in_=wpost[i * 128:(i + 1) * 128, :])
            for i in range(NT)]
    lnf = c.sb([128, 512], F32, "lnf")
    prev3 = []
    with c.scope():
        pb = [c.ps([128, 512]) for _ in range(2)]
        pb_war = {0: at_evs, 1: at_evs}
        tmp = [c.sb([128, 512], F32) for _ in range(2)]
        tmp_war = {0: [], 1: []}
        st3 = {"prev": []}

        for ci in range(HALF // 512):
            c0 = ci * 512
            ld3 = [e.dma("sync", out=xc[i][:, :],
                         in_=xT[i * 128:(i + 1) * 128, c0:c0 + 512],
                         deps=ln_evs + prev3) for i in range(NT)]
            ch3 = []

            def consume(m, evf, c0=c0, ch3=ch3, ld3=ld3):
                bank = m % 2
                ev = None
                for k in range(NT):
                    ev = e.op("tensor", lambda t, k=k, m=m, c0=c0, bank=bank: t.matmul(
                        pb[bank][:, :], wp_sb[k][:, m * 128:(m + 1) * 128],
                        a2[k][:, c0:c0 + 512],
                        start=(k == 0), stop=(k == NT - 1)),
                        deps=at_evs + wpev + pb_war[bank])
                tb = tmp[bank]
                ev_1 = e.op("vector", lambda v, m=m, tb=tb: v.tensor_tensor(
                    tb[:, :], xc[m][:, :], lnf[:, :], OP.add),
                    deps=[evf] + tmp_war[bank])
                ev_2 = e.op("vector", lambda v, tb=tb, bank=bank: v.tensor_tensor(
                    tb[:, :], tb[:, :], pb[bank][:, :], OP.add), deps=[ev_1, ev])
                ev_st = e.dma("sync", out=yT[m * 128:(m + 1) * 128, c0:c0 + 512],
                              in_=tb[:, :], deps=[ev_2])
                pb_war[bank] = [ev_2]
                tmp_war[bank] = [ev_st]
                ch3.append(ev_2)
                ch3.append(ev_st)
                return [ev_1]

            emit_layernorm_fm(c, xc, 512, None, deps=ld3 + at_evs,
                              shared_f32=lnf, consume_fn=consume)
            prev3 = ch3
    return c.finish()


def build_stage5():
    c = Ctx("stage5")
    e = c.e
    xT = c.inp("xT", [D, T])
    wqs = c.inp("wqs", [D, D])
    wks = c.inp("wks", [D, D])
    wvs = c.inp("wvs", [D, D])
    yT = c.out("yT", [D, HALF])

    xln = [c.sb([128, T], BF16, f"xln{i}") for i in range(NT)]
    a2 = [c.sb([128, HALF], F32, f"a2_{i}") for i in range(NT)]
    xc = [c.sb([128, 512], F32) for _ in range(NT)]

    ln_evs = []
    prev = []
    for ci in range(T // 512):
        c0 = ci * 512
        ld = [e.dma("sync", out=xc[i][:, :], in_=xT[i * 128:(i + 1) * 128, c0:c0 + 512],
                    deps=prev) for i in range(NT)]
        evs = emit_layernorm_fm(c, xc, 512, xln, deps=ld, col_off=c0)
        prev = evs
        ln_evs.extend(evs)

    at_evs = emit_full_attention(c, xln, HALF, T, wqs, wks, wvs,
                                 a2, ln_evs, bound=24.0, apply_gelu=False)

    prev3 = []
    tmp = [c.sb([128, 512], F32) for _ in range(2)]
    tmp_war = {0: [], 1: []}
    for ci in range(HALF // 512):
        c0 = ci * 512
        ch3 = []
        for m in range(NT):
            bank = m % 2
            xb = xc[m]
            ev_x = e.dma("sync", out=xb[:, :],
                         in_=xT[m * 128:(m + 1) * 128, c0:c0 + 512],
                         deps=ln_evs + prev3)
            tb = tmp[bank]
            ev_1 = e.op("vector", lambda v, m=m, c0=c0, xb=xb, tb=tb: v.tensor_tensor(
                tb[:, :], xb[:, :], a2[m][:, c0:c0 + 512], OP.add),
                deps=[ev_x] + at_evs + tmp_war[bank])
            ev_st = e.dma("sync", out=yT[m * 128:(m + 1) * 128, c0:c0 + 512],
                          in_=tb[:, :], deps=[ev_1])
            tmp_war[bank] = [ev_st]
            ch3 += [ev_1, ev_st]
        prev3 = ch3
    return c.finish()


def run_stage5(x_full, W):
    nc = _get_prog("stage5", build_stage5)
    Wr = np.ascontiguousarray(W.reshape(D, 16, 192))
    wqs = np.ascontiguousarray(Wr[:, :, 0:64].reshape(D, 1024))
    wks = np.ascontiguousarray(Wr[:, :, 64:128].reshape(D, 1024))
    wvs = np.ascontiguousarray(Wr[:, :, 128:192].reshape(D, 1024))
    in_maps = []
    for core in range(N_CORES):
        b, h = core // 2, core % 2
        own = x_full[b, h * HALF:(h + 1) * HALF, :]
        other = x_full[b, (1 - h) * HALF:(2 - h) * HALF, :]
        xT = np.ascontiguousarray(np.concatenate([own, other], axis=0).T)
        in_maps.append(dict(xT=xT, wqs=wqs, wks=wks, wvs=wvs))
    res = run_bass_kernel_spmd(nc, in_maps, core_ids=list(range(N_CORES)))
    out = np.empty_like(x_full)
    for core in range(N_CORES):
        b, h = core // 2, core % 2
        out[b, h * HALF:(h + 1) * HALF, :] = res.results[core]["yT"].T
    return out


def build_stage7():
    c = Ctx("stage7")
    e = c.e
    Q0 = HALF
    xT = c.inp("xT", [D, T])
    wq = c.inp("wq", [D, 256])
    wk = c.inp("wk", [D, 256])
    wv = c.inp("wv", [D, D])
    wo = c.inp("wo", [D, D])
    pmcol = c.inp("pmcol", [128, 1])
    dmask = c.inp("dmask", [512, 512])
    hmask = c.inp("hmask", [512, 512])
    yT = c.out("yT", [D, HALF])

    SQI = 1.0 / np.sqrt(32.0)
    SQB = float(np.sqrt(2.0) / 2.0)

    xln = [c.sb([128, T], BF16, f"xln{i}") for i in range(NT)]
    a2 = [c.sb([128, HALF], BF16, f"a2_{i}") for i in range(NT)]
    xc = [c.sb([128, 512], F32) for _ in range(NT)]
    wq_sb = [c.sb([128, 256], BF16) for _ in range(NT)]
    wk_sb = [c.sb([128, 256], BF16) for _ in range(NT)]
    wv_sb = [c.sb([128, D], BF16, f"wv{i}") for i in range(NT)]
    qa = [c.sb([16, HALF], BF16) for _ in range(2)]
    ka = [c.sb([16, T], BF16) for _ in range(2)]
    v2e = c.sb([128, (T // 128) * 130], BF16, "t7_v2e")
    pstr = [c.sb([128, 512], BF16) for _ in range(2)]
    half_c = c.sb([128, 512], BF16, "half_c")
    dm = [c.sb([128, 512], BF16) for _ in range(4)]
    hm = [c.sb([128, 512], BF16) for _ in range(4)]
    pm_sb = c.sb([128, 1], F32)
    ysc = c.sb([64, 512], F32)
    ysc2 = c.sb([64, 512], F32)
    rrow = c.sb([1, 512], F32)
    zrow = c.sb([1, 512], F32)
    ones64 = c.sb([1, 64], F32)
    sqb_col = c.sb([128, 1], F32)
    ev_con = [
        e.op("vector", lambda v: v.memset(half_c[:, :], 0.5)),
        e.op("vector", lambda v: v.memset(ones64[:, :], 1.0)),
        e.op("vector", lambda v: v.memset(sqb_col[:, :], SQB)),
    ]
    ld_c = [e.dma("sync", out=pm_sb[:, :], in_=pmcol[:, :])]
    for j in range(4):
        ld_c.append(e.dma("gpsimd", out=dm[j][:, :], in_=dmask[j * 128:(j + 1) * 128, :]))
        ld_c.append(e.dma("gpsimd", out=hm[j][:, :], in_=hmask[j * 128:(j + 1) * 128, :]))
    wev = []
    for i in range(NT):
        wev.append(e.dma("gpsimd", out=wq_sb[i][:, :], in_=wq[i * 128:(i + 1) * 128, :]))
        wev.append(e.dma("gpsimd", out=wk_sb[i][:, :], in_=wk[i * 128:(i + 1) * 128, :]))
        wev.append(e.dma("gpsimd", out=wv_sb[i][:, :], in_=wv[i * 128:(i + 1) * 128, :]))

    ln_evs = []
    prev = []
    for ci in range(T // 512):
        c0 = ci * 512
        ld = [e.dma("sync", out=xc[i][:, :], in_=xT[i * 128:(i + 1) * 128, c0:c0 + 512],
                    deps=prev) for i in range(NT)]
        evs = emit_layernorm_fm(c, xc, 512, xln, deps=ld, col_off=c0)
        prev = evs
        ln_evs.extend(evs)

    NKT = T // 128
    with c.scope():
        ps_s = [c.ps([128, 512]) for _ in range(2)]
        ps_y = c.ps([128, 512], name="t7_psy")
        ps_b = c.ps([128, 512], name="t7_psb")
        war = {"ps0": [], "ps1": [], "psy": [], "psb": [], "p0": [], "p1": [],
               "qk": [], "v2e": [], "ysc": [], "ysc2": [], "rrow": []}
        out_evs = []
        for hp in range(8):
            kq_evs = []
            for hh in range(2):
                h = hp * 2 + hh
                for ci in range(HALF // 512):
                    c0 = ci * 512
                    bank = ci % 2
                    ev = None
                    for k in range(NT):
                        ev = e.op("tensor", lambda t, k=k, h=h, c0=c0, bank=bank: t.matmul(
                            ps_s[bank][0:16, :], wq_sb[k][:, h * 16:h * 16 + 16],
                            xln[k][:, Q0 + c0:Q0 + c0 + 512],
                            start=(k == 0), stop=(k == NT - 1)),
                            deps=ln_evs + wev + war[f"ps{bank}"])
                    ev_e = e.op("vector", lambda v, hh=hh, c0=c0, bank=bank: v.tensor_copy(
                        qa[hh][:, c0:c0 + 512], ps_s[bank][0:16, :]),
                        deps=[ev] + war["qk"])
                    war[f"ps{bank}"] = [ev_e]
                    kq_evs.append(ev_e)
                for ci in range(T // 512):
                    c0 = ci * 512
                    bank = ci % 2
                    ev = None
                    for k in range(NT):
                        ev = e.op("tensor", lambda t, k=k, h=h, c0=c0, bank=bank: t.matmul(
                            ps_s[bank][0:16, :], wk_sb[k][:, h * 16:h * 16 + 16],
                            xln[k][:, c0:c0 + 512],
                            start=(k == 0), stop=(k == NT - 1)),
                            deps=ln_evs + wev + war[f"ps{bank}"])
                    ev_e = e.op("vector", lambda v, hh=hh, c0=c0, bank=bank: v.tensor_copy(
                        ka[hh][:, c0:c0 + 512], ps_s[bank][0:16, :]),
                        deps=[ev] + war["qk"])
                    war[f"ps{bank}"] = [ev_e]
                    kq_evs.append(ev_e)
            v_evs = []
            for t0 in range(NKT):
                bank = t0 % 2
                ev = None
                for k in range(NT):
                    ev = e.op("tensor", lambda t, k=k, t0=t0, hp=hp, bank=bank: t.matmul(
                        ps_s[bank][:, 0:128], xln[k][:, t0 * 128:(t0 + 1) * 128],
                        wv_sb[k][:, hp * 128:(hp + 1) * 128],
                        start=(k == 0), stop=(k == NT - 1)),
                        deps=ln_evs + wev + war[f"ps{bank}"])

                b0 = t0 * 130
                ev_e1 = e.op("vector", lambda v, b0=b0, bank=bank: v.tensor_copy(
                    v2e[:, b0:b0 + 64], ps_s[bank][:, 0:64]),
                    deps=[ev] + war["v2e"])
                ev_e = e.op("vector", lambda v, b0=b0, bank=bank: v.tensor_copy(
                    v2e[:, b0 + 65:b0 + 129], ps_s[bank][:, 64:128]),
                    deps=[ev_e1])
                if t0 < NKT // 2:
                    ev_o1 = e.op("vector", lambda v, b0=b0: v.tensor_copy(
                        v2e[:, b0 + 64:b0 + 65], pm_sb[:, 0:1]),
                        deps=[ev_e] + ld_c)
                    ev_o = e.op("vector", lambda v, b0=b0: v.tensor_copy(
                        v2e[:, b0 + 129:b0 + 130], pm_sb[:, 0:1]), deps=[ev_o1])
                else:
                    ev_o1 = e.op("vector", lambda v, b0=b0: v.memset(
                        v2e[:, b0 + 64:b0 + 65], 1.0), deps=[ev_e] + ld_c)
                    ev_o = e.op("vector", lambda v, b0=b0: v.memset(
                        v2e[:, b0 + 129:b0 + 130], 1.0), deps=[ev_o1])
                war[f"ps{bank}"] = [ev_e]
                v_evs.append(ev_o)
                v_evs.append(ev_e)
            head_evs = []
            for hh in range(2):
                qh, kh = qa[hh], ka[hh]
                for qc in range(HALF // 512):
                    q0 = qc * 512
                    nfull = (HALF // 128) + qc * 4
                    strips = list(range(nfull + 4))
                    ev_pv = None
                    for ks in strips:
                        bank = ks % 2
                        diag = ks >= nfull
                        ev_s = e.op("tensor", lambda t, ks=ks, q0=q0, qh=qh, kh=kh, bank=bank: t.matmul(
                            ps_s[bank][:, :], kh[:, ks * 128:(ks + 1) * 128],
                            qh[:, q0:q0 + 512],
                            start=True, stop=True),
                            deps=kq_evs + war[f"ps{bank}"] + war[f"p{bank}"])
                        ev_p = e.op("scalar", lambda s, bank=bank: s.activation(
                            pstr[bank][:, :], ps_s[bank][:, :], AF.Square,
                            bias=sqb_col[:, 0:1], scale=SQI),
                            deps=[ev_s] + ev_con)
                        war[f"ps{bank}"] = [ev_p]
                        if diag:
                            j = ks - nfull
                            ev_p = e.op("vector", lambda v, j=j, bank=bank: v.tensor_tensor(
                                pstr[bank][:, :], pstr[bank][:, :], dm[j][:, :],
                                OP.mult), deps=[ev_p] + ld_c)
                            war[f"ps{bank}"][0] = ev_p
                        vsl = v2e[:, ks * 130 + hh * 65:ks * 130 + hh * 65 + 65]
                        ev_pv = e.op("tensor", lambda t, vsl=vsl, bank=bank, first=(ks == 0): t.matmul(
                            ps_y[0:65, :], vsl, pstr[bank][:, :],
                            start=first, stop=False),
                            deps=[ev_p] + v_evs + (war["psy"] if ks == 0 else []))
                        war[f"p{bank}"] = [ev_pv]
                        rhs_h = hm[ks - nfull] if diag else half_c
                        ev_pv = e.op("tensor", lambda t, vsl=vsl, rhs_h=rhs_h, last=(ks == strips[-1]): t.matmul(
                            ps_y[0:65, :], vsl, rhs_h[:, :],
                            start=False, stop=last),
                            deps=[ev_pv] + ev_con + ld_c)
                    ev_z = e.op("vector", lambda v: v.tensor_scalar_add(
                        zrow[0:1, :], ps_y[64:65, :], 1e-12), deps=[ev_pv] + war["rrow"])
                    ev_r = e.op("vector", lambda v: v.reciprocal(
                        rrow[0:1, :], zrow[0:1, :]), deps=[ev_z])
                    ev_b = e.op("tensor", lambda t: t.matmul(
                        ps_b[0:64, :], ones64[0:1, :], rrow[0:1, :],
                        start=True, stop=True), deps=[ev_r] + ev_con + war["psb"])
                    ev_ye = e.op("scalar", lambda s: s.copy(
                        ysc[:, :], ps_y[0:64, :]), deps=[ev_pv] + war["ysc"])
                    war["psy"] = [ev_ye]
                    ev_n = e.op("vector", lambda v: v.tensor_tensor(
                        ysc2[:, :], ysc[:, :], ps_b[0:64, :], OP.mult),
                        deps=[ev_ye, ev_b] + war["ysc2"])
                    war["rrow"] = [ev_b]
                    war["psb"] = [ev_n]
                    war["ysc"] = [ev_n]
                    hb = hh * 64
                    ev_g = e.op("vector", lambda v, hp=hp, hb=hb, q0=q0: v.tensor_copy(
                        a2[hp][hb:hb + 64, q0:q0 + 512], ysc2[:, :]), deps=[ev_n])
                    war["ysc2"] = [ev_g]
                    head_evs.append(ev_g)
            war["qk"] = head_evs
            war["v2e"] = head_evs
            out_evs.extend(head_evs)
    at_evs = out_evs

    wo_sb = wv_sb
    woev = [e.dma("gpsimd", out=wo_sb[i][:, :], in_=wo[i * 128:(i + 1) * 128, :],
                  deps=at_evs) for i in range(NT)]
    prev3 = []
    with c.scope():
        pb = [c.ps([128, 512]) for _ in range(2)]
        pb_war = {0: at_evs, 1: at_evs}
        tmp = [c.sb([128, 512], F32) for _ in range(2)]
        tmp_war = {0: [], 1: []}
        for ci in range(HALF // 512):
            c0 = ci * 512
            for m in range(NT):
                bank = m % 2
                ev = None
                for k in range(NT):
                    ev = e.op("tensor", lambda t, k=k, m=m, c0=c0, bank=bank: t.matmul(
                        pb[bank][:, :], wo_sb[k][:, m * 128:(m + 1) * 128],
                        a2[k][:, c0:c0 + 512],
                        start=(k == 0), stop=(k == NT - 1)),
                        deps=at_evs + woev + pb_war[bank])
                xb = xc[m]
                ev_x = e.dma("sync", out=xb[:, :],
                             in_=xT[m * 128:(m + 1) * 128, Q0 + c0:Q0 + c0 + 512],
                             deps=ln_evs + prev3)
                tb = tmp[bank]
                ev_1 = e.op("vector", lambda v, xb=xb, tb=tb, bank=bank: v.tensor_tensor(
                    tb[:, :], xb[:, :], pb[bank][:, :], OP.add),
                    deps=[ev, ev_x] + tmp_war[bank])
                ev_st = e.dma("sync", out=yT[m * 128:(m + 1) * 128, c0:c0 + 512],
                              in_=tb[:, :], deps=[ev_1])
                pb_war[bank] = [ev_1]
                tmp_war[bank] = [ev_st]
                prev3 = [ev_1, ev_st]
    return c.finish()


def run_stage7(x_full, Wq, Wk, Wv, Wo):
    nc = _get_prog("stage7", build_stage7)
    r = np.arange(512)
    kk = np.arange(512)
    dmask = (kk[:, None] <= r[None, :]).astype(np.float32)
    hmask = 0.5 * dmask
    in_maps = []
    for core in range(N_CORES):
        b, h = core // 2, core % 2
        own = x_full[b, h * HALF:(h + 1) * HALF, :]
        if h == 0:
            prev = np.zeros_like(own)
            pm = np.zeros((128, 1), np.float32)
        else:
            prev = x_full[b, 0:HALF, :]
            pm = np.ones((128, 1), np.float32)
        xT = np.ascontiguousarray(np.concatenate([prev, own], axis=0).T)
        in_maps.append(dict(xT=xT, wq=Wq, wk=Wk, wv=Wv, wo=Wo, pmcol=pm,
                            dmask=dmask, hmask=hmask))
    res = run_bass_kernel_spmd(nc, in_maps, core_ids=list(range(N_CORES)))
    out = np.empty_like(x_full)
    for core in range(N_CORES):
        b, h = core // 2, core % 2
        out[b, h * HALF:(h + 1) * HALF, :] = res.results[core]["yT"].T
    return out


def build_stage3a():
    c = Ctx("stage3a")
    e = c.e
    Q0 = HALF
    xT = c.inp("xT", [D, T])
    qkvw = c.inp("qkvw", [D, 3072])
    kvb_row = c.inp("kvb_row", [1, 2048])
    qb_col = c.inp("qb_col", [D, 1])
    kb_col = c.inp("kb_col", [D, 1])
    ob_col = c.inp("ob_col", [D, 1])
    n1_col = c.inp("n1_col", [D, 1])
    ow = c.inp("ow", [D, D])
    cosw_row = c.inp("cosw_row", [1, T])
    sinw_row = c.inp("sinw_row", [1, T])
    coswc = c.inp("coswc", [128, T // 128])
    sinwc = c.inp("sinwc", [128, T // 128])
    dmask = c.inp("dmask", [512, 512])
    hT = c.out("hT", [D, HALF])
    NKT = T // 128

    xc = [c.sb([128, 512], F32) for _ in range(NT)]
    a2 = [c.sb([128, HALF], BF16, f"a2_{i}") for i in range(NT)]
    cos_b = c.sb([128, HALF], BF16, "cos_b")
    sin_b = c.sb([128, HALF], BF16, "sin_b")
    cwc = c.sb([128, NKT], F32)
    swc = c.sb([128, NKT], F32)
    kb_r = c.sb([1, 128], BF16)
    vb_r = c.sb([1, 128], BF16)
    qb = [c.sb([128, 1], F32) for _ in range(NT)]
    obc = [c.sb([128, 1], F32) for _ in range(NT)]
    n1c = [c.sb([128, 1], F32) for _ in range(NT)]
    dm = [c.sb([128, 512], BF16) for _ in range(4)]
    ones1 = c.sb([1, 128], BF16)
    onesr = c.sb([1, 128], F32)
    ones64 = c.sb([1, 64], F32)
    crow = c.sb([1, 512], F32)
    ld_c = [e.dma("sync", out=cwc[:, :], in_=coswc[:, :]),
            e.dma("sync", out=swc[:, :], in_=sinwc[:, :])]
    kbc = [c.sb([128, 1], F32) for _ in range(NT)]
    for i in range(NT):
        ld_c.append(e.dma("sync", out=qb[i][:, :], in_=qb_col[i * 128:(i + 1) * 128, :]))
        ld_c.append(e.dma("sync", out=kbc[i][:, :], in_=kb_col[i * 128:(i + 1) * 128, :]))
        ld_c.append(e.dma("sync", out=obc[i][:, :], in_=ob_col[i * 128:(i + 1) * 128, :]))
        ld_c.append(e.dma("sync", out=n1c[i][:, :], in_=n1_col[i * 128:(i + 1) * 128, :]))
    for j in range(4):
        ld_c.append(e.dma("gpsimd", out=dm[j][:, :], in_=dmask[j * 128:(j + 1) * 128, :]))
    ev_con = [e.op("vector", lambda v: v.memset(ones1[:, :], 1.0)),
              e.op("vector", lambda v: v.memset(onesr[:, :], 1.0)),
              e.op("vector", lambda v: v.memset(ones64[:, :], 1.0))]
    with c.scope():
        ps_t = c.ps([128, 512])
        war_t = []
        for src_row, dst in ((cosw_row, cos_b), (sinw_row, sin_b)):
            for ci in range(HALF // 512):
                c0 = ci * 512
                ev_l = e.dma("sync", out=crow[0:1, :], in_=src_row[0:1, Q0 + c0:Q0 + c0 + 512],
                             deps=war_t)
                ev_m = e.op("tensor", lambda t, c0=c0: t.matmul(
                    ps_t[:, :], onesr[0:1, :], crow[0:1, :], start=True, stop=True),
                    deps=[ev_l] + ev_con + war_t)
                ev_e = e.op("vector", lambda v, c0=c0, dst=dst: v.tensor_copy(
                    dst[:, c0:c0 + 512], ps_t[:, :]), deps=[ev_m])
                war_t = [ev_e]
        tab_evs = [ev_e]

    rms1 = [c.sb([128, T], BF16, f"rms1_{i}") for i in range(NT)]
    ln_evs = []
    prev = []
    for ci in range(T // 512):
        c0 = ci * 512
        ld = [e.dma("sync", out=xc[i][:, :], in_=xT[i * 128:(i + 1) * 128, c0:c0 + 512],
                    deps=prev) for i in range(NT)]
        evs = emit_layernorm_fm(c, xc, 512, rms1, deps=ld + ld_c, col_off=c0,
                                rms1_gamma=n1c, rms1_eps=1e-6)
        prev = evs
        ln_evs.extend(evs)

    with c.scope():
        wqp = [c.sb([128, 128], BF16) for _ in range(NT)]
        wkp = [c.sb([128, 128], BF16) for _ in range(NT)]
        wvp = [c.sb([128, 128], BF16) for _ in range(NT)]
        qcp = c.sb([128, HALF], BF16, "qcp")
        qsp = c.sb([128, HALF], BF16, "qsp")
        kfc = c.sb([128, HALF], BF16, "kfc")
        kfs = c.sb([128, HALF], BF16, "kfs")
        ktc = c.sb([128, NKT * 128], BF16, "ktc")
        kts = c.sb([128, NKT * 128], BF16, "kts")
        v2e = c.sb([128, NKT * 130], BF16, "c_v2e")
        qrel = c.sb([128, 512], F32)
        sbf = [c.sb([128, 512], BF16) for _ in range(2)]
        stc_bf = c.sb([128, 65], BF16)
        sts_bf = c.sb([128, 65], BF16)
        stc_lo = c.sb([128, 65], BF16)
        sts_lo = c.sb([128, 65], BF16)
        ysc = c.sb([64, 512], F32)
        ysc2 = c.sb([64, 512], F32)
        rrow = c.sb([1, 512], F32)
        zrow = c.sb([1, 512], F32)
        ps_s = [c.ps([128, 512]) for _ in range(2)]
        ps_y = c.ps([128, 512])
        ps_b = c.ps([128, 512])
        ps_stc = c.ps([64, 65])
        ps_sts = c.ps([64, 65])
        war = {"ps0": [], "ps1": [], "psy": [], "psb": [], "s0": [], "s1": [],
               "qk": [], "v2e": [], "ysc": [], "ysc2": [], "rrow": [],
               "w": [], "qrel": [], "st": [], "stbf": []}
        v2e_3d = v2e[:, :].rearrange("p (t c) -> p t c", c=130)
        ev_vo = [e.op("vector", lambda v: v.memset(v2e_3d[:, :, 64:65], 1.0)),
                 e.op("vector", lambda v: v.memset(v2e_3d[:, :, 129:130], 1.0))]
        out_evs = []
        for hp in range(8):
            wev = [e.dma("gpsimd", out=kb_r[:, :],
                         in_=kvb_row[0:1, hp * 128:hp * 128 + 128], deps=war["w"]),
                   e.dma("gpsimd", out=vb_r[:, :],
                         in_=kvb_row[0:1, 1024 + hp * 128:1024 + hp * 128 + 128],
                         deps=war["w"])]
            for k in range(NT):
                wev.append(e.dma("gpsimd", out=wqp[k][:, :],
                                 in_=qkvw[k * 128:(k + 1) * 128, hp * 128:hp * 128 + 128],
                                 deps=war["w"]))
                wev.append(e.dma("gpsimd", out=wkp[k][:, :],
                                 in_=qkvw[k * 128:(k + 1) * 128, 1024 + hp * 128:1024 + hp * 128 + 128],
                                 deps=war["w"]))
                wev.append(e.dma("gpsimd", out=wvp[k][:, :],
                                 in_=qkvw[k * 128:(k + 1) * 128, 2048 + hp * 128:2048 + hp * 128 + 128],
                                 deps=war["w"]))
            proj_mm = []
            kq_evs = []
            for ci in range(HALF // 512):
                c0 = ci * 512
                bank = ci % 2
                ev = None
                for k in range(NT):
                    ev = e.op("tensor", lambda t, k=k, c0=c0, bank=bank: t.matmul(
                        ps_s[bank][:, :], wqp[k][:, :], rms1[k][:, Q0 + c0:Q0 + c0 + 512],
                        start=(k == 0), stop=(k == NT - 1)),
                        deps=ln_evs + wev + war[f"ps{bank}"])
                proj_mm.append(ev)
                ev_e = e.op("scalar", lambda s, hp=hp, bank=bank: s.activation(
                    qrel[:, :], ps_s[bank][:, :], AF.Relu, bias=qb[hp][:, 0:1]),
                    deps=[ev] + war["qrel"])
                war[f"ps{bank}"] = [ev_e]
                evq = []
                evq.append(e.op("vector", lambda v, c0=c0: v.tensor_tensor(
                    qcp[:, c0:c0 + 512], qrel[:, :],
                    cos_b[:, c0:c0 + 512], OP.mult),
                    deps=[ev_e] + tab_evs + war["qk"]))
                evq.append(e.op("vector", lambda v, c0=c0: v.tensor_tensor(
                    qsp[:, c0:c0 + 512], qrel[:, :],
                    sin_b[:, c0:c0 + 512], OP.mult),
                    deps=[ev_e] + tab_evs + war["qk"]))
                war["qrel"] = evq
                kq_evs += evq
            for ci in range(HALF // 512):
                c0 = ci * 512
                bank = ci % 2
                ev = None
                for k in range(NT):
                    ev = e.op("tensor", lambda t, k=k, c0=c0, bank=bank: t.matmul(
                        ps_s[bank][:, :], wkp[k][:, :], rms1[k][:, Q0 + c0:Q0 + c0 + 512],
                        start=(k == 0), stop=(k == NT - 1)),
                        deps=ln_evs + wev + war[f"ps{bank}"])
                proj_mm.append(ev)
                ev_e = e.op("scalar", lambda s, hp=hp, bank=bank: s.activation(
                    qrel[:, :], ps_s[bank][:, :], AF.Relu, bias=kbc[hp][:, 0:1]),
                    deps=[ev] + war["qrel"] + ld_c)
                war[f"ps{bank}"] = [ev_e]
                ev1 = e.op("vector", lambda v, c0=c0: v.tensor_tensor(
                    kfc[:, c0:c0 + 512], qrel[:, :], cos_b[:, c0:c0 + 512], OP.mult),
                    deps=[ev_e] + tab_evs + war["qk"])
                ev2 = e.op("vector", lambda v, c0=c0: v.tensor_tensor(
                    kfs[:, c0:c0 + 512], qrel[:, :], sin_b[:, c0:c0 + 512], OP.mult),
                    deps=[ev_e] + tab_evs + war["qk"])
                war["qrel"] = [ev1, ev2]
                kq_evs += [ev1, ev2]
            v_evs = []
            for t0 in range(NKT):
                bank = t0 % 2
                ev = None
                for k in range(NT):
                    ev = e.op("tensor", lambda t, k=k, t0=t0, bank=bank: t.matmul(
                        ps_s[bank][:, 0:128], rms1[k][:, t0 * 128:(t0 + 1) * 128],
                        wkp[k][:, :], start=(k == 0), stop=False),
                        deps=ln_evs + wev + war[f"ps{bank}"])
                ev = e.op("tensor", lambda t, bank=bank: t.matmul(
                    ps_s[bank][:, 0:128], ones1[0:1, :], kb_r[0:1, :],
                    start=False, stop=True),
                    deps=[ev] + wev + ev_con)
                proj_mm.append(ev)
                ev_e = e.op("scalar", lambda s, bank=bank: s.activation(
                    qrel[:, 0:128], ps_s[bank][:, 0:128], AF.Relu),
                    deps=[ev] + war["qrel"])
                war[f"ps{bank}"] = [ev_e]
                ev1 = e.op("vector", lambda v, t0=t0: v.tensor_scalar_mul(
                    ktc[:, t0 * 128:(t0 + 1) * 128], qrel[:, 0:128], cwc[:, t0:t0 + 1]),
                    deps=[ev_e] + ld_c + war["qk"])
                ev2 = e.op("vector", lambda v, t0=t0: v.tensor_scalar_mul(
                    kts[:, t0 * 128:(t0 + 1) * 128], qrel[:, 0:128], swc[:, t0:t0 + 1]),
                    deps=[ev_e] + ld_c + war["qk"])
                war["qrel"] = [ev1, ev2]
                kq_evs += [ev1, ev2]
                bank = (t0 + 1) % 2
                ev = None
                for k in range(NT):
                    ev = e.op("tensor", lambda t, k=k, t0=t0, bank=bank: t.matmul(
                        ps_s[bank][:, 0:128], rms1[k][:, t0 * 128:(t0 + 1) * 128],
                        wvp[k][:, :], start=(k == 0), stop=False),
                        deps=ln_evs + wev + war[f"ps{bank}"])
                ev = e.op("tensor", lambda t, bank=bank: t.matmul(
                    ps_s[bank][:, 0:128], ones1[0:1, :], vb_r[0:1, :],
                    start=False, stop=True),
                    deps=[ev] + wev + ev_con)
                proj_mm.append(ev)

                b0 = t0 * 130
                ev_e1 = e.op("vector", lambda v, b0=b0, bank=bank: v.tensor_copy(
                    v2e[:, b0:b0 + 64], ps_s[bank][:, 0:64]),
                    deps=[ev] + war["v2e"] + ev_vo)
                ev_e = e.op("vector", lambda v, b0=b0, bank=bank: v.tensor_copy(
                    v2e[:, b0 + 65:b0 + 129], ps_s[bank][:, 64:128]),
                    deps=[ev_e1])
                war[f"ps{bank}"] = [ev_e]
                v_evs.append(ev_e)
            war["w"] = proj_mm
            head_evs = []
            for hh in range(2):
                hb = hh * 64
                ev_su = None
                NPS = HALF // 128
                for t0 in range(NPS):
                    ev_su = e.op("tensor", lambda t, t0=t0, hb=hb, hh=hh, first=(t0 == 0), lst=(t0 == NPS - 1): t.matmul(
                        ps_stc[:, :], ktc[:, t0 * 128 + hb:t0 * 128 + hb + 64],
                        v2e[:, t0 * 130 + hh * 65:t0 * 130 + hh * 65 + 65],
                        start=first, stop=lst, skip_group_check=True),
                        deps=kq_evs + v_evs + war["st"])
                    ev_su = e.op("tensor", lambda t, t0=t0, hb=hb, hh=hh, first=(t0 == 0), lst=(t0 == NPS - 1): t.matmul(
                        ps_sts[:, :], kts[:, t0 * 128 + hb:t0 * 128 + hb + 64],
                        v2e[:, t0 * 130 + hh * 65:t0 * 130 + hh * 65 + 65],
                        start=first, stop=lst, skip_group_check=True),
                        deps=kq_evs + v_evs + war["st"])
                for qc in range(HALF // 512):
                    q0 = qc * 512
                    ev_sc = e.op("scalar", lambda s, hb=hb: s.copy(
                        stc_bf[hb:hb + 64, :], ps_stc[:, :]),
                        deps=[ev_su] + war["stbf"])
                    ev_ss = e.op("scalar", lambda s, hb=hb: s.copy(
                        sts_bf[hb:hb + 64, :], ps_sts[:, :]),
                        deps=[ev_su] + war["stbf"])
                    ev_sc = e.op("vector", lambda v, hb=hb: v.tensor_tensor(
                        stc_lo[hb:hb + 64, :], ps_stc[:, :], stc_bf[hb:hb + 64, :],
                        OP.subtract), deps=[ev_sc])
                    ev_ss = e.op("vector", lambda v, hb=hb: v.tensor_tensor(
                        sts_lo[hb:hb + 64, :], ps_sts[:, :], sts_bf[hb:hb + 64, :],
                        OP.subtract), deps=[ev_ss])
                    ev_pv = None
                    for j in range(4):
                        ks = (Q0 + q0) // 128 + j
                        bank = j % 2
                        ko = ks * 128 - Q0
                        ev_s1 = e.op("tensor", lambda t, ko=ko, q0=q0, hb=hb, bank=bank: t.matmul(
                            ps_s[bank][:, :], kfc[hb:hb + 64, ko:ko + 128],
                            qcp[hb:hb + 64, q0:q0 + 512], start=True, stop=False),
                            deps=kq_evs + war[f"ps{bank}"] + war[f"s{bank}"])
                        ev_s2 = e.op("tensor", lambda t, ko=ko, q0=q0, hb=hb, bank=bank: t.matmul(
                            ps_s[bank][:, :], kfs[hb:hb + 64, ko:ko + 128],
                            qsp[hb:hb + 64, q0:q0 + 512], start=False, stop=True),
                            deps=[ev_s1])
                        ev_m = e.op("vector", lambda v, j=j, bank=bank: v.tensor_tensor(
                            sbf[bank][:, :], ps_s[bank][:, :], dm[j][:, :], OP.mult),
                            deps=[ev_s2] + ld_c + war[f"s{bank}"])
                        war[f"ps{bank}"] = [ev_m]
                        ev_pv = e.op("tensor", lambda t, ks=ks, hh=hh, bank=bank, first=(j == 0): t.matmul(
                            ps_y[0:65, :], v2e[:, ks * 130 + hh * 65:ks * 130 + hh * 65 + 65],
                            sbf[bank][:, :], start=first, stop=False),
                            deps=[ev_m] + v_evs + (war["psy"] if j == 0 else []))
                        war[f"s{bank}"] = [ev_pv]
                    ev_pv = e.op("tensor", lambda t, q0=q0, hb=hb: t.matmul(
                        ps_y[0:65, :], stc_bf[hb:hb + 64, :], qcp[hb:hb + 64, q0:q0 + 512],
                        start=False, stop=False), deps=[ev_pv, ev_sc])
                    ev_pv = e.op("tensor", lambda t, q0=q0, hb=hb: t.matmul(
                        ps_y[0:65, :], stc_lo[hb:hb + 64, :], qcp[hb:hb + 64, q0:q0 + 512],
                        start=False, stop=False), deps=[ev_pv, ev_sc])
                    ev_pv = e.op("tensor", lambda t, q0=q0, hb=hb: t.matmul(
                        ps_y[0:65, :], sts_bf[hb:hb + 64, :], qsp[hb:hb + 64, q0:q0 + 512],
                        start=False, stop=False), deps=[ev_pv, ev_ss])
                    ev_pv = e.op("tensor", lambda t, q0=q0, hb=hb: t.matmul(
                        ps_y[0:65, :], sts_lo[hb:hb + 64, :], qsp[hb:hb + 64, q0:q0 + 512],
                        start=False, stop=True), deps=[ev_pv, ev_ss])
                    war["stbf"] = [ev_pv]
                    ev_z = e.op("vector", lambda v: v.tensor_scalar_add(
                        zrow[0:1, :], ps_y[64:65, :], 1e-6), deps=[ev_pv] + war["rrow"])
                    ev_r = e.op("vector", lambda v: v.reciprocal(
                        rrow[0:1, :], zrow[0:1, :]), deps=[ev_z])
                    ev_b = e.op("tensor", lambda t: t.matmul(
                        ps_b[0:64, :], ones64[0:1, :], rrow[0:1, :],
                        start=True, stop=True), deps=[ev_r] + ev_con + war["psb"])
                    ev_ye = e.op("scalar", lambda s: s.copy(
                        ysc[:, :], ps_y[0:64, :]), deps=[ev_pv] + war["ysc"])
                    war["psy"] = [ev_ye]
                    ev_n = e.op("vector", lambda v: v.tensor_tensor(
                        ysc2[:, :], ysc[:, :], ps_b[0:64, :], OP.mult),
                        deps=[ev_ye, ev_b] + war["ysc2"])
                    war["rrow"] = [ev_b]
                    war["psb"] = [ev_n]
                    war["ysc"] = [ev_n]
                    ev_g = e.op("vector", lambda v, hp=hp, hb=hb, q0=q0: v.tensor_copy(
                        a2[hp][hb:hb + 64, q0:q0 + 512], ysc2[:, :]), deps=[ev_n])
                    war["ysc2"] = [ev_g]
                    head_evs.append(ev_g)
                    for j in range(4):
                        ks = (Q0 + q0) // 128 + j
                        stp = j == 3
                        ev_su = e.op("tensor", lambda t, ks=ks, hb=hb, hh=hh, stp=stp: t.matmul(
                            ps_stc[:, :], ktc[:, ks * 128 + hb:ks * 128 + hb + 64],
                            v2e[:, ks * 130 + hh * 65:ks * 130 + hh * 65 + 65],
                            start=False, stop=stp, skip_group_check=True),
                            deps=[ev_sc, ev_ss] + kq_evs + v_evs)
                        ev_su = e.op("tensor", lambda t, ks=ks, hb=hb, hh=hh, stp=stp: t.matmul(
                            ps_sts[:, :], kts[:, ks * 128 + hb:ks * 128 + hb + 64],
                            v2e[:, ks * 130 + hh * 65:ks * 130 + hh * 65 + 65],
                            start=False, stop=stp, skip_group_check=True),
                            deps=[ev_sc, ev_ss] + kq_evs + v_evs)
                war["st"] = [ev_sc, ev_ss, ev_su]
            war["qk"] = head_evs
            war["v2e"] = head_evs
            out_evs.extend(head_evs)
    at_evs = out_evs

    ow_sb = [c.sb([128, D], BF16, f"ow{i}") for i in range(NT)]
    owev = [e.dma("gpsimd", out=ow_sb[i][:, :], in_=ow[i * 128:(i + 1) * 128, :],
                  deps=at_evs) for i in range(NT)]
    lnf = c.sb([128, 512], F32, "lnf3")
    prev3 = []
    with c.scope():
        pb = [c.ps([128, 512]) for _ in range(2)]
        pb_war = {0: at_evs, 1: at_evs}
        tmp = [c.sb([128, 512], F32) for _ in range(2)]
        tmp_war = {0: [], 1: []}
        for ci in range(HALF // 512):
            c0 = ci * 512
            ld3 = [e.dma("sync", out=xc[i][:, :],
                         in_=xT[i * 128:(i + 1) * 128, Q0 + c0:Q0 + c0 + 512],
                         deps=ln_evs + prev3) for i in range(NT)]
            ch3 = []

            def consume(m, evf, c0=c0, ch3=ch3):
                bank = m % 2
                ev = None
                for k in range(NT):
                    ev = e.op("tensor", lambda t, k=k, m=m, c0=c0, bank=bank: t.matmul(
                        pb[bank][:, :], ow_sb[k][:, m * 128:(m + 1) * 128],
                        a2[k][:, c0:c0 + 512],
                        start=(k == 0), stop=(k == NT - 1)),
                        deps=at_evs + owev + pb_war[bank])
                tb = tmp[bank]
                ev_1 = e.op("vector", lambda v, m=m, tb=tb, bank=bank: v.scalar_tensor_tensor(
                    tb[:, :], pb[bank][:, :], obc[m][:, 0:1], lnf[:, :],
                    OP.add, OP.add), deps=[evf, ev] + tmp_war[bank] + ld_c)
                ev_st = e.dma("sync", out=hT[m * 128:(m + 1) * 128, c0:c0 + 512],
                              in_=tb[:, :], deps=[ev_1])
                pb_war[bank] = [ev_1]
                tmp_war[bank] = [ev_st]
                ch3 += [ev_1, ev_st]
                return [ev_1]

            emit_layernorm_fm(c, xc, 512, None, deps=ld3 + at_evs,
                              shared_f32=lnf, consume_fn=consume)
            prev3 = ch3
    return c.finish()


def build_stage3b():
    c = Ctx("stage3b")
    e = c.e
    xT = c.inp("xT", [D, HALF])
    hT = c.inp("hT", [D, HALF])
    n2_col = c.inp("n2_col", [D, 1])
    w1 = c.inp("w1", [D, 4096])
    b1_col = c.inp("b1_col", [4096, 1])
    w2 = c.inp("w2", [4096, D])
    b2_col = c.inp("b2_col", [D, 1])
    yT = c.out("yT", [D, HALF])

    hx = [c.sb([128, HALF], F32, f"h{i}") for i in range(NT)]
    mi = [c.sb([128, HALF], BF16, f"mi{i}") for i in range(NT)]
    h1g = [c.sb([128, 512], BF16, f"h1g{i}") for i in range(32)]
    xc2 = [c.sb([128, 512], F32) for _ in range(2)]
    n2c = [c.sb([128, 1], F32) for _ in range(NT)]
    b1c = [c.sb([128, 1], F32) for _ in range(32)]
    b2c = [c.sb([128, 1], F32) for _ in range(NT)]
    w1p = [[c.sb([128, 128], BF16) for _ in range(NT)] for _ in range(2)]
    w2p = [c.sb([128, D], BF16) for _ in range(2)]

    ld = []
    for i in range(NT):
        ld.append(e.dma("sync", out=hx[i][:, :], in_=hT[i * 128:(i + 1) * 128, :]))
        ld.append(e.dma("sync", out=n2c[i][:, :], in_=n2_col[i * 128:(i + 1) * 128, :]))
        ld.append(e.dma("sync", out=b2c[i][:, :], in_=b2_col[i * 128:(i + 1) * 128, :]))
    for j in range(32):
        ld.append(e.dma("sync", out=b1c[j][:, :], in_=b1_col[j * 128:(j + 1) * 128, :]))

    rms_evs = emit_layernorm_fm(c, hx, HALF, mi, deps=ld, gamma=n2c, eps=1e-6,
                                rms=True)

    with c.scope():
        pbank = [c.ps([128, 512]) for _ in range(2)]
        obank = [c.ps([128, 512]) for _ in range(4)]
        pb_war = {0: [], 1: []}
        w1_war = {0: [], 1: []}
        w2_war = {0: [], 1: []}
        ob_war = {i: [] for i in range(4)}
        tmp = [c.sb([128, 512], F32) for _ in range(2)]
        tmp_war = {0: [], 1: []}
        xcw = {0: [], 1: []}
        for ci in range(HALF // 512):
            c0 = ci * 512
            g_evs = []
            for f1 in range(32):
                wp = f1 % 2
                wev = [e.dma("gpsimd", out=w1p[wp][k][:, :],
                             in_=w1[k * 128:(k + 1) * 128, f1 * 128:(f1 + 1) * 128],
                             deps=w1_war[wp]) for k in range(NT)]
                bank = f1 % 2
                ev = None
                for k in range(NT):
                    ev = e.op("tensor", lambda t, k=k, c0=c0, wp=wp, bank=bank: t.matmul(
                        pbank[bank][:, :], w1p[wp][k][:, :], mi[k][:, c0:c0 + 512],
                        start=(k == 0), stop=(k == NT - 1)),
                        deps=rms_evs + wev + pb_war[bank])
                w1_war[wp] = [ev]
                ev_g = e.op("scalar", lambda s, f1=f1, bank=bank: s.activation(
                    h1g[f1][:, :], pbank[bank][:, :], AF.Gelu_apprx_tanh,
                    bias=b1c[f1][:, 0:1]), deps=[ev] + ld + g_evs[-2:-1])
                pb_war[bank] = [ev_g]
                g_evs.append(ev_g)
            for grp in range(2):
                ms = list(range(grp * 4, grp * 4 + 4))
                ev_k = {m: None for m in ms}
                for k in range(32):
                    wp = k % 2
                    wev2 = [e.dma("gpsimd", out=w2p[wp][:, :],
                                  in_=w2[k * 128:(k + 1) * 128, :],
                                  deps=w2_war[wp])]
                    for m in ms:
                        bank = m - grp * 4
                        ev_k[m] = e.op("tensor", lambda t, k=k, m=m, wp=wp, bank=bank: t.matmul(
                            obank[bank][:, :], w2p[wp][:, m * 128:(m + 1) * 128],
                            h1g[k][:, :], start=(k == 0), stop=(k == 31)),
                            deps=g_evs + wev2 + ob_war[bank])
                    w2_war[wp] = [ev_k[m] for m in ms]
                for m in ms:
                    bank = m - grp * 4
                    tb = tmp[m % 2]
                    xb = xc2[m % 2]
                    ev_x = e.dma("sync", out=xb[:, :],
                                 in_=xT[m * 128:(m + 1) * 128, c0:c0 + 512],
                                 deps=xcw[m % 2])
                    ev_1 = e.op("vector", lambda v, m=m, c0=c0, tb=tb, bank=bank: v.scalar_tensor_tensor(
                        tb[:, :], obank[bank][:, :], b2c[m][:, 0:1],
                        hx[m][:, c0:c0 + 512], OP.add, OP.add),
                        deps=[ev_k[m]] + tmp_war[m % 2] + ld)
                    ev_2 = e.op("vector", lambda v, tb=tb, xb=xb: v.tensor_tensor(
                        tb[:, :], tb[:, :], xb[:, :], OP.add), deps=[ev_1, ev_x])
                    ev_st = e.dma("sync", out=yT[m * 128:(m + 1) * 128, c0:c0 + 512],
                                  in_=tb[:, :], deps=[ev_2])
                    ob_war[bank] = [ev_1]
                    tmp_war[m % 2] = [ev_st]
                    xcw[m % 2] = [ev_2]
    return c.finish()


def run_stage3(x_full, inp):
    nc_a = _get_prog("stage3a", build_stage3a)
    nc_b = _get_prog("stage3b", build_stage3b)
    qkvw = np.asarray(inp["L0_attn_b_qkv_w"])
    qkvb = np.asarray(inp["L0_attn_b_qkv_b"])
    ow = np.asarray(inp["L0_attn_b_out_w"])
    ob = np.asarray(inp["L0_attn_b_out_b"])
    n1 = np.asarray(inp["L0_attn_b_norm1_w"])
    n2 = np.asarray(inp["L0_attn_b_norm2_w"])
    w1 = np.asarray(inp["L0_attn_b_mlp_w1"])
    b1 = np.asarray(inp["L0_attn_b_mlp_b1"])
    w2 = np.asarray(inp["L0_attn_b_mlp_w2"])
    b2 = np.asarray(inp["L0_attn_b_mlp_b2"])
    pos = np.arange(T, dtype=np.float64)
    cw_g = np.cos(np.pi / 2 * pos / T).astype(np.float32)
    sw_g = np.sin(np.pi / 2 * pos / T).astype(np.float32)
    r = np.arange(512)
    kk = np.arange(512)
    dmask = (kk[:, None] <= r[None, :]).astype(np.float32)
    in_maps = []
    for core in range(N_CORES):
        b, h = core // 2, core % 2
        own = x_full[b, h * HALF:(h + 1) * HALF, :]
        if h == 0:
            prev = np.zeros_like(own)
            cw = np.concatenate([np.zeros(HALF, np.float32), cw_g[0:HALF]])
            sw = np.concatenate([np.zeros(HALF, np.float32), sw_g[0:HALF]])
        else:
            prev = x_full[b, 0:HALF, :]
            cw, sw = cw_g, sw_g
        xT = np.ascontiguousarray(np.concatenate([prev, own], axis=0).T)
        in_maps.append(dict(
            xT=xT, qkvw=qkvw,
            kvb_row=np.ascontiguousarray(qkvb[1024:3072].reshape(1, 2048)),
            qb_col=np.ascontiguousarray(qkvb[0:1024].reshape(D, 1)),
            kb_col=np.ascontiguousarray(qkvb[1024:2048].reshape(D, 1)),
            ob_col=np.ascontiguousarray(ob.reshape(D, 1)),
            n1_col=np.ascontiguousarray(n1.reshape(D, 1)),
            ow=ow,
            cosw_row=np.ascontiguousarray(cw.reshape(1, T)),
            sinw_row=np.ascontiguousarray(sw.reshape(1, T)),
            coswc=np.ascontiguousarray(cw.reshape(T // 128, 128).T),
            sinwc=np.ascontiguousarray(sw.reshape(T // 128, 128).T),
            dmask=dmask))
    res = run_bass_kernel_spmd(nc_a, in_maps, core_ids=list(range(N_CORES)))
    in_maps_b = []
    for core in range(N_CORES):
        b, h = core // 2, core % 2
        own = x_full[b, h * HALF:(h + 1) * HALF, :]
        in_maps_b.append(dict(
            xT=np.ascontiguousarray(own.T),
            hT=np.ascontiguousarray(res.results[core]["hT"]),
            n2_col=np.ascontiguousarray(n2.reshape(D, 1)),
            w1=w1, b1_col=np.ascontiguousarray(b1.reshape(4096, 1)),
            w2=w2, b2_col=np.ascontiguousarray(b2.reshape(D, 1))))
    res_b = run_bass_kernel_spmd(nc_b, in_maps_b, core_ids=list(range(N_CORES)))
    out = np.empty_like(x_full)
    for core in range(N_CORES):
        b, h = core // 2, core % 2
        out[b, h * HALF:(h + 1) * HALF, :] = res_b.results[core]["yT"].T
    return out


DBG_L1 = False
SER_MLP = False
SER_ATTN = False
TE = 2688
O0 = 640
NTE = TE // 128


def build_l1attn(apply_ln):
    c = Ctx(f"l1attn{int(apply_ln)}")
    e = c.e
    e.serialize = SER_ATTN
    xT = c.inp("xT", [D, TE])
    wqkv = c.inp("wqkv", [D, 1536])
    wo = c.inp("wo", [512, D])
    qs_row = c.inp("qs_row", [1, 64])
    ks_row = c.inp("ks_row", [1, 64])
    tqc_i = c.inp("tqc", [128, NTE * 64])
    tqs_i = c.inp("tqs", [128, NTE * 64])
    tkc_i = c.inp("tkc", [128, NTE * 64])
    tks_i = c.inp("tks", [128, NTE * 64])
    pmask = c.inp("pmask", [512, 512])
    dmask = c.inp("dmask", [512, 512])
    pm_col = c.inp("pm_col", [128, 1])
    id_in = c.inp("ident_in", [128, 128])
    ymT = c.out("ymT", [D, HALF])
    if DBG_L1:
        dbg_qf = c.out("dbg_qf", [128, 512])
        dbg_qn = c.out("dbg_qn", [128, 512])
        dbg_ns = c.out("dbg_ns", [128, 8])
        dbg_qr = c.out("dbg_qr", [128, 512])
        dbg_o2 = c.out("dbg_o2", [512, HALF])
        dbg_qT = c.out("dbg_qT", [128, HALF])
        dbg_kT = c.out("dbg_kT", [128, (NTE - 1) * 128])
        dbg_v = c.out("dbg_v", [128, (NTE - 1) * 520])

    xin = [c.sb([128, TE], BF16, f"xin{i}") for i in range(NT)]
    xc = [c.sb([128, 512], F32) for _ in range(NT)]
    wq_sb = [c.sb([128, 1536], BF16, f"wq{i}") for i in range(NT)]

    tqc = c.sb([128, 16 * 64], BF16, "tqc_s")
    tqs = c.sb([128, 16 * 64], BF16, "tqs_s")
    tkc = c.sb([128, 20 * 64], BF16, "tkc_s")
    tks = c.sb([128, 20 * 64], BF16, "tks_s")
    pms = [c.sb([128, 512], BF16) for _ in range(4)]
    dms = [c.sb([128, 512], BF16) for _ in range(4)]
    pm_sb = c.sb([128, 1], F32)
    qsb = c.sb([128, 64], BF16, "qsb")
    ksb = c.sb([128, 64], BF16, "ksb")
    qT = [c.sb([128, HALF], BF16, f"qT{p}") for p in range(4)]
    kT = [c.sb([128, (NTE - 1) * 128], BF16, f"kT{p}") for p in range(4)]
    v2e = c.sb([128, (NTE - 1) * 520], BF16, "v2e")
    o2 = [c.sb([128, HALF], BF16, f"o2_{p}") for p in range(4)]
    ident = c.sb([128, 128], BF16, "ident")
    qrow = c.sb([1, 64], F32)
    nb48 = c.sb([128, 1], F32)
    ones64 = c.sb([1, 64], F32)
    pstr = [c.sb([128, 512], BF16) for _ in range(2)]
    pstr2 = [c.sb([128, 512], BF16) for _ in range(2)]
    ysc = c.sb([64, 512], BF16)
    ysc2 = c.sb([64, 512], BF16)
    rrow = c.sb([1, 512], F32)
    qf = c.sb([128, 512], F32)
    qsq = c.sb([128, 512], BF16)
    nsum = c.sb([128, 8], F32)
    nsum2 = c.sb([128, 8], F32)
    qn = c.sb([128, 512], F32)
    qrot = [c.sb([128, 512], BF16) for _ in range(2)]
    rhs_scr = c.sb([128, 512], BF16)

    ev_con = [e.op("vector", lambda v: v.memset(nb48[:, :], -48.0)),
              e.op("vector", lambda v: v.memset(ones64[:, :], 1.0))]
    ld_c = [e.dma("gpsimd", out=ident[:, :], in_=id_in[:, :])]
    for t, ti, t0w in ((tqc, tqc_i, (5, 21)), (tqs, tqs_i, (5, 21)),
                       (tkc, tkc_i, (1, 21)), (tks, tks_i, (1, 21))):
        ld_c.append(e.dma("gpsimd", out=t[:, :], in_=ti[:, t0w[0] * 64:t0w[1] * 64]))
    ld_c.append(e.dma("sync", out=pm_sb[:, :], in_=pm_col[:, :]))
    for j in range(4):
        ld_c.append(e.dma("gpsimd", out=pms[j][:, :], in_=pmask[j * 128:(j + 1) * 128, :]))
        ld_c.append(e.dma("gpsimd", out=dms[j][:, :], in_=dmask[j * 128:(j + 1) * 128, :]))
    wev = []
    for i in range(NT):
        wev.append(e.dma("gpsimd", out=wq_sb[i][:, :], in_=wqkv[i * 128:(i + 1) * 128, :]))

    onesr_l = c.sb([1, 128], F32, "onesr_l")
    ev_orl = e.op("vector", lambda v: v.memset(onesr_l[:, :], 1.0))
    with c.scope():
        ps_t = c.ps([128, 64])
        ev_l = e.dma("sync", out=qrow[0:1, :], in_=qs_row[0:1, :])
        ev_m = e.op("tensor", lambda t: t.matmul(
            ps_t[:, :], onesr_l[0:1, :], qrow[0:1, :], start=True, stop=True),
            deps=[ev_l, ev_orl] + ev_con)
        ev_q1 = e.op("vector", lambda v: v.tensor_copy(qsb[:, :], ps_t[:, :]),
                     deps=[ev_m])
        ev_l2 = e.dma("sync", out=qrow[0:1, :], in_=ks_row[0:1, :], deps=[ev_m])
        ev_m2 = e.op("tensor", lambda t: t.matmul(
            ps_t[:, :], onesr_l[0:1, :], qrow[0:1, :], start=True, stop=True),
            deps=[ev_l2, ev_q1, ev_orl])
        ev_q2 = e.op("vector", lambda v: v.tensor_copy(ksb[:, :], ps_t[:, :]),
                     deps=[ev_m2])
        qsk_evs = [ev_q1, ev_q2]

    ln_evs = []
    if apply_ln:
        prev = []
        for ci in range(6):
            c0 = ci * 512
            w = min(512, TE - c0)
            ld = [e.dma("sync", out=xc[i][:, :w], in_=xT[i * 128:(i + 1) * 128, c0:c0 + w],
                        deps=prev) for i in range(NT)]
            evs = emit_layernorm_fm(c, xc, w, xin, deps=ld, col_off=c0)
            prev = evs
            ln_evs.extend(evs)
    else:
        for i in range(NT):
            ln_evs.append(e.dma("gpsimd", out=xin[i][:, :], in_=xT[i * 128:(i + 1) * 128, :]))

    def emit_side(t0, psum, tab_c, tab_s, scale_b, is_q, deps, war):
        out = qrot[0 if is_q else 1]
        t0 = t0 - (5 if is_q else 1)
        prev_scr = list(war["scr"])
        ev0 = e.op("vector", lambda v: v.tensor_copy(qf[:, :], psum[:, :]),
                   deps=deps + prev_scr)
        ev1 = e.op("scalar", lambda s: s.activation(qsq[:, :], qf[:, :], AF.Square),
                   deps=[ev0] + prev_scr)
        red = [ev1]
        for h in range(8):
            red.append(e.op("vector", lambda v, h=h: v.reduce_sum(
                nsum[:, h:h + 1], qsq[:, h * 64:(h + 1) * 64],
                mybir.AxisListType.X), deps=red[-1:]))
        ev3 = e.op("vector", lambda v: v.tensor_scalar_max(
            nsum2[:, :], nsum[:, :], 1e-24), deps=red[-1:] + prev_scr)
        ev2 = e.op("scalar", lambda s: s.activation(nsum2[:, :], nsum2[:, :], AF.Ln),
                   deps=[ev3])
        ev4 = e.op("scalar", lambda s: s.activation(nsum2[:, :], nsum2[:, :], AF.Exp,
                                                    scale=-0.5), deps=[ev2])
        qf3 = qf[:, :].rearrange("p (h d) -> p h d", d=64)
        qn3 = qn[:, :].rearrange("p (h d) -> p h d", d=64)
        scr3 = qsq[:, :].rearrange("p (h d) -> p h d", d=64)
        rstd_b = nsum2[:, 0:8].unsqueeze(2).broadcast_to([128, 8, 64])
        ev5 = e.op("vector", lambda v: v.tensor_tensor(qn3, qf3, rstd_b, OP.mult),
                   deps=[ev4])
        sc_b = scale_b[:, 0:64].unsqueeze(1).broadcast_to([128, 8, 64])
        ev6 = e.op("vector", lambda v: v.tensor_tensor(qn3, qn3, sc_b, OP.mult),
                   deps=[ev5] + qsk_evs)
        tc = tab_c[:, t0 * 64:(t0 + 1) * 64].unsqueeze(1).broadcast_to([128, 8, 64])
        ev7 = e.op("vector", lambda v: v.tensor_tensor(
            rhs_scr[:, :].rearrange("p (h d) -> p h d", d=64), qn3, tc, OP.mult),
            deps=[ev6] + ld_c)
        ts_lo = tab_s[:, t0 * 64:t0 * 64 + 32].unsqueeze(1).broadcast_to([128, 8, 32])
        ts_hi = tab_s[:, t0 * 64 + 32:t0 * 64 + 64].unsqueeze(1).broadcast_to([128, 8, 32])
        ev8 = e.op("vector", lambda v: v.scalar_tensor_tensor(
            scr3[:, :, 0:32], qn3[:, :, 32:64], -1.0, ts_lo, OP.mult, OP.mult),
            deps=[ev7])
        ev9 = e.op("vector", lambda v: v.tensor_tensor(
            scr3[:, :, 32:64], qn3[:, :, 0:32], ts_hi, OP.mult), deps=[ev8])
        ev10 = e.op("vector", lambda v, out=out: v.tensor_tensor(
            out[:, :], rhs_scr[:, :], qsq[:, :], OP.add),
            deps=[ev9] + war["rot" + ("q" if is_q else "k")])
        war["scr"] = [ev1, ev4, ev10]
        return ev10

    qT_evs, kT_evs, v_evs = [], [], []
    with c.scope():
        ps_q = c.ps([128, 512])
        ps_k = c.ps([128, 512])
        ps_v = c.ps([128, 512])
        ps_tr = c.ps([128, 128], BF16)
        war = {"q": [], "k": [], "v": [], "tr": [], "scr": [], "rotq": [],
               "rotk": []}
        v2e_3d = v2e[:, :].rearrange("p (t c) -> p t c", c=65)
        ev_vo = [e.op("vector", lambda v: v.memset(v2e_3d[:, :, 64:65], 1.0))]
        ev_vo.append(e.op("vector", lambda v: v.tensor_copy(
            v2e_3d[:, 0:32, 64:65],
            pm_sb[:, 0:1].unsqueeze(1).broadcast_to([128, 32, 1])),
            deps=ev_vo + ld_c))
        for t0 in range(1, NTE):
            own = t0 >= 5
            sides = [("k", ps_k, 512), ("v", ps_v, 1024)]
            if own:
                sides = [("q", ps_q, 0)] + sides
            mm_evs = {}
            for nm, ps, col0 in sides:
                ev = None
                for k in range(NT):
                    if k < 4:
                        lo = t0 * 128
                    else:
                        lo = t0 * 128 - 1
                    ev = e.op("tensor", lambda t, k=k, ps=ps, col0=col0, lo=lo: t.matmul(
                        ps[:, :], xin[k][:, lo:lo + 128], wq_sb[k][:, col0:col0 + 512],
                        start=(k == 0), stop=(k == NT - 1)),
                        deps=ln_evs + wev + war[nm])
                mm_evs[nm] = ev
            b0v = (t0 - 1) * 520
            ev_v = None
            for h8 in range(8):
                ev_v = e.op("vector", lambda v, b0=b0v, h=h8: v.tensor_copy(
                    v2e[:, b0 + h * 65:b0 + h * 65 + 64],
                    ps_v[:, h * 64:(h + 1) * 64]),
                    deps=([mm_evs["v"]] + ev_vo + v_evs[-1:]) if h8 == 0 else [ev_v])
            war["v"] = [ev_v]
            v_evs.append(ev_v)
            ev_k = emit_side(t0, ps_k, tkc, tks, ksb, False, [mm_evs["k"]], war)
            war["k"] = [ev_k]
            if DBG_L1 and t0 == NTE - 1:
                dd = [e.dma("gpsimd", out=dbg_qf[:, :], in_=qf[:, :], deps=[ev_k]),
                      e.dma("gpsimd", out=dbg_qn[:, :], in_=qn[:, :], deps=[ev_k]),
                      e.dma("gpsimd", out=dbg_ns[:, :], in_=nsum2[:, :], deps=[ev_k])]
                war["scr"] = war["scr"] + dd
            if own:
                ev_q = emit_side(t0, ps_q, tqc, tqs, qsb, True, [mm_evs["q"]], war)
                war["q"] = [ev_q]
            else:
                ev_q = None
            tr_evs = []
            for p in range(4):
                ev_t = e.op("tensor", lambda t, p=p: t.transpose(
                    ps_tr[:, :], qrot[1][:, p * 128:(p + 1) * 128], ident[:, :]),
                    deps=[ev_k, ld_c[0]] + war["tr"])
                ev_te = e.op("scalar", lambda sc, p=p, t0=t0: sc.copy(
                    kT[p][:, (t0 - 1) * 128:t0 * 128], ps_tr[:, :]), deps=[ev_t])
                war["tr"] = [ev_te]
                tr_evs.append(ev_te)
                if own:
                    ev_t2 = e.op("tensor", lambda t, p=p: t.transpose(
                        ps_tr[:, :], qrot[0][:, p * 128:(p + 1) * 128], ident[:, :]),
                        deps=[ev_q, ld_c[0]] + war["tr"])
                    ev_te2 = e.op("scalar", lambda sc, p=p, t0=t0: sc.copy(
                        qT[p][:, (t0 - 5) * 128:(t0 - 4) * 128], ps_tr[:, :]),
                        deps=[ev_t2])
                    war["tr"] = [ev_te2]
                    tr_evs.append(ev_te2)
            war["rotk"] = tr_evs
            war["rotq"] = tr_evs
            kT_evs += tr_evs
        pp_done = kT_evs + v_evs

        ps_y = c.ps([128, 512])
        ps_b = c.ps([128, 512])
        awar = {"ps0": [], "ps1": [], "psy": [], "psb": [], "p0": [], "p1": [],
                "ysc": [], "ysc2": [], "rrow": [], "o2": []}
        at_evs = []
        for p in range(4):
            for hh in range(2):
                hb = hh * 64
                for w in range(4):
                    q0 = w * 512
                    ev_pv = None
                    for j in range(8):
                        kcol = (w * 4 + j) * 128
                        bank = j % 2
                        msk = pms[j] if j < 4 else dms[j - 4]
                        ev_s = e.op("tensor", lambda t, kcol=kcol, q0=q0, hb=hb, p=p, bank=bank: t.matmul(
                            [ps_q, ps_k][bank][:, :],
                            kT[p][hb:hb + 64, kcol:kcol + 128],
                            qT[p][hb:hb + 64, q0:q0 + 512], start=True, stop=True),
                            deps=pp_done + awar[f"ps{bank}"] + awar[f"p{bank}"])
                        ev_p = e.op("scalar", lambda s, bank=bank: s.activation(
                            pstr[bank][:, :], [ps_q, ps_k][bank][:, :], AF.Exp,
                            bias=nb48[:, 0:1], scale=8.0), deps=[ev_s] + ev_con)
                        awar[f"ps{bank}"] = [ev_p]
                        ev_mk = e.op("vector", lambda v, msk=msk, bank=bank: v.tensor_tensor(
                            pstr2[bank][:, :], pstr[bank][:, :], msk[:, :], OP.mult),
                            deps=[ev_p] + ld_c + awar[f"p{bank}"])
                        vsl = v2e[:, (w * 4 + j) * 520 + hh * 65 + p * 130:
                                  (w * 4 + j) * 520 + hh * 65 + p * 130 + 65]
                        ev_pv = e.op("tensor", lambda t, vsl=vsl, bank=bank, first=(j == 0), last=(j == 7): t.matmul(
                            ps_y[0:65, :], vsl, pstr2[bank][:, :],
                            start=first, stop=last),
                            deps=[ev_mk] + v_evs + (awar["psy"] if j == 0 else []))
                        awar[f"p{bank}"] = [ev_pv]
                    ev_r = e.op("vector", lambda v: v.reciprocal(
                        rrow[0:1, :], ps_y[64:65, :]), deps=[ev_pv] + awar["rrow"])
                    ev_b = e.op("tensor", lambda t: t.matmul(
                        ps_b[0:64, :], ones64[0:1, :], rrow[0:1, :],
                        start=True, stop=True), deps=[ev_r] + ev_con + awar["psb"])
                    ev_ye = e.op("scalar", lambda s: s.copy(
                        ysc[:, :], ps_y[0:64, :]), deps=[ev_pv] + awar["ysc"])
                    awar["psy"] = [ev_ye]
                    ev_n = e.op("vector", lambda v: v.tensor_tensor(
                        ysc2[:, :], ysc[:, :], ps_b[0:64, :], OP.mult),
                        deps=[ev_ye, ev_b] + awar["ysc2"])
                    awar["rrow"] = [ev_b]
                    awar["psb"] = [ev_n]
                    awar["ysc"] = [ev_n]
                    ev_g = e.op("vector", lambda v, p=p, hb=hb, q0=q0: v.tensor_copy(
                        o2[p][hb:hb + 64, q0:q0 + 512], ysc2[:, :]), deps=[ev_n])
                    awar["ysc2"] = [ev_g]
                    at_evs.append(ev_g)

    dbg_evs = []
    if DBG_L1:
        for p in range(4):
            dbg_evs.append(e.dma("gpsimd", out=dbg_o2[p * 128:(p + 1) * 128, :],
                                 in_=o2[p][:, :], deps=at_evs))
        dbg_evs.append(e.dma("gpsimd", out=dbg_qT[:, :], in_=qT[0][:, :], deps=at_evs))
        dbg_evs.append(e.dma("gpsimd", out=dbg_kT[:, :], in_=kT[0][:, :], deps=at_evs))
        dbg_evs.append(e.dma("gpsimd", out=dbg_v[:, :], in_=v2e[:, :], deps=at_evs))
        dbg_evs.append(e.dma("gpsimd", out=dbg_qr[:, :], in_=qrot[1][:, :], deps=at_evs))
    wo_sb = [qT[i] for i in range(4)]
    woev = [e.dma("gpsimd", out=wo_sb[i][:, 0:D], in_=wo[i * 128:(i + 1) * 128, :],
                  deps=at_evs + dbg_evs) for i in range(4)]
    lnf = qf
    prev3 = []
    with c.scope():
        pb = [c.ps([128, 512]) for _ in range(2)]
        pb_war = {0: at_evs, 1: at_evs}
        tmp = [c.sb([128, 512], F32) for _ in range(2)]
        tmp_war = {0: [], 1: []}
        for ci in range(HALF // 512):
            c0 = ci * 512
            ld3 = [e.dma("sync", out=xc[i][:, :512],
                         in_=xT[i * 128:(i + 1) * 128, O0 + c0:O0 + c0 + 512],
                         deps=(ln_evs if apply_ln else []) + prev3) for i in range(NT)]
            ch3 = []

            def consume(m, evf, c0=c0, ch3=ch3, ld3=None):
                bank = m % 2
                ev = None
                for k in range(4):
                    ev = e.op("tensor", lambda t, k=k, m=m, c0=c0, bank=bank: t.matmul(
                        pb[bank][:, :], wo_sb[k][:, m * 128:(m + 1) * 128],
                        o2[k][:, c0:c0 + 512],
                        start=(k == 0), stop=(k == 3)),
                        deps=at_evs + woev + pb_war[bank])
                tb = tmp[bank]
                src = lnf[:, :] if evf is not None else xc[m][:, :512]
                ev_1 = e.op("vector", lambda v, tb=tb, bank=bank, src=src: v.tensor_tensor(
                    tb[:, :], src, pb[bank][:, :], OP.add),
                    deps=([evf] if evf is not None else (ld3 or [])) + [ev] + tmp_war[bank])
                ev_st = e.dma("sync", out=ymT[m * 128:(m + 1) * 128, c0:c0 + 512],
                              in_=tb[:, :], deps=[ev_1])
                pb_war[bank] = [ev_1]
                tmp_war[bank] = [ev_st]
                ch3 += [ev_1, ev_st]
                return [ev_1]

            if apply_ln:
                emit_layernorm_fm(c, xc, 512, None, deps=ld3 + at_evs,
                                  shared_f32=lnf, consume_fn=consume)
            else:
                for m in range(NT):
                    consume(m, None, ld3=[ld3[m]])
            prev3 = ch3
    return c.finish()


def build_l1mlp(last_layer):
    c = Ctx(f"l1mlp{int(last_layer)}")
    e = c.e
    e.serialize = SER_MLP
    HID = 2730
    NF = 22
    xmT = c.inp("xmT", [D, 1 + HALF])
    if last_layer:
        xoT = c.inp("xoT", [D, HALF])
    lg_col = c.inp("lg_col", [D, 1])
    lb_col = c.inp("lb_col", [D, 1])
    w1a = c.inp("w1a", [D, HID])
    w1g = c.inp("w1g", [D, HID])
    b1a_col = c.inp("b1a_col", [HID, 1])
    b1g_col = c.inp("b1g_col", [HID, 1])
    w2 = c.inp("w2", [HID, D])
    b2_col = c.inp("b2_col", [D, 1])
    yT = c.out("yT", [D, HALF])

    xm = [c.sb([128, 1 + HALF], F32, f"xm{i}") for i in range(NT)]
    mln = [c.sb([128, HALF], BF16, f"mln{i}") for i in range(NT)]
    lgc = [c.sb([128, 1], F32) for _ in range(NT)]
    lbc = [c.sb([128, 1], F32) for _ in range(NT)]
    b2c = [c.sb([128, 1], F32) for _ in range(NT)]
    b1ac = [c.sb([128, 1], F32) for _ in range(NF)]
    b1gc = [c.sb([128, 1], F32) for _ in range(NF)]
    w1ap = [[c.sb([128, 128], BF16) for _ in range(NT)] for _ in range(2)]
    w1gp = [[c.sb([128, 128], BF16) for _ in range(NT)] for _ in range(2)]
    w2p = [c.sb([128, D], BF16) for _ in range(2)]
    mf = [c.sb([128, 512], BF16, f"mf{j}") for j in range(NF)]
    gsc = [c.sb([128, 512], BF16) for _ in range(2)]
    asc = [c.sb([128, 512], BF16) for _ in range(2)]
    xo_c = [c.sb([128, 512], F32) for _ in range(2)]
    tmp = [c.sb([128, 512], F32) for _ in range(2)]

    ld = []
    for i in range(NT):
        ld.append(e.dma("sync", out=xm[i][:, :], in_=xmT[i * 128:(i + 1) * 128, :]))
        ld.append(e.dma("sync", out=lgc[i][:, :], in_=lg_col[i * 128:(i + 1) * 128, :]))
        ld.append(e.dma("sync", out=lbc[i][:, :], in_=lb_col[i * 128:(i + 1) * 128, :]))
        ld.append(e.dma("sync", out=b2c[i][:, :], in_=b2_col[i * 128:(i + 1) * 128, :]))
    for j in range(NF):
        r = min(128, HID - j * 128)
        ld.append(e.dma("sync", out=b1ac[j][0:r, :], in_=b1a_col[j * 128:j * 128 + r, :]))
        ld.append(e.dma("sync", out=b1gc[j][0:r, :], in_=b1g_col[j * 128:j * 128 + r, :]))

    xv = [xm[i][:, 1:1 + HALF] if i < 4 else xm[i][:, 0:HALF] for i in range(NT)]
    ln_evs = emit_layernorm_fm(c, xv, HALF, mln, deps=ld, ln_gamma=lgc, ln_beta=lbc)

    with c.scope():
        pa = c.ps([128, 512])
        pg = c.ps([128, 512])
        po = [c.ps([128, 512]) for _ in range(4)]
        pa_war, pg_war = [], []
        po_war = {i: [] for i in range(4)}
        w1_war = {0: [], 1: []}
        w2_war = {0: [], 1: []}
        tmp_war = {0: [], 1: []}
        xo_war = {0: [], 1: []}
        mf_war = []
        for ci in range(HALF // 512):
            c0 = ci * 512
            g_evs = []
            for f1 in range(NF):
                r = min(128, HID - f1 * 128)
                wp = f1 % 2
                wevA = [e.dma("gpsimd", out=w1ap[wp][k][:, 0:r],
                              in_=w1a[k * 128:(k + 1) * 128, f1 * 128:f1 * 128 + r],
                              deps=w1_war[wp]) for k in range(NT)]
                wevG = [e.dma("gpsimd", out=w1gp[wp][k][:, 0:r],
                              in_=w1g[k * 128:(k + 1) * 128, f1 * 128:f1 * 128 + r],
                              deps=w1_war[wp]) for k in range(NT)]
                evA = evG = None
                for k in range(NT):
                    evA = e.op("tensor", lambda t, k=k, c0=c0, wp=wp, r=r: t.matmul(
                        pa[0:r, :], w1ap[wp][k][:, 0:r], mln[k][:, c0:c0 + 512],
                        start=(k == 0), stop=(k == NT - 1)),
                        deps=ln_evs + wevA + pa_war)
                for k in range(NT):
                    evG = e.op("tensor", lambda t, k=k, c0=c0, wp=wp, r=r: t.matmul(
                        pg[0:r, :], w1gp[wp][k][:, 0:r], mln[k][:, c0:c0 + 512],
                        start=(k == 0), stop=(k == NT - 1)),
                        deps=ln_evs + wevG + pg_war)
                w1_war[wp] = [evA, evG]
                sb_g = gsc[f1 % 2]
                sb_a = asc[f1 % 2]
                ev_g = e.op("scalar", lambda s, f1=f1, r=r, sb_g=sb_g: s.activation(
                    sb_g[0:r, :], pg[0:r, :], AF.Gelu, bias=b1gc[f1][0:r, 0:1]),
                    deps=[evG] + ld + g_evs[-2:-1])
                pg_war = [ev_g]
                ev_a = e.op("vector", lambda v, f1=f1, r=r, sb_a=sb_a: v.tensor_scalar_add(
                    sb_a[0:r, :], pa[0:r, :], b1ac[f1][0:r, 0:1]),
                    deps=[evA] + ld + g_evs[-2:-1])
                pa_war = [ev_a]
                ev_m = e.op("vector", lambda v, f1=f1, r=r, sb_a=sb_a, sb_g=sb_g: v.tensor_tensor(
                    mf[f1][0:r, :], sb_a[0:r, :], sb_g[0:r, :], OP.mult),
                    deps=[ev_g, ev_a] + mf_war)
                g_evs.append(ev_m)
            mf_war = []
            for grp in range(2):
                ms = list(range(grp * 4, grp * 4 + 4))
                ev_k = {m: None for m in ms}
                for k in range(NF):
                    r = min(128, HID - k * 128)
                    wp = k % 2
                    wev2 = [e.dma("gpsimd", out=w2p[wp][0:r, :],
                                  in_=w2[k * 128:k * 128 + r, :],
                                  deps=w2_war[wp])]
                    for m in ms:
                        bank = m - grp * 4
                        ev_k[m] = e.op("tensor", lambda t, k=k, m=m, r=r, wp=wp, bank=bank: t.matmul(
                            po[bank][:, :], w2p[wp][0:r, m * 128:(m + 1) * 128],
                            mf[k][0:r, :], start=(k == 0), stop=(k == NF - 1)),
                            deps=g_evs + wev2 + po_war[bank])
                    w2_war[wp] = [ev_k[m] for m in ms]
                    mf_war.append(ev_k[ms[-1]])
                for m in ms:
                    bank = m - grp * 4
                    tb = tmp[m % 2]
                    ev_1 = e.op("vector", lambda v, m=m, c0=c0, tb=tb, bank=bank: v.scalar_tensor_tensor(
                        tb[:, :], po[bank][:, :], b2c[m][:, 0:1],
                        xm[m][:, 1 + c0:1 + c0 + 512], OP.add, OP.add),
                        deps=[ev_k[m]] + ld + tmp_war[m % 2])
                    if last_layer:
                        xb = xo_c[m % 2]
                        ev_x = e.dma("sync", out=xb[:, :],
                                     in_=xoT[m * 128:(m + 1) * 128, c0:c0 + 512],
                                     deps=xo_war[m % 2])
                        ev_1 = e.op("vector", lambda v, tb=tb, xb=xb: v.tensor_tensor(
                            tb[:, :], tb[:, :], xb[:, :], OP.add), deps=[ev_1, ev_x])
                        xo_war[m % 2] = [ev_1]
                    ev_st = e.dma("sync", out=yT[m * 128:(m + 1) * 128, c0:c0 + 512],
                                  in_=tb[:, :], deps=[ev_1])
                    po_war[bank] = [ev_1]
                    tmp_war[m % 2] = [ev_st]
    return c.finish()


def _l1ffn_tables(h):
    n, DIM_HEAD, WINDOW = T, 64, 512
    half_idx = np.arange(0, DIM_HEAD, 2, dtype=np.float64)
    inv_freq = 1.0 / 10000.0 ** (half_idx / DIM_HEAD)
    g0 = h * HALF - O0
    pos = np.arange(g0, g0 + TE, dtype=np.float64)
    freqs = pos[:, None] * inv_freq[None, :]
    freqs = np.concatenate([freqs, freqs], axis=-1)
    base = (half_idx + 0.4 * DIM_HEAD) / (1.4 * DIM_HEAD)
    power = (pos - n // 2) / float(WINDOW // 2)
    scale = base[None, :] ** power[:, None]
    scale = np.concatenate([scale, scale], axis=-1)
    cos, sin = np.cos(freqs), np.sin(freqs)
    tqc = (cos * scale).astype(np.float32)
    tqs = (sin * scale).astype(np.float32)
    tkc = (cos / scale).astype(np.float32)
    tks = (sin / scale).astype(np.float32)
    if h == 0:
        tqc[:O0] = tqs[:O0] = tkc[:O0] = tks[:O0] = 0.0
    def lay(tab):
        t3 = tab.reshape(NTE, 128, 64)
        return np.ascontiguousarray(t3.transpose(1, 0, 2).reshape(128, NTE * 64))
    return lay(tqc), lay(tqs), lay(tkc), lay(tks)


_L1_MASKS = None


def _l1_masks():
    global _L1_MASKS
    if _L1_MASKS is None:
        kk = np.arange(512)[:, None]
        r = np.arange(512)[None, :]
        pmask = (kk >= r).astype(np.float32)
        dmask = (kk <= r).astype(np.float32)
        _L1_MASKS = (pmask, dmask, np.zeros((512, 512), np.float32))
    return _L1_MASKS


def run_stage68(x_full, inp):
    nc_a1 = _get_prog("l1attn1", lambda: build_l1attn(True))
    nc_a0 = _get_prog("l1attn0", lambda: build_l1attn(False))
    nc_m0 = _get_prog("l1mlp0", lambda: build_l1mlp(False))
    nc_m1 = _get_prog("l1mlp1", lambda: build_l1mlp(True))
    pmask, dmask, zmask = _l1_masks()
    ident = np.eye(128, dtype=np.float32)
    x_cur = x_full
    HID = 2730
    for li in range(2):
        wqkv = np.ascontiguousarray(inp["L1_ffn_wqkv"][li])
        wo = np.ascontiguousarray(inp["L1_ffn_wo"][li])
        qs = np.ascontiguousarray(np.asarray(inp["L1_ffn_q_scale"][li]).reshape(1, 64))
        ks = np.ascontiguousarray(np.asarray(inp["L1_ffn_k_scale"][li]).reshape(1, 64))
        lg = np.ascontiguousarray(np.asarray(inp["L1_ffn_ln_g"][li]).reshape(D, 1))
        lb = np.ascontiguousarray(np.asarray(inp["L1_ffn_ln_b"][li]).reshape(D, 1))
        w1 = np.asarray(inp["L1_ffn_w1"][li])
        b1 = np.asarray(inp["L1_ffn_b1"][li])
        w2 = np.ascontiguousarray(np.asarray(inp["L1_ffn_w2"][li]))
        b2 = np.ascontiguousarray(np.asarray(inp["L1_ffn_b2"][li]).reshape(D, 1))
        w1a = np.ascontiguousarray(w1[:, :HID])
        w1g = np.ascontiguousarray(w1[:, HID:])
        b1a = np.ascontiguousarray(b1[:HID].reshape(HID, 1))
        b1g = np.ascontiguousarray(b1[HID:].reshape(HID, 1))
        in_maps = []
        for core in range(N_CORES):
            b, h = core // 2, core % 2
            own = x_cur[b, h * HALF:(h + 1) * HALF, :]
            if h == 0:
                ptail = np.zeros((O0, D), np.float32)
                pmc = np.zeros((128, 1), np.float32)
            else:
                ptail = x_cur[b, HALF - O0:HALF, :]
                pmc = np.ones((128, 1), np.float32)
            xT = np.ascontiguousarray(
                np.concatenate([ptail, own], axis=0).T.astype(np.float32))
            tqc, tqs, tkc, tks = _l1ffn_tables(h)
            in_maps.append(dict(xT=xT, wqkv=wqkv, wo=wo, qs_row=qs, ks_row=ks,
                                tqc=tqc, tqs=tqs, tkc=tkc, tks=tks,
                                pmask=pmask, dmask=dmask, pm_col=pmc,
                                ident_in=ident))
        nc_a = nc_a1 if li == 0 else nc_a0
        res = run_bass_kernel_spmd(nc_a, in_maps, core_ids=list(range(N_CORES)))
        xm = np.empty_like(x_full)
        for core in range(N_CORES):
            b, h = core // 2, core % 2
            xm[b, h * HALF:(h + 1) * HALF, :] = res.results[core]["ymT"].T
        in_maps = []
        for core in range(N_CORES):
            b, h = core // 2, core % 2
            own = xm[b, h * HALF:(h + 1) * HALF, :]
            halo = (np.zeros((1, D), np.float32) if h == 0
                    else xm[b, HALF - 1:HALF, :])
            xmT = np.ascontiguousarray(np.concatenate([halo, own], axis=0).T)
            m = dict(xmT=xmT, lg_col=lg, lb_col=lb, w1a=w1a, w1g=w1g,
                     b1a_col=b1a, b1g_col=b1g, w2=w2, b2_col=b2)
            if li == 1:
                m["xoT"] = np.ascontiguousarray(
                    x_full[b, h * HALF:(h + 1) * HALF, :].T)
            in_maps.append(m)
        nc_m = nc_m1 if li == 1 else nc_m0
        res = run_bass_kernel_spmd(nc_m, in_maps, core_ids=list(range(N_CORES)))
        x_new = np.empty_like(x_full)
        for core in range(N_CORES):
            b, h = core // 2, core % 2
            x_new[b, h * HALF:(h + 1) * HALF, :] = res.results[core]["yT"].T
        x_cur = x_new
    return x_cur


_ONES_ROW = {}


def ones_row_g(c):
    if c.name not in _ONES_ROW:
        t = c.sb([1, 128], F32, "ones_row_g")
        c.e.op("vector", lambda v: v.memset(t[:, :], 1.0))
        _ONES_ROW[c.name] = t
    return _ONES_ROW[c.name]



_PROG_CACHE = {}


def _get_prog(name, builder):
    if name not in _PROG_CACHE:
        _PROG_CACHE[name] = builder()
    return _PROG_CACHE[name]


def run_stage2(x_full, wr, wk, wv, mu_r, mu_k):
    nc = _get_prog("stage2", build_stage2)
    in_maps = []
    wrT = np.ascontiguousarray(wr.T)
    wkT = np.ascontiguousarray(wk.T)
    wvT = np.ascontiguousarray(wv.T)
    murc = np.ascontiguousarray(mu_r.reshape(D, 1))
    mukc = np.ascontiguousarray(mu_k.reshape(D, 1))
    for core in range(N_CORES):
        b, h = core // 2, core % 2
        xT = np.ascontiguousarray(x_full[b, h * HALF:(h + 1) * HALF, :].T)
        if h == 0:
            prevT = np.zeros((D, 1), np.float32)
        else:
            prevT = np.ascontiguousarray(x_full[b, h * HALF - 1, :].reshape(D, 1))
        in_maps.append(dict(xT=xT, prevT=prevT, wrT=wrT, wkT=wkT, wvT=wvT,
                            mur=murc, muk=mukc))
    res = run_bass_kernel_spmd(nc, in_maps, core_ids=list(range(N_CORES)))
    out = np.empty_like(x_full)
    for core in range(N_CORES):
        b, h = core // 2, core % 2
        out[b, h * HALF:(h + 1) * HALF, :] = res.results[core]["yT"].T
    return out


def run_stage1(x_full, wq, wk, wv, wpost):
    nc = _get_prog("stage1", build_stage1)
    in_maps = []
    for core in range(N_CORES):
        b, h = core // 2, core % 2
        own = x_full[b, h * HALF:(h + 1) * HALF, :]
        other = x_full[b, (1 - h) * HALF:(2 - h) * HALF, :]
        xT = np.ascontiguousarray(np.concatenate([own, other], axis=0).T)
        in_maps.append(dict(xT=xT, wq=wq, wk=wk, wv=wv, wpost=wpost))
    res = run_bass_kernel_spmd(nc, in_maps, core_ids=list(range(N_CORES)))
    out = np.empty_like(x_full)
    for core in range(N_CORES):
        b, h = core // 2, core % 2
        out[b, h * HALF:(h + 1) * HALF, :] = res.results[core]["yT"].T
    return out


def kernel(**inputs):
    g = {k: np.ascontiguousarray(np.asarray(v, dtype=np.float32))
         for k, v in inputs.items()}
    x = g["L0_attn_a_X"]
    x = run_stage1(x, g["L0_attn_a_Wq"], g["L0_attn_a_Wk"], g["L0_attn_a_Wv"],
                   g["L0_attn_a_Wpost"])
    x = run_stage2(x, g["L0_ffn_W_r"], g["L0_ffn_W_k"], g["L0_ffn_W_v"],
                   g["L0_ffn_mu_r"], g["L0_ffn_mu_k"])
    x = run_stage3(x, g)
    x = run_stage2(x, g["L0_ffn_W_r"], g["L0_ffn_W_k"], g["L0_ffn_W_v"],
                   g["L0_ffn_mu_r"], g["L0_ffn_mu_k"])
    x = run_stage5(x, g["L1_attn_a_W"])
    x = run_stage68(x, g)
    x = run_stage7(x, g["L1_attn_b_Wq"], g["L1_attn_b_Wk"], g["L1_attn_b_Wv"],
                   g["L1_attn_b_Wo"])
    x = run_stage68(x, g)
    return x
